# version 01
# baseline (speedup 1.0000x reference)
import numpy as np
import ml_dtypes
import concourse.bass as bass
import concourse.mybir as mybir
from concourse.bass_utils import run_bass_kernel_spmd

F32 = mybir.dt.float32
BF16 = mybir.dt.bfloat16
AF = mybir.ActivationFunctionType
ALU = mybir.AluOpType

D = 2048
DI = 4096
NH = 32
NS = 64
DIP = 8448
T = 1024
TH = 1028
NCH = 8
G = 4096
GD = 256
GRP = 256
NGRP = T // GRP
BIG = 1.0e30
DBG = False
DBGCC = False
DBG3 = False
LNEPS = 1e-5
F32EPS = float(np.finfo(np.float32).eps)

_CACHE = {}


class Eng:
    def __init__(self, kb, key, semkey, dmakey):
        self.kb, self.key, self.semkey, self.dmakey = kb, key, semkey, dmakey

    def w(self, tok):
        if tok is None:
            return
        k, v = tok
        self.kb.q[self.key].append(lambda e, k=k, v=v: e.wait_ge(self.kb.sem[k], v))

    def ws(self, toks):
        best = {}
        for t in toks:
            if t is None:
                continue
            k, v = t
            best[k] = max(best.get(k, 0), v)
        for k, v in best.items():
            self.w((k, v))

    def d(self):
        self.kb.q[self.key].append(lambda e: e.drain())

    def o(self, fname, *a, inc=False, **kw):
        tok = None
        if inc:
            v = self.kb.bump(self.semkey)
            tok = (self.semkey, v)
            self.kb.q[self.key].append(
                lambda e, a=a, kw=kw: getattr(e, fname)(*a, **kw).then_inc(
                    self.kb.sem[self.semkey], 1))
        else:
            self.kb.q[self.key].append(
                lambda e, a=a, kw=kw: getattr(e, fname)(*a, **kw))
        return tok

    def dma(self, dst, src, **kw):
        v = self.kb.bump(self.dmakey, 16)
        tok = (self.dmakey, v)
        self.kb.q[self.key].append(
            lambda e, kw=kw: e.dma_start(out=dst, in_=src, **kw).then_inc(
                self.kb.sem[self.dmakey], 16))
        return tok

    def cc(self, kind, op, groups, ins, outs):
        v = self.kb.bump("cc", 1)
        self.kb.q[self.key].append(
            lambda e: e.collective_compute(kind, op, replica_groups=groups,
                                           ins=ins, outs=outs).then_inc(
                self.kb.sem["cc"]))
        return ("cc", v)


class Kb:
    def __init__(self, nc):
        self.nc = nc
        self.q = {k: [] for k in "tvsgy"}
        self.cnt = {k: 0 for k in ["et", "ev", "es", "eg", "sq", "cq", "gq", "cc"]}
        self.sem = {}
        self.t = Eng(self, "t", "et", None)
        self.v = Eng(self, "v", "ev", None)
        self.s = Eng(self, "s", "es", "cq")
        self.g = Eng(self, "g", "eg", "gq")
        self.y = Eng(self, "y", None, "sq")
        self.all = [self.t, self.v, self.s, self.g, self.y]

    def bump(self, s, n=1):
        self.cnt[s] += n
        return self.cnt[s]

    def barrier_all(self):
        toks = [(k, self.cnt[k]) for k in self.cnt if self.cnt[k] > 0]
        for e in self.all:
            e.ws(toks)


def build_core():
    nc = bass.Bass("TRN2", target_bir_lowering=False)
    kb = Kb(nc)
    KT, KV, KS, KG, KY = kb.t, kb.v, kb.s, kb.g, kb.y

    def din(name, shape, dt=F32):
        return nc.dram_tensor(name, shape, dt, kind="ExternalInput").ap()

    u_fm = din("u_fm", [D, TH])
    WinT = din("WinT", [D, DIP])
    inb = din("inb", [128, 66])
    cwd = din("cwd", [128, 33 * 4])
    cbd = din("cbd", [128, 33])
    dtbd = din("dtbd", [32, 1])
    alogd = din("alogd", [32, 1])
    lnwd = din("lnwd", [128, 32])
    lnbd = din("lnbd", [128, 32])
    WoutT = din("WoutT", [DI, D])
    outbd = din("outbd", [128, 16])
    WcT = din("WcT", [D, G])
    nwd = din("nwd", [128, 16])
    genesd = din("genesd", [G, GD])
    WdTd = din("WdTd", [GD, D])
    WuTd = din("WuTd", [GD, D])
    tempd = din("tempd", [1, 1])
    scld = din("scld", [128, 1])
    pmd = din("pmd", [128, 4])
    pm1d = din("pm1d", [128, 4])
    self32d = din("self32d", [32, 32 * 128])
    sel64d = din("sel64d", [32, 32 * 64], BF16)
    selPd = din("selPd", [32, 16 * 128])
    onesrfd = din("onesrfd", [1, 128])
    onescbd = din("onescbd", [128, 1], BF16)
    idbd = din("idbd", [128, 128], BF16)
    idfd = din("idfd", [128, 128])

    out_d = nc.dram_tensor("out", [D, T], F32, kind="ExternalOutput").ap()

    KI = "ExternalOutput" if DBG else "Internal"
    z_hbm = nc.dram_tensor("z_hbm", [DI, T], F32, kind=KI).ap()
    x_hbm = nc.dram_tensor("x_hbm", [T, DI], BF16, kind=KI).ap()
    r1_hbm = nc.dram_tensor("r1_hbm", [D, T], F32, kind=KI).ap()
    s_hbm = nc.dram_tensor("s_hbm", [128, NCH * 16 * 128], BF16).ap()
    dbgA = dbgB = dbgC = dbgD = dbgE = dbgF = dbgG = dbgH = None
    if DBG or DBG3:
        dbgA = nc.dram_tensor("dbgA", [32, T], F32, kind="ExternalOutput").ap()
        dbgB = nc.dram_tensor("dbgB", [128, 4096], F32, kind="ExternalOutput").ap()
        dbgC = nc.dram_tensor("dbgC", [128, 64], BF16, kind="ExternalOutput").ap()
        dbgD = nc.dram_tensor("dbgD", [128, TH], BF16, kind="ExternalOutput").ap()
        dbgE = nc.dram_tensor("dbgE", [128, T], BF16, kind="ExternalOutput").ap()
        dbgF = nc.dram_tensor("dbgF", [8, T], F32, kind="ExternalOutput").ap()
        dbgG = nc.dram_tensor("dbgG", [128, T], BF16, kind="ExternalOutput").ap()
        dbgH = nc.dram_tensor("dbgH", [128, 2064], F32, kind="ExternalOutput").ap()
    cc_in = nc.dram_tensor("cc_in", [128, 16 * 128 + 16], F32).ap()
    cc_out = nc.dram_tensor("cc_out", [4 * 128, 16 * 128 + 16], F32).ap()
    dbgI = (nc.dram_tensor("dbgI", [4 * 128, 16 * 128 + 16], F32,
                           kind="ExternalOutput").ap() if (DBG or DBGCC) else None)

    import math as _math
    _lo = nc.s._sbuf_addr_for_side("left") if hasattr(nc, "s") else None
    try:
        _lo = nc.m.bass._sbuf_addr_for_side("left")
    except Exception:
        _lo = None
    _ALO = [None, None]

    def _init_region():
        lo = nc._sbuf_addr_for_side("left")
        hi = nc._sbuf_addr_for_side("right")
        _ALO[0] = [(lo, hi)]
        _ALO[1] = {}

    def sbA(name, shape, dt=F32, hi=False):
        if _ALO[0] is None:
            _init_region()
        nb = int(np.prod(shape[1:])) * mybir.dt.size(dt)
        nb = (nb + 31) // 32 * 32
        free = _ALO[0]
        order = reversed(list(enumerate(free))) if hi else list(enumerate(free))
        for i, (a, b) in order:
            if hi:
                addr = (b - nb) // 32 * 32
                if addr < a:
                    continue
                if addr + nb == b:
                    if addr == a:
                        free.pop(i)
                    else:
                        free[i] = (a, addr)
                else:
                    free[i] = (a, addr)
                    if addr + nb < b:
                        free.insert(i + 1, (addr + nb, b))
                _ALO[1][name] = (addr, nb)
                return nc.alloc_sbuf_tensor_at(name, list(shape), dt, offset=addr)
            a0 = (a + 31) // 32 * 32
            if b - a0 >= nb:
                addr = a0
                if addr + nb == b and a0 == a:
                    free.pop(i)
                else:
                    free[i] = (addr + nb, b)
                _ALO[1][name] = (addr, nb)
                return nc.alloc_sbuf_tensor_at(name, list(shape), dt, offset=addr)
        raise AssertionError(
            f"manual sbuf OOM for {name} {shape} need {nb} free={free}")

    def sbF(*names):
        free = _ALO[0]
        for n in names:
            a, nb = _ALO[1].pop(n)
            free.append((a, a + nb))
        free.sort()
        out = []
        for a, b in free:
            if out and out[-1][1] == a:
                out[-1] = (out[-1][0], b)
            else:
                out.append((a, b))
        _ALO[0] = [tuple(x) for x in out]
        _ALO[0] = out

    from contextlib import ExitStack
    es = ExitStack()

    def pst(name, shape, dt=F32):
        return es.enter_context(nc.psum_tensor(name, shape, dt))

    for s in ["et", "ev", "es", "eg", "sq", "cq", "gq", "cc"]:
        kb.sem[s] = es.enter_context(nc.semaphore(s))

    p_big = pst("p_big", [128, 1024])
    p_big2 = pst("p_big2", [128, 1024])
    p_a = pst("p_a", [128, 128])
    p_b = pst("p_b", [128, 128])
    p_c = pst("p_c", [128, 128], BF16)
    p_ra = pst("p_ra", [1, 512])

    cw_s = sbA("cw_s", [128, 33 * 4])
    cb_s = sbA("cb_s", [128, 33])
    inb_s = sbA("inb_s", [128, 66])
    dtb_s = sbA("dtb_s", [32, 1])
    alog_s = sbA("alog_s", [32, 1])
    nAcol = sbA("nAcol", [32, 1])
    lnw_s = sbA("lnw_s", [128, 32])
    lnb_s = sbA("lnb_s", [128, 32])
    lnwb2 = sbA("lnwb2", [128, 32, 2], BF16)
    outb_s = sbA("outb_s", [128, 16])
    nw_s = sbA("nw_s", [128, 16])
    nw_bs = sbA("nw_bs", [128, 16], BF16)
    temp_s = sbA("temp_s", [1, 1])
    scl_s = sbA("scl_s", [128, 1])
    pm_s = sbA("pm_s", [128, 4])
    pm1_s = sbA("pm1_s", [128, 4])
    self32_s = sbA("self32_s", [32, 32 * 128])
    sel64_s = sbA("sel64_s", [32, 32 * 64], BF16)
    selP_s = sbA("selP_s", [32, 16 * 128])
    ones_rf = sbA("ones_rf", [1, 128])
    ones_cb = sbA("ones_cb", [128, 1], BF16)
    id_bs = sbA("id_bs", [128, 128], BF16)
    id_fs = sbA("id_fs", [128, 128])
    mtri = sbA("mtri", [128, 128])
    zc32 = sbA("zc32", [32, 1])
    zcol128 = sbA("zcol128", [128, 1])
    onec32 = sbA("onec32", [32, 1])
    el1 = sbA("el1", [1, 1])
    ef1 = sbA("ef1", [1, 1])
    nwsc = sbA("nwsc", [128, 16])
    tcl_r = sbA("tcl_r", [1, 1])

    def rowt(name, n=T):
        return sbA(name, [1, n])

    dtraw = sbA("dtraw", [32, TH])
    wst_f = sbA("wst_f", [128, 16, 128])
    wst_f2 = sbA("wst_f2", [128, 16, 128])
    wst_b = sbA("wst_b", [128, 16, 128], BF16)
    wst_b2 = sbA("wst_b2", [128, 16, 128], BF16)
    trst = sbA("trst", [128, 128], BF16)

    t_l = None
    for dst, src in [
        (cw_s[:], cwd), (cb_s[:], cbd), (inb_s[:], inb), (dtb_s[:], dtbd),
        (alog_s[:], alogd), (lnw_s[:], lnwd), (lnb_s[:], lnbd), (outb_s[:], outbd),
        (nw_s[:], nwd), (temp_s[:], tempd), (scl_s[:], scld), (pm_s[:], pmd),
        (pm1_s[:], pm1d), (self32_s[:], self32d), (sel64_s[:], sel64d),
        (selP_s[:], selPd), (ones_rf[:], onesrfd), (ones_cb[:], onescbd),
        (id_bs[:], idbd), (id_fs[:], idfd),
    ]:
        t_l = KY.dma(dst, src)
    tok_small = t_l

    KG.o("memset", mtri[:], 0.0)
    KG.d()
    KG.o("affine_select", mtri[:], mtri[:], pattern=[[1, 128]],
         compare_op=ALU.is_ge, fill=-BIG, base=0, channel_multiplier=-1, inc=True)

    KV.w(tok_small)
    KV.o("memset", zc32[:], 0.0)
    KV.o("memset", zcol128[:], 0.0)
    KV.o("memset", onec32[:], 1.0)
    KV.o("memset", el1[:], LNEPS)
    KV.o("memset", ef1[:], F32EPS)
    KV.o("tensor_scalar", nwsc[:], nw_s[:], scl_s[:, 0:1], None, ALU.mult)
    KV.o("tensor_copy", nw_bs[:], nw_s[:])
    KV.o("tensor_copy", lnwb2[:, :, 0], lnw_s[:])
    KV.o("tensor_copy", lnwb2[:, :, 1], lnb_s[:])
    KV.o("tensor_scalar", tcl_r[:], temp_s[:], 0.1, None, ALU.max, inc=True)

    KS.w(tok_small)
    KS.o("activation", nAcol[:], alog_s[:], AF.Exp, bias=zc32[:], scale=1.0)
    KS.d()
    KS.o("mul", nAcol[:], nAcol[:], -1.0, inc=True)
    kb.barrier_all()

    uf_st = sbA("uf_st", [128, 16, 257])
    u_b = sbA("u_b", [128, 16, TH], BF16)
    zst = sbA("zst", [128, 2, 1024])
    tok_uc = None
    for h in range(4):
        t_u = KY.dma(uf_st[:], u_fm.rearrange("(kc k) t -> k kc t", k=128)
                     [:, :, h * 257:(h + 1) * 257])
        KG.w(t_u)
        tok_uc = KG.o("tensor_copy", u_b[:, :, h * 257:(h + 1) * 257], uf_st[:],
                      inc=True)
        if h < 3:
            KY.w(tok_uc)
    sbF("uf_st")

    xBC = sbA("xBC", [128, 33, TH], BF16)
    sweeps = [(0, 512, p_big, 0), (512, 512, p_big, 512), (1024, 4, p_big2, 0)]
    tok_wbuf = [None, None]
    tok_wcast = [None, None]
    tok_zdma = [None, None]
    tok_evac_prev = None
    for oc in range(66):
        buf_f = wst_f if oc % 2 == 0 else wst_f2
        buf_b = wst_b if oc % 2 == 0 else wst_b2
        KY.ws([tok_wcast[oc % 2]])
        t_wl = KY.dma(buf_f[:], WinT[:, oc * 128:(oc + 1) * 128]
                      .rearrange("(kc k) m -> k kc m", k=128))
        KG.ws([t_wl, tok_wbuf[oc % 2], tok_uc])
        t_wc = KG.o("tensor_copy", buf_b[:], buf_f[:], inc=True)
        tok_wcast[oc % 2] = t_wc
        KT.w(t_wc)
        if tok_evac_prev is not None:
            KT.w(tok_evac_prev)
        last_mm = None
        for (t0, n, pp, po) in sweeps:
            for kc in range(16):
                tokm = KT.o("matmul", pp[:, po:po + n], buf_b[:, kc, :],
                            u_b[:, kc, t0:t0 + n], start=(kc == 0), stop=(kc == 15),
                            inc=(kc == 15))
                if kc == 15:
                    last_mm = tokm
        tok_wbuf[oc % 2] = last_mm
        KS.w(last_mm)
        if oc < 32:
            sl = oc % 2
            KS.w(tok_zdma[sl])
            KS.o("activation", zst[:, sl, 0:1020], p_big[:, 4:1024], AF.Silu,
                 bias=inb_s[:, oc:oc + 1], scale=1.0)
            KS.d()
            t_ev = KS.o("activation", zst[:, sl, 1020:1024], p_big2[:, 0:4], AF.Silu,
                        bias=inb_s[:, oc:oc + 1], scale=1.0, inc=True)
            KY.w(t_ev)
            tok_zdma[sl] = KY.dma(z_hbm[oc * 128:(oc + 1) * 128, :], zst[:, sl, :])
        elif oc < 65:
            c = oc - 32
            KV.w(last_mm)
            KV.o("tensor_scalar", xBC[:, c, 0:1024], p_big[:, 0:1024],
                 inb_s[:, oc:oc + 1], None, ALU.add)
            t_ev = KV.o("tensor_scalar", xBC[:, c, 1024:1028], p_big2[:, 0:4],
                        inb_s[:, oc:oc + 1], None, ALU.add, inc=True)
        else:
            KV.w(last_mm)
            KV.o("tensor_scalar", dtraw[:, 0:1024], p_big[0:32, 0:1024],
                 inb_s[0:32, oc:oc + 1], None, ALU.add)
            t_ev = KV.o("tensor_scalar", dtraw[:, 1024:1028], p_big2[0:32, 0:4],
                        inb_s[0:32, oc:oc + 1], None, ALU.add, inc=True)
        tok_evac_prev = t_ev
    kb.barrier_all()
    sbF("u_b", "zst", "wst_f", "wst_f2", "wst_b", "wst_b2")

    dt_sp = sbA("dt_sp", [32, T])
    Acum = sbA("Acum", [32, T])
    nAc = sbA("nAc", [32, T], hi=True)
    AcumT = sbA("AcumT", [128, NCH, 32], hi=True)
    dtT = sbA("dtT", [128, NCH, 32], hi=True)
    dtdsT = sbA("dtdsT", [128, NCH, 32], hi=True)
    sdo_b = sbA("sdo_b", [32, T], BF16, hi=True)
    Dmat = sbA("Dmat", [32, NCH])
    eLast = sbA("eLast", [32, NCH])
    eStart = sbA("eStart", [32, NCH])
    nStart = sbA("nStart", [32, NCH])
    Dagg = sbA("Dagg", [32, 1])
    Dp = sbA("Dp", [128, NCH, 16])
    zero32 = sbA("zero32", [32, T])
    sp_a = sbA("sp_a", [32, T])
    sp_b = sbA("sp_b", [32, T])
    Bn = sbA("Bn", [64, T], BF16, hi=True)
    Ctn = sbA("Ctn", [64, T], BF16, hi=True)
    Btok = sbA("Btok", [128, NCH, 64], BF16, hi=True)
    KV.o("memset", zero32[:], 0.0)
    KV.d()
    KS.o("activation", sp_a[:], dtraw[0:32, 4:1028], AF.Relu, bias=dtb_s[:], scale=1.0)
    KS.o("activation", sp_b[:], dtraw[0:32, 4:1028], AF.Abs, bias=dtb_s[:], scale=1.0)
    KS.d()
    KS.o("activation", sp_b[:], sp_b[:], AF.Exp, bias=zc32[:], scale=-1.0)
    KS.d()
    t_sp = KS.o("activation", sp_b[:], sp_b[:], AF.Ln, bias=onec32[:], scale=1.0,
                inc=True)
    KV.w(t_sp)
    KV.o("tensor_add", dt_sp[:], sp_a[:], sp_b[:])
    KV.d()
    KV.o("tensor_scalar", sp_a[:], dt_sp[:], nAcol[:], None, ALU.mult)
    KV.d()
    KV.o("tensor_tensor_scan", Acum[:], sp_a[:], zero32[:], 0.0, ALU.add, ALU.add)
    KV.d()
    KV.o("tensor_scalar", nAc[:], Acum[:], -1.0, None, ALU.mult)
    KV.o("tensor_copy", eLast[:], Acum.ap().rearrange("p (c l) -> p c l", l=128)[:, :, 127])
    KV.d()
    KV.o("memset", eStart[:, 0:1], 0.0)
    KV.o("tensor_copy", eStart[:, 1:8], eLast[:, 0:7])
    KV.d()
    KV.o("tensor_sub", eLast[:], eLast[:], eStart[:])
    KV.d()
    t_nst = KV.o("tensor_scalar", nStart[:], eStart[:], -1.0, None, ALU.mult, inc=True)
    KS.w(t_nst)
    KS.o("activation", Dmat[:], eLast[:], AF.Exp, bias=zc32[:], scale=1.0)
    KS.o("activation", Dagg[:], Acum[:, 1023:1024], AF.Exp, bias=zc32[:], scale=1.0)
    for ct in range(NCH):
        KS.o("activation", sdo_b[:, ct * 128:(ct + 1) * 128],
             Acum[:, ct * 128:(ct + 1) * 128], AF.Exp,
             bias=nStart[:, ct:ct + 1], scale=1.0)
    KS.d()
    t_ds = None
    for ct in range(NCH):
        t_ds = KS.o("activation", sp_b[:, ct * 128:(ct + 1) * 128],
                    nAc[:, ct * 128:(ct + 1) * 128], AF.Exp,
                    bias=Acum[:, ct * 128 + 127:ct * 128 + 128], scale=1.0,
                    inc=(ct == NCH - 1))
    KV.w(t_ds)
    t_dtds = KV.o("tensor_mul", sp_b[:], sp_b[:], dt_sp[:], inc=True)

    KT.w(t_dtds)
    for ct in range(NCH):
        for srcr, dstt in [(nAc, AcumT), (dt_sp, dtT), (sp_b, dtdsT)]:
            t_tr = KT.o("transpose", p_a[:, 0:32], srcr[:, ct * 128:(ct + 1) * 128],
                        id_fs[0:32, 0:32], inc=True)
            KV.w(t_tr)
            t_ev = KV.o("tensor_copy", dstt[:, ct, :], p_a[:, 0:32], inc=True)
            KT.w(t_ev)

    convq = sbA("convq", [128, T])
    xc_b = sbA("xc_b", [128, T], BF16)
    tok_xstore = None
    tok_trev = None
    tok_bt = None
    for c in range(33):
        KV.o("tensor_scalar", convq[:], xBC[:, c, 1:1025],
             cw_s[:, c * 4:c * 4 + 1], None, ALU.mult)
        for k in range(1, 3):
            KV.d()
            KV.o("scalar_tensor_tensor", convq[:], xBC[:, c, 1 + k:1025 + k],
                 cw_s[:, c * 4 + k:c * 4 + k + 1], convq[:], ALU.mult, ALU.add)
        KV.d()
        t_cq = KV.o("scalar_tensor_tensor", convq[:], xBC[:, c, 4:1028],
                    cw_s[:, c * 4 + 3:c * 4 + 4], convq[:], ALU.mult, ALU.add,
                    inc=True)
        KS.ws([t_cq, tok_trev, tok_bt])
        t_sil = KS.o("activation", xc_b[:], convq[:], AF.Silu,
                     bias=cb_s[:, c:c + 1], scale=1.0, inc=True)
        if DBG and c == 0:
            KY.w(t_sil)
            t_d = KY.dma(dbgE, xc_b[:])
            KY.w(t_d)
            t_d2 = KY.dma(dbgD, xBC[:, 0, :])
            KY.w(t_d2)
        KV.w(t_sil)
        if c < 32:
            for ct in range(NCH):
                KT.ws([t_sil, tok_trev])
                t_tr = KT.o("transpose", p_c[:], xc_b[:, ct * 128:(ct + 1) * 128],
                            id_bs[:], inc=True)
                KV.ws([t_tr, tok_xstore])
                tok_trev = KV.o("tensor_copy", trst[:], p_c[:], inc=True)
                KY.w(tok_trev)
                tok_xstore = KY.dma(
                    x_hbm[ct * 128:(ct + 1) * 128, c * 128:(c + 1) * 128], trst[:])
        else:
            KV.w(t_sil)
            KV.o("tensor_copy", Bn[:], xc_b[0:64, :])
            KV.d()
            t_bn = KV.o("tensor_copy", Ctn[:], xc_b[64:128, :], inc=True)
            KT.w(t_bn)
            for ct in range(NCH):
                t_tr = KT.o("transpose", p_c[:, 0:64],
                            Bn[:, ct * 128:(ct + 1) * 128], id_bs[0:64, 0:64],
                            inc=True)
                KV.w(t_tr)
                tok_bt = KV.o("tensor_copy", Btok[:, ct, :], p_c[:, 0:64], inc=True)
                KT.w(tok_bt)
    if DBG or DBG3:
        t_d = KY.dma(dbgA, Acum[:])
        KY.w(t_d)
    kb.barrier_all()
    sbF("xBC", "convq", "xc_b", "dt_sp", "zero32", "sp_a", "sp_b", "dtraw")

    StIn = sbA("StIn", [128, NCH, 16, 128], BF16, hi=True)
    tmat = sbA("tmat", [128, 32, T], BF16, hi=True)
    agg = sbA("agg", [128, 16, 128])
    xchk = sbA("xchk", [128, 32, 128], BF16)
    xdt_ds = sbA("xdt_ds", [128, 128], BF16)
    sstg = sbA("sstg", [128, 128], BF16)
    sslice = sbA("sslice", [128, 16, 128], BF16)
    Dap = sbA("Dap", [128, 16])

    for j in range(16):
        t_mm = KT.o("matmul", p_a[:, 0:8], selP_s[:, j * 128:(j + 1) * 128],
                    Dmat[:], start=True, stop=True, inc=True)
        KV.w(t_mm)
        t_dp = KV.o("tensor_copy", Dp[:, :, j], p_a[:, 0:8], inc=True)
        KT.w(t_dp)
        t_mm = KT.o("matmul", p_a[:, 0:1], selP_s[:, j * 128:(j + 1) * 128],
                    Dagg[:], start=True, stop=True, inc=True)
        KV.w(t_mm)
        t_dap = KV.o("tensor_copy", Dap[:, j:j + 1], p_a[:, 0:1], inc=True)
        KT.w(t_dap)

    tok_schunk = None
    tok_sdma = None
    tok_xwar = None
    for ct in range(NCH):
        KY.w(tok_xwar)
        t_x = KY.dma(xchk[:], x_hbm[ct * 128:(ct + 1) * 128, :]
                     .rearrange("p (hc q) -> p hc q", q=128))
        KV.w(t_x)
        for h in range(NH):
            t_xs = KV.o("tensor_scalar", xdt_ds[:], xchk[:, h, :],
                        dtdsT[:, ct, h:h + 1], None, ALU.mult, inc=True)
            KT.w(t_xs)
            half = 64 * (h % 2)
            t_mm = KT.o("matmul", p_b[0:64, :], Btok[:, ct, :], xdt_ds[:],
                        start=True, stop=True, inc=True)
            KV.ws([t_mm, tok_sdma])
            tok_schunk = KV.o("tensor_copy", sstg[half:half + 64, :],
                              p_b[0:64, :], inc=True)
            KT.w(tok_schunk)
            KV.d()
            KY.w(tok_schunk)
            tok_sdma = KY.dma(
                s_hbm[half:half + 64,
                      (ct * 16 + h // 2) * 128:(ct * 16 + h // 2 + 1) * 128],
                sstg[half:half + 64, :])
        tok_xwar = tok_schunk
    t_ss = None
    for ct in range(NCH):
        KY.ws([tok_sdma, t_ss])
        t_sl = KY.dma(sslice[:].rearrange("p a b -> p (a b)"),
                      s_hbm[:, ct * 2048:(ct + 1) * 2048])
        KV.w(t_sl)
        if ct == 0:
            KV.o("tensor_copy", agg[:].rearrange("p a b -> p (a b)"),
                 sslice[:].rearrange("p a b -> p (a b)"))
        else:
            for j in range(16):
                KV.o("scalar_tensor_tensor", agg[:, j, :], agg[:, j, :],
                     Dp[:, ct, j:j + 1], sslice[:, j, :], ALU.mult, ALU.add)
        KV.d()
        t_ss = KV.o("tensor_copy", Dap[0:1, 0:1], Dap[0:1, 0:1], inc=True)
    ccstg = sbA("ccstg", [128, 16 * 128 + 16])
    KV.o("tensor_copy", ccstg[:, 0:2048], agg[:].rearrange("p a b -> p (a b)"))
    KV.d()
    t_ccs = KV.o("tensor_copy", ccstg[:, 2048:2064], Dap[:], inc=True)
    KG.w(t_ccs)
    if DBG or DBG3:
        KY.w(t_ccs)
        t_d = KY.dma(dbgH, ccstg[:])
        KY.w(t_d)
    t_ci = KG.dma(cc_in, ccstg[:])
    KG.w(t_ci)
    t_cc = KG.cc("AllGather", ALU.bypass, [[0, 1, 2, 3], [4, 5, 6, 7]],
                 [cc_in.opt()], [cc_out.opt()])
    KG.w(t_cc)
    if DBG or DBGCC:
        t_d = KG.dma(dbgI, cc_out)
        KG.w(t_d)
    gsl = sbA("gsl", [128, 2048])
    Dg4 = sbA("Dg4", [128, 4, 16])
    Ecf = sbA("Ecf", [128, 4, 16])
    Wcf = sbA("Wcf", [128, 4, 16])
    t_go = None
    for r in range(4):
        t_go = KG.dma(Dg4[:, r, :], cc_out[r * 128:(r + 1) * 128, 2048:2064])
    KV.w(t_go)
    for r in range(4):
        KV.o("tensor_scalar", Ecf[:, r, :], Dg4[:, r, :],
             pm_s[:, r:r + 1], None, ALU.mult)
        KV.d()
        KV.o("tensor_scalar", Ecf[:, r, :], Ecf[:, r, :], pm1_s[:, r:r + 1],
             None, ALU.add)
        KV.d()
    KV.o("memset", Wcf[:, 3, :], 1.0)
    KV.d()
    KV.o("tensor_copy", Wcf[:, 2, :], Ecf[:, 3, :])
    KV.d()
    KV.o("tensor_mul", Wcf[:, 1, :], Wcf[:, 2, :], Ecf[:, 2, :])
    KV.d()
    KV.o("tensor_mul", Wcf[:, 0, :], Wcf[:, 1, :], Ecf[:, 1, :])
    KV.d()
    for r in range(4):
        KV.o("tensor_scalar", Wcf[:, r, :], Wcf[:, r, :], pm_s[:, r:r + 1],
             None, ALU.mult)
    KV.d()
    KV.o("memset", agg[:].rearrange("p a b -> p (a b)"), 0.0)
    t_gz = KV.o("tensor_copy", Dap[0:1, 0:1], Dap[0:1, 0:1], inc=True)
    for r in range(4):
        KG.ws([t_gz])
        t_gs = KG.dma(gsl[:], cc_out[r * 128:(r + 1) * 128, 0:2048])
        KV.w(t_gs)
        for j in range(16):
            KV.o("scalar_tensor_tensor", agg[:, j, :],
                 gsl[:, j * 128:(j + 1) * 128], Wcf[:, r, j:j + 1],
                 agg[:, j, :], ALU.mult, ALU.add)
        KV.d()
        t_gz = KV.o("tensor_copy", Dap[0:1, 0:1], Dap[0:1, 0:1], inc=True)
    for ct in range(NCH):
        for j in range(16):
            KV.o("tensor_copy", StIn[:, ct, j, :], agg[:, j, :])
        KV.d()
        if ct < NCH - 1:
            KY.w(t_ss)
            t_sl = KY.dma(sslice[:].rearrange("p a b -> p (a b)"),
                          s_hbm[:, ct * 2048:(ct + 1) * 2048])
            KV.w(t_sl)
            for j in range(16):
                KV.o("scalar_tensor_tensor", agg[:, j, :], agg[:, j, :],
                     Dp[:, ct, j:j + 1], sslice[:, j, :], ALU.mult, ALU.add)
            KV.d()
            t_ss = KV.o("tensor_copy", Dap[0:1, 0:1], Dap[0:1, 0:1], inc=True)
    t_dbgs = KV.o("tensor_copy", Dap[0:1, 0:1], Dap[0:1, 0:1], inc=True)
    if DBG or DBG3:
        KY.w(t_dbgs)
        t_d = KY.dma(dbgC, StIn[:, 0, 0, 0:64])
        KY.w(t_d)
    kb.barrier_all()
    sbF("agg", "ccstg", "gsl", "Dg4", "Ecf", "Wcf", "Dap", "Btok", "selP_s",
        "Dmat", "eLast", "eStart", "nStart", "Dagg", "Dp", "sstg", "sslice")

    Pm = sbA("Pm", [128, 128], BF16)
    Eh = sbA("Eh", [128, 128], BF16)
    Th_ = sbA("Th_", [128, 128], BF16)
    ygate = sbA("ygate", [128, 128])
    ybf = sbA("ybf", [128, 128], BF16)
    ysq = sbA("ysq", [128, 128], BF16)
    xdt_p = sbA("xdt_p", [128, 128], BF16)
    CSs = sbA("CSs", [128, 128], BF16)
    ztile = sbA("ztile", [128, 32, 128])
    sry_r = rowt("sry_r"); sry2_r = rowt("sry2_r")

    tok_gate = None
    tok_th = None
    tok_statev = None
    tok_xdtp = None
    t_smm = None
    t_ysq = None
    for ct in range(NCH):
        KY.ws([tok_xdtp])
        t_x = KY.dma(xchk[:], x_hbm[ct * 128:(ct + 1) * 128, :]
                     .rearrange("p (hc q) -> p hc q", q=128))
        KY.w(tok_gate)
        t_z = KY.dma(ztile[:], z_hbm[:, ct * 128:(ct + 1) * 128]
                     .rearrange("(hc p) q -> p hc q", p=128))
        KT.ws([tok_th])
        t_pmm = KT.o("matmul", p_b[:], Bn[:, ct * 128:(ct + 1) * 128],
                     Ctn[:, ct * 128:(ct + 1) * 128], start=True, stop=True, inc=True)
        KV.w(t_pmm)
        t_pev = KV.o("tensor_copy", Pm[:], p_b[:], inc=True)
        KT.w(t_pev)
        for h in range(NH):
            KV.ws([t_x, tok_statev])
            tok_xdtp = KV.o("tensor_scalar", xdt_p[:], xchk[:, h, :],
                            dtT[:, ct, h:h + 1], None, ALU.mult, inc=True)
            half = 64 * (h % 2)
            t_csm = KT.o("matmul", p_b[0:64, :128], sel64_s[:, h * 64:(h + 1) * 64],
                         sdo_b[:, ct * 128:(ct + 1) * 128], start=True, stop=True,
                         inc=True)
            KV.w(t_csm)
            t_css = KV.o("tensor_mul", CSs[half:half + 64, :],
                         Ctn[:, ct * 128:(ct + 1) * 128],
                         p_b[0:64, :128], inc=True)
            KT.w(t_css)
            KT.ws([tok_gate])
            KT.o("matmul", p_a[:], self32_s[:, h * 128:(h + 1) * 128],
                 Acum[:, ct * 128:(ct + 1) * 128], start=True, stop=False)
            t_emm = KT.o("matmul", p_a[:], id_fs[:], mtri[:], start=False, stop=True,
                         inc=True)
            KS.w(t_emm)
            t_eh = KS.o("activation", Eh[:], p_a[:], AF.Exp,
                        bias=AcumT[:, ct, h:h + 1], scale=1.0, inc=True)
            KV.ws([t_eh, t_pev])
            tok_th = KV.o("tensor_mul", Th_[:], Pm[:], Eh[:], inc=True)
            KT.ws([tok_th, tok_xdtp, t_eh])
            KT.o("matmul", p_a[:], xdt_p[:], Th_[:], start=True, stop=False)
            t_ymm = KT.o("matmul", p_a[:], StIn[half:half + 64, ct, h // 2, :],
                         CSs[half:half + 64, :], start=False, stop=True, inc=True)
            KV.ws([t_ymm, t_z, t_smm, t_ysq])
            tok_gate = KV.o("tensor_mul", ygate[:], p_a[:], ztile[:, h, :], inc=True)
            KV.d()
            t_ybf = KV.o("tensor_copy", ybf[:], ygate[:], inc=True)
            KS.ws([tok_gate, t_smm])
            t_ysq = KS.o("square", ysq[:], ygate[:], inc=True)
            t_tm = KV.o("tensor_scalar", tmat[:, h, ct * 128:(ct + 1) * 128],
                        ygate[:], lnw_s[:, h:h + 1], lnb_s[:, h:h + 1],
                        ALU.mult, op1=ALU.add, inc=True)
            q0 = 256 * (ct % 2)
            KT.ws([t_ybf, t_ysq, tok_statev])
            KT.o("matmul", p_ra[0:1, q0:q0 + 128], ones_cb[:], ybf[:],
                 start=(h == 0), stop=(h == NH - 1))
            t_smm = KT.o("matmul", p_ra[0:1, q0 + 128:q0 + 256], ones_cb[:], ysq[:],
                         start=(h == 0), stop=(h == NH - 1), inc=True)
            if h == NH - 1:
                KV.w(t_smm)
                KV.o("tensor_copy", sry_r[0:1, ct * 128:(ct + 1) * 128],
                     p_ra[0:1, q0:q0 + 128])
                KV.d()
                tok_statev = KV.o("tensor_copy", sry2_r[0:1, ct * 128:(ct + 1) * 128],
                                  p_ra[0:1, q0 + 128:q0 + 256], inc=True)
    kb.barrier_all()
    sbF("CSs", "xchk", "xdt_ds", "ztile", "Pm", "Eh", "Th_", "ygate", "ybf", "ysq",
        "xdt_p", "StIn", "self32_s", "sel64_s", "nAc", "AcumT", "dtT", "dtdsT",
        "sdo_b", "Bn", "Ctn", "Acum")

    muy_r = rowt("muy_r"); ay_r = rowt("ay_r")
    f1_r = rowt("f1_r"); f2_r = rowt("f2_r")
    f12 = sbA("f12", [2, T])
    tmp_r = rowt("tmp_r"); tmp2_r = rowt("tmp2_r"); tmp3_r = rowt("tmp3_r")
    KV.o("tensor_scalar", muy_r[:], sry_r[:], 1.0 / DI, None, ALU.mult)
    KV.o("tensor_scalar", tmp_r[:], sry2_r[:], 1.0 / DI, None, ALU.mult)
    KV.d()
    KV.o("tensor_mul", tmp2_r[:], muy_r[:], muy_r[:])
    KV.d()
    t_var = KV.o("tensor_sub", tmp_r[:], tmp_r[:], tmp2_r[:], inc=True)
    KS.w(t_var)
    t_sd = KS.o("activation", tmp2_r[:], tmp_r[:], AF.Sqrt, bias=el1[:], scale=1.0,
                inc=True)
    KV.w(t_sd)
    KV.o("reciprocal", ay_r[:], tmp2_r[:])
    KV.d()
    KV.o("tensor_mul", f1_r[:], muy_r[:], ay_r[:])
    KV.d()
    KV.o("tensor_scalar", f1_r[:], f1_r[:], -1.0, None, ALU.mult)
    KV.o("tensor_scalar", f2_r[:], ay_r[:], -1.0, None, ALU.mult)
    KV.d()
    KV.o("tensor_scalar", f2_r[:], f2_r[:], 1.0, None, ALU.add)
    t_rows = KV.o("tensor_copy", tmp3_r[0:1, 0:1], tmp3_r[0:1, 0:1], inc=True)
    KY.w(t_rows)
    t_f12a = KY.dma(f12[0:1, :], f1_r[:])
    t_f12b = KY.dma(f12[1:2, :], f2_r[:])
    if DBG:
        KY.ws([("ev", kb.cnt["ev"]), ("es", kb.cnt["es"])])
        for i, rr in enumerate([sry_r, sry2_r, muy_r, ay_r, f1_r, f2_r]):
            KY.dma(dbgF[i:i + 1, :], rr[:])
        t_d = KY.dma(dbgG, tmat[:, 0, :])
        KY.w(t_d)
    ay_rep = sbA("ay_rep", [128, T])
    KT.w(t_rows)
    for hh in range(2):
        t_mm = KT.o("matmul", p_big[:, hh * 512:hh * 512 + 512], ones_rf[:],
                    ay_r[0:1, hh * 512:(hh + 1) * 512], start=True, stop=True,
                    inc=True)
        KV.w(t_mm)
        t_ayr = KV.o("tensor_copy", ay_rep[:, hh * 512:(hh + 1) * 512],
                     p_big[:, hh * 512:hh * 512 + 512], inc=True)
        KT.w(t_ayr)
    kb.barrier_all()

    wst_f = sbA("wst5_f", [128, 16, 128])
    wst_f2 = sbA("wst5_f2", [128, 16, 128])
    wst_b = sbA("wst5_b", [128, 16, 128], BF16)
    wst_b2 = sbA("wst5_b2", [128, 16, 128], BF16)
    stg1 = sbA("stg1", [128, T])
    stg2 = sbA("stg2", [128, T])
    ust = sbA("ust", [128, T])
    bfst = sbA("bfst", [128, T], BF16)
    sqst = sbA("sqst", [128, T], BF16)
    v12 = sbA("v12", [128, 16, 2])
    r3f = sbA("r3f", [2, 16, 128])
    sr_r = rowt("sr_r"); sr2_r = rowt("sr2_r")
    KV.o("memset", sr_r[:], 0.0)
    KV.o("memset", sr2_r[:], 0.0, inc=True)

    tok_wdone = [None, None]
    tok_wcast5 = [None, None]
    tok_r1dma = None
    tok_stadd = None
    tok_r1tok = None
    for dm in range(16):
        t_wc = [None, None]
        for hf in range(2):
            buf_f = wst_f if hf == 0 else wst_f2
            buf_b = wst_b if hf == 0 else wst_b2
            KY.ws([tok_wcast5[hf]])
            t_wl = KY.dma(buf_f[:], WoutT[hf * 2048:(hf + 1) * 2048,
                                          dm * 128:(dm + 1) * 128]
                          .rearrange("(kc k) m -> k kc m", k=128))
            KG.ws([t_wl, tok_wdone[hf]])
            t_wc[hf] = KG.o("tensor_copy", buf_b[:], buf_f[:], inc=True)
            tok_wcast5[hf] = t_wc[hf]
        KT.ws(t_wc)
        for kcg in range(32):
            buf_b = wst_b if kcg < 16 else wst_b2
            t_vmm = KT.o("matmul", p_b[:, 0:2], buf_b[:, kcg % 16, :],
                         lnwb2[:, kcg, :], start=(kcg == 0), stop=(kcg == 31),
                         inc=(kcg == 31))
        KV.w(t_vmm)
        t_v12 = KV.o("tensor_copy", v12[:, dm, :], p_b[:, 0:2], inc=True)
        KT.w(t_v12)
        t_tr = KT.o("transpose", p_a[0:2, 0:128], v12[:, dm, :], id_fs[:], inc=True)
        KV.w(t_tr)
        t_r3 = KV.o("tensor_copy", r3f[:, dm, :], p_a[0:2, 0:128], inc=True)
        KT.ws([t_r3, ("sq", kb.cnt["sq"])])
        for hh in range(2):
            t_r2m = KT.o("matmul", p_big2[:, hh * 512:hh * 512 + 512], r3f[:, dm, :],
                         f12[:, hh * 512:(hh + 1) * 512], start=True, stop=True,
                         inc=(hh == 1))
        for kcg in range(32):
            buf_b = wst_b if kcg < 16 else wst_b2
            for hh in range(2):
                t_mmm = KT.o("matmul", p_big[:, hh * 512:(hh + 1) * 512],
                             buf_b[:, kcg % 16, :],
                             tmat[:, kcg, hh * 512:(hh + 1) * 512],
                             start=(kcg == 0), stop=(kcg == 31),
                             inc=(kcg == 31 and hh == 1))
        tok_wdone[0] = t_mmm
        tok_wdone[1] = t_mmm
        KY.ws([tok_r1tok])
        t_ul = KY.dma(ust[:], u_fm[dm * 128:(dm + 1) * 128, 4:1028])
        KV.ws([t_r2m, t_mmm, t_ul, tok_r1dma])
        t_s1 = KV.o("tensor_scalar", stg1[:], p_big2[:],
                    outb_s[:, dm:dm + 1], None, ALU.add, inc=True)
        KV.o("tensor_mul", stg2[:], p_big[:], ay_rep[:])
        KV.d()
        KV.o("tensor_add", stg2[:], stg2[:], stg1[:])
        KV.d()
        t_r1 = KV.o("tensor_add", stg2[:], stg2[:], ust[:], inc=True)
        tok_r1tok = t_r1
        KY.w(t_r1)
        tok_r1dma = KY.dma(r1_hbm[dm * 128:(dm + 1) * 128, :], stg2[:])
        KV.d()
        t_bf = KV.o("tensor_copy", bfst[:], stg2[:], inc=True)
        KS.w(t_r1)
        t_sq = KS.o("square", sqst[:], stg2[:], inc=True)
        for hh in range(2):
            KT.ws([t_bf, t_sq, tok_stadd])
            t_sa = KT.o("matmul", p_ra[0:1, 0:512], ones_cb[:],
                        bfst[:, hh * 512:(hh + 1) * 512], start=True, stop=True,
                        inc=True)
            KV.w(t_sa)
            KV.o("tensor_add", sr_r[0:1, hh * 512:(hh + 1) * 512],
                 sr_r[0:1, hh * 512:(hh + 1) * 512], p_ra[0:1, 0:512])
            KV.d()
            t_sb_tok = KV.o("tensor_copy", tmp_r[0:1, 0:1], tmp_r[0:1, 0:1], inc=True)
            KT.w(t_sb_tok)
            t_sa = KT.o("matmul", p_ra[0:1, 0:512], ones_cb[:],
                        sqst[:, hh * 512:(hh + 1) * 512], start=True, stop=True,
                        inc=True)
            KV.w(t_sa)
            KV.o("tensor_add", sr2_r[0:1, hh * 512:(hh + 1) * 512],
                 sr2_r[0:1, hh * 512:(hh + 1) * 512], p_ra[0:1, 0:512])
            KV.d()
            tok_stadd = KV.o("tensor_copy", tmp_r[0:1, 0:1], tmp_r[0:1, 0:1],
                             inc=True)
    kb.barrier_all()
    sbF("stg1", "ust", "v12", "r3f", "ay_rep", "tmat", "sry_r", "sry2_r",
        "muy_r", "ay_r", "f1_r", "f2_r", "f12", "wst5_f", "wst5_f2", "wst5_b",
        "wst5_b2")

    mu1_r = rowt("mu1_r"); a1_r = rowt("a1_r"); b1_r = rowt("b1_r")
    c_r = rowt("c_r"); ca_r = rowt("ca_r"); cb_r = rowt("cb_r")
    KV.o("tensor_scalar", mu1_r[:], sr_r[:], 1.0 / D, None, ALU.mult)
    KV.o("tensor_scalar", tmp_r[:], sr2_r[:], 1.0 / D, None, ALU.mult)
    KV.d()
    KV.o("tensor_mul", tmp2_r[:], mu1_r[:], mu1_r[:])
    KV.d()
    t_var = KV.o("tensor_sub", tmp_r[:], tmp_r[:], tmp2_r[:], inc=True)
    KS.w(t_var)
    t_sd = KS.o("activation", tmp2_r[:], tmp_r[:], AF.Sqrt, bias=el1[:], scale=1.0,
                inc=True)
    KV.w(t_sd)
    KV.o("reciprocal", a1_r[:], tmp2_r[:])
    KV.d()
    KV.o("tensor_mul", b1_r[:], mu1_r[:], a1_r[:])
    KV.d()
    KV.o("tensor_scalar", b1_r[:], b1_r[:], -1.0, None, ALU.mult)
    KV.d()
    KV.o("tensor_mul", tmp_r[:], a1_r[:], a1_r[:])
    KV.d()
    KV.o("tensor_mul", tmp_r[:], tmp_r[:], sr2_r[:])
    KV.d()
    KV.o("tensor_scalar", tmp_r[:], tmp_r[:], 1.0 / D, None, ALU.mult)
    KV.o("tensor_mul", tmp2_r[:], a1_r[:], b1_r[:])
    KV.d()
    KV.o("tensor_mul", tmp2_r[:], tmp2_r[:], sr_r[:])
    KV.d()
    KV.o("tensor_scalar", tmp2_r[:], tmp2_r[:], 2.0 / D, None, ALU.mult)
    KV.d()
    KV.o("tensor_add", tmp_r[:], tmp_r[:], tmp2_r[:])
    KV.o("tensor_mul", tmp3_r[:], b1_r[:], b1_r[:])
    KV.d()
    t_msx = KV.o("tensor_add", tmp_r[:], tmp_r[:], tmp3_r[:], inc=True)
    KS.w(t_msx)
    t_sd = KS.o("activation", tmp2_r[:], tmp_r[:], AF.Sqrt, bias=ef1[:], scale=1.0,
                inc=True)
    KV.w(t_sd)
    KV.o("reciprocal", c_r[:], tmp2_r[:])
    KV.d()
    KV.o("reciprocal", tmp3_r[0:1, 0:1], tcl_r[:])
    KV.d()
    KV.o("tensor_mul", ca_r[:], c_r[:], a1_r[:])
    KV.d()
    KV.o("tensor_scalar", ca_r[:], ca_r[:], tmp3_r[0:1, 0:1], None, ALU.mult)
    KV.o("tensor_mul", cb_r[:], c_r[:], b1_r[:])
    KV.d()
    t_cb = KV.o("tensor_scalar", cb_r[:], cb_r[:], tmp3_r[0:1, 0:1], None, ALU.mult,
                inc=True)
    a1_rep = sbA("a1_rep", [128, T])
    b1_rep = sbA("b1_rep", [128, T])
    c_rep = sbA("c_rep", [128, T])
    ca_rep = sbA("ca_rep", [128, T])
    KT.w(t_cb)
    tok_repev = None
    for src_r, dst in [(a1_r, a1_rep), (b1_r, b1_rep), (c_r, c_rep), (ca_r, ca_rep)]:
        for hh in range(2):
            KT.w(tok_repev)
            t_mm = KT.o("matmul", p_big[:, 0:512], ones_rf[:],
                        src_r[0:1, hh * 512:(hh + 1) * 512], start=True, stop=True,
                        inc=True)
            KV.w(t_mm)
            tok_repev = KV.o("tensor_copy", dst[:, hh * 512:(hh + 1) * 512],
                             p_big[:, 0:512], inc=True)
    kb.barrier_all()
    sbF("sr_r", "sr2_r", "mu1_r", "a1_r", "b1_r", "c_r", "ca_r", "tmp_r",
        "tmp2_r", "tmp3_r")

    genes_f = sbA("genes_f", [128, 32, GD])
    genes_b = sbA("genes_b", [128, 32, GD], BF16)
    WdT_b = sbA("WdT_b", [128, 2, 16, 128], BF16)
    WuT_b = sbA("WuT_b", [128, 2, 16, 128], BF16)
    t_gl = KY.dma(genes_f[:], genesd.rearrange("(gc p) n -> p gc n", p=128))
    KG.w(t_gl)
    t_gc = KG.o("tensor_copy", genes_b[:].rearrange("p a b -> p (a b)"),
                genes_f[:].rearrange("p a b -> p (a b)"), inc=True)
    for dst, srcw in [(WdT_b, WdTd), (WuT_b, WuTd)]:
        KY.w(t_gc)
        stview = genes_f[:].rearrange("p a b -> p (a b)")[:, 0:4096]
        t_wl = KY.dma(stview.rearrange("p (kk m) -> p kk m", kk=2),
                      srcw.rearrange("(kk p) m -> p kk m", p=128))
        KG.w(t_wl)
        t_gc = KG.o("tensor_copy", dst[:].rearrange("p a b c -> p (a b c)"),
                    stview, inc=True)
    kb.barrier_all()
    sbF("genes_f")

    lgT = sbA("lgT", [128, 2, 32, 128])
    scr = sbA("scr", [128, G])
    tv = sbA("tv", [128, 32])
    tvE = sbA("tvE", [128, 32])
    negm = sbA("negm", [128, 1])
    sw = sbA("sw", [128, 1])
    rden = sbA("rden", [128, 1])
    wm = sbA("wm", [128, G], BF16)
    wT = sbA("wT", [128, 32, 128], BF16)
    ex_b = sbA("ex_b", [128, GD], BF16)
    eT = sbA("eT", [128, 2, 128], BF16)
    r2b = sbA("r2b", [128, 16, GRP])
    r1s = sbA("r1s", [128, GRP])
    xnb = sbA("xnb", [128, 16, GRP], BF16)
    lgst = sbA("lgst", [128, GRP])
    xtmp = sbA("xtmp", [128, 128])
    tmp_r = rowt("tmp_r", GRP); tmp2_r = rowt("tmp2_r", GRP)
    dtile = sbA("dtile", [128, 128])
    utile = sbA("utile", [128, 128])
    m1t = sbA("m1t", [128, 128])
    g2t = sbA("g2t", [128, 128])
    m2_rep = sbA("m2_rep", [128, GRP])
    s2_rep = sbA("s2_rep", [128, GRP])
    gout = sbA("gout", [128, GRP])
    wh_f = sbA("wh_f", [128, 8, 128])
    wh_f2 = sbA("wh_f2", [128, 8, 128])
    wh_b = sbA("wh_b", [128, 8, 128], BF16)
    wh_b2 = sbA("wh_b2", [128, 8, 128], BF16)
    wcnw_b = sbA("wcnw_b", [1, G], BF16)
    cb_b = sbA("cb_b", [1, T], BF16)
    mu2_r = rowt("mu2_r", GRP); rs2_r = rowt("rs2_r", GRP)
    t2a_r = rowt("t2a_r", GRP); t2b_r = rowt("t2b_r", GRP)

    KV.o("tensor_copy", cb_b[:], cb_r[:])
    KV.o("tensor_copy", negm[:], negm[:], inc=True)
    tok_outdma = None
    tok_lgev = None
    tok_lgtr = None
    tok_xnwar = None
    tok_wcbuf = [None, None]
    tok_wccast = [None, None]
    for g in range(NGRP):
        g0 = g * GRP
        t_x4 = None
        for dm in range(16):
            KY.ws([tok_xnwar])
            t_r1g = KY.dma(r1s[:], r1_hbm[dm * 128:(dm + 1) * 128, g0:g0 + GRP])
            KV.w(t_r1g)
            t_x4 = KV.o("tensor_scalar", xnb[:, dm, :], r1s[:],
                        nw_s[:, dm:dm + 1], None, ALU.mult, inc=True)
            tok_xnwar = t_x4
        for gc in range(32):
            t_lmm = None
            for hf in range(2):
                buf_f = wh_f if hf == 0 else wh_f2
                buf_b = wh_b if hf == 0 else wh_b2
                KY.ws([tok_wccast[hf]])
                t_wl = KY.dma(buf_f[:], WcT[hf * 1024:(hf + 1) * 1024,
                                            gc * 128:(gc + 1) * 128]
                              .rearrange("(kc k) m -> k kc m", k=128))
                KG.ws([t_wl, tok_wcbuf[hf]])
                t_wc = KG.o("tensor_copy", buf_b[:], buf_f[:], inc=True)
                tok_wccast[hf] = t_wc
                KT.ws([t_wc, t_x4, tok_lgev])
                for kc in range(8):
                    t_lmm = KT.o("matmul", p_big[:, 0:GRP], buf_b[:, kc, :],
                                 xnb[:, hf * 8 + kc, :],
                                 start=(hf == 0 and kc == 0),
                                 stop=(hf == 1 and kc == 7),
                                 inc=(kc == 7))
                tok_wcbuf[hf] = t_lmm
            if g == 0:
                for hf in range(2):
                    buf_b = wh_b if hf == 0 else wh_b2
                    for kc in range(8):
                        t_wv = KT.o("matmul", p_b[:, 0:1], buf_b[:, kc, :],
                                    nw_bs[:, hf * 8 + kc:hf * 8 + kc + 1],
                                    start=(hf == 0 and kc == 0),
                                    stop=(hf == 1 and kc == 7),
                                    inc=(hf == 1 and kc == 7))
                tok_wcbuf[0] = t_wv
                tok_wcbuf[1] = t_wv
                KV.w(t_wv)
                t_wcol = KV.o("tensor_copy", negm[:], p_b[:, 0:1], inc=True)
                KT.w(t_wcol)
                t_wt = KT.o("transpose", p_a[0:1, 0:128], negm[:], id_fs[:],
                            inc=True)
                KV.w(t_wt)
                t_wrow = KV.o("tensor_copy", wcnw_b[0:1, gc * 128:(gc + 1) * 128],
                              p_a[0:1, 0:128], inc=True)
                KT.w(t_wrow)
            t_r1m = KT.o("matmul", p_big2[:, 0:GRP],
                         wcnw_b[0:1, gc * 128:(gc + 1) * 128],
                         cb_b[0:1, g0:g0 + GRP], start=True, stop=True, inc=True)
            KV.ws([t_lmm, t_r1m, tok_lgtr])
            KV.o("tensor_mul", lgst[:], p_big[:, 0:GRP], ca_rep[:, g0:g0 + GRP])
            KV.d()
            t_lg = KV.o("tensor_add", lgst[:], lgst[:], p_big2[:, 0:GRP], inc=True)
            for tc in range(2):
                KT.ws([t_lg, tok_lgev])
                t_tr = KT.o("transpose", p_a[:], lgst[:, tc * 128:(tc + 1) * 128],
                            id_fs[:], inc=True)
                tok_lgtr = t_tr
                KV.w(t_tr)
                tok_lgev = KV.o("tensor_copy", lgT[:, tc, gc, :], p_a[:], inc=True)
        if DBG and g == 0:
            KY.ws([tok_lgev])
            t_d = KY.dma(dbgB, lgT[:, 0, :, :].rearrange("p a b -> p (a b)"))
            KY.w(t_d)
        t2z = KV.o("memset", t2a_r[:], 0.0)
        KV.o("memset", t2b_r[:], 0.0)
        KV.d()
        tok_r2 = None
        tok_l2 = None
        tok_r1war = None
        for tc in range(2):
            lg = lgT[:, tc, :, :].rearrange("p a b -> p (a b)")
            KV.ws([tok_lgev])
            KV.o("max", out=tv[:, 0:8], in_=lg)
            KV.d()
            KV.o("match_replace", out=scr[:], in_to_replace=tv[:, 0:8],
                 in_values=lg, imm_value=-BIG)
            KV.d()
            KV.o("max", out=tv[:, 8:16], in_=scr[:])
            KV.d()
            KV.o("match_replace", out=scr[:], in_to_replace=tv[:, 8:16],
                 in_values=scr[:], imm_value=-BIG)
            KV.d()
            KV.o("max", out=tv[:, 16:24], in_=scr[:])
            KV.d()
            KV.o("match_replace", out=scr[:], in_to_replace=tv[:, 16:24],
                 in_values=scr[:], imm_value=-BIG)
            KV.d()
            KV.o("max", out=tv[:, 24:32], in_=scr[:])
            KV.d()
            t_negm = KV.o("tensor_scalar", negm[:], tv[:, 0:1], -1.0, None, ALU.mult,
                          inc=True)
            KS.w(t_negm)
            t_sw = KS.o("activation", tvE[:], tv[:], AF.Exp, bias=negm[:], scale=1.0,
                        accum_out=sw[:], inc=True)
            KV.w(t_sw)
            t_rden = KV.o("reciprocal", rden[:], sw[:], inc=True)
            t_ef = KS.o("activation", wm[:], lg, AF.Exp, bias=negm[:], scale=1.0,
                        inc=True)
            KV.w(t_ef)
            t_wm = KV.o("scalar_tensor_tensor", wm[:], lg, tv[:, 31:32], wm[:],
                        ALU.is_ge, ALU.mult, inc=True)
            tok_wtev = None
            for gc in range(32):
                KT.ws([t_wm, tok_wtev])
                t_tr = KT.o("transpose", p_c[:], wm[:, gc * 128:(gc + 1) * 128],
                            id_bs[:], inc=True)
                KV.w(t_tr)
                tok_wtev = KV.o("tensor_copy", wT[:, gc, :], p_c[:], inc=True)
            KT.ws([tok_wtev, tok_r2])
            for gc in range(32):
                t_xmm = KT.o("matmul", p_big[:, 0:GD], wT[:, gc, :],
                             genes_b[:, gc, :], start=(gc == 0), stop=(gc == 31),
                             inc=(gc == 31))
            KV.ws([t_xmm, t_rden])
            t_exb = KV.o("tensor_scalar", ex_b[:], p_big[:, 0:GD], rden[:], None,
                         ALU.mult, inc=True)
            tok_etev = None
            for kk in range(2):
                KT.ws([t_exb, tok_etev])
                t_tr = KT.o("transpose", p_c[:], ex_b[:, kk * 128:(kk + 1) * 128],
                            id_bs[:], inc=True)
                KV.w(t_tr)
                tok_etev = KV.o("tensor_copy", eT[:, kk, :], p_c[:], inc=True)
            t_dt = None
            t_ut = None
            for dm in range(16):
                KT.ws([tok_etev, t_dt])
                KT.o("matmul", p_a[:], WdT_b[:, 0, dm, :], eT[:, 0, :],
                     start=True, stop=False)
                t_dmm = KT.o("matmul", p_a[:], WdT_b[:, 1, dm, :], eT[:, 1, :],
                             start=False, stop=True, inc=True)
                KT.w(t_ut)
                KT.o("matmul", p_b[:], WuT_b[:, 0, dm, :], eT[:, 0, :],
                     start=True, stop=False)
                t_umm = KT.o("matmul", p_b[:], WuT_b[:, 1, dm, :], eT[:, 1, :],
                             start=False, stop=True, inc=True)
                KS.w(t_dmm)
                t_dt = KS.o("activation", dtile[:], p_a[:], AF.Tanh,
                            bias=zcol128[:], scale=1.0, inc=True)
                KV.w(t_umm)
                t_ut = KV.o("tensor_copy", utile[:], p_b[:], inc=True)
                KY.ws([tok_r1war, t_x4])
                t_r1sl = KY.dma(r1s[:, 0:128],
                                r1_hbm[dm * 128:(dm + 1) * 128,
                                       g0 + tc * 128:g0 + (tc + 1) * 128])
                KV.ws([t_dt, t_r1sl])
                KV.o("scalar_tensor_tensor", m1t[:], dtile[:], nwsc[:, dm:dm + 1],
                     utile[:], ALU.mult, ALU.mult)
                KV.o("tensor_mul", xtmp[:], r1s[:, 0:128],
                     a1_rep[:, g0 + tc * 128:g0 + (tc + 1) * 128])
                KV.d()
                KV.o("tensor_add", xtmp[:], xtmp[:],
                     b1_rep[:, g0 + tc * 128:g0 + (tc + 1) * 128])
                KV.o("tensor_mul", g2t[:], m1t[:],
                     c_rep[:, g0 + tc * 128:g0 + (tc + 1) * 128])
                KV.d()
                KV.o("tensor_scalar", g2t[:], g2t[:], 1.0, None, ALU.add)
                KV.d()
                t_r2 = KV.o("tensor_mul", r2b[:, dm, tc * 128:(tc + 1) * 128],
                            xtmp[:], g2t[:], inc=True)
                tok_r1war = t_r2
                tok_r2 = t_r2
                KV.d()
                t_bf = KV.o("tensor_copy", bfst[:, 0:128],
                            r2b[:, dm, tc * 128:(tc + 1) * 128], inc=True)
                KS.w(t_r2)
                t_sq = KS.o("square", sqst[:, 0:128],
                            r2b[:, dm, tc * 128:(tc + 1) * 128], inc=True)
                KT.ws([t_bf, t_sq, tok_l2])
                t_sa = KT.o("matmul", p_ra[0:1, 0:128], ones_cb[:], bfst[:, 0:128],
                            start=True, stop=True)
                t_sb = KT.o("matmul", p_ra[0:1, 128:256], ones_cb[:], sqst[:, 0:128],
                            start=True, stop=True, inc=True)
                KV.w(t_sb)
                KV.o("tensor_add", t2a_r[0:1, tc * 128:(tc + 1) * 128],
                     t2a_r[0:1, tc * 128:(tc + 1) * 128], p_ra[0:1, 0:128])
                KV.d()
                KV.o("tensor_add", t2b_r[0:1, tc * 128:(tc + 1) * 128],
                     t2b_r[0:1, tc * 128:(tc + 1) * 128], p_ra[0:1, 128:256])
                KV.d()
                tok_l2 = KV.o("tensor_copy", tmp_r[0:1, 0:1], tmp_r[0:1, 0:1],
                              inc=True)
        KV.o("tensor_scalar", mu2_r[:], t2a_r[:], 1.0 / D, None, ALU.mult)
        KV.o("tensor_scalar", tmp_r[0:1, 0:GRP], t2b_r[:], 1.0 / D, None, ALU.mult)
        KV.d()
        KV.o("tensor_mul", tmp2_r[0:1, 0:GRP], mu2_r[:], mu2_r[:])
        KV.d()
        t_v2 = KV.o("tensor_sub", tmp_r[0:1, 0:GRP], tmp_r[0:1, 0:GRP],
                    tmp2_r[0:1, 0:GRP], inc=True)
        KS.w(t_v2)
        t_sd2 = KS.o("activation", tmp2_r[0:1, 0:GRP], tmp_r[0:1, 0:GRP], AF.Sqrt,
                     bias=el1[:], scale=1.0, inc=True)
        KV.w(t_sd2)
        t_rs2 = KV.o("reciprocal", rs2_r[:], tmp2_r[0:1, 0:GRP], inc=True)
        KT.w(t_rs2)
        t_mm = KT.o("matmul", p_big[:, 0:GRP], ones_rf[:], mu2_r[:], start=True,
                    stop=True, inc=True)
        KV.w(t_mm)
        t_m2 = KV.o("tensor_copy", m2_rep[:], p_big[:, 0:GRP], inc=True)
        KT.w(t_m2)
        t_mm = KT.o("matmul", p_big[:, 0:GRP], ones_rf[:], rs2_r[:], start=True,
                    stop=True, inc=True)
        KV.w(t_mm)
        t_s2 = KV.o("tensor_copy", s2_rep[:], p_big[:, 0:GRP], inc=True)
        KT.w(t_s2)
        tok_xnwar = tok_r1war
        for dm in range(16):
            KV.ws([t_s2, tok_outdma])
            KV.o("tensor_sub", gout[:], r2b[:, dm, :], m2_rep[:])
            KV.d()
            t_o = KV.o("tensor_mul", gout[:], gout[:], s2_rep[:], inc=True)
            KY.w(t_o)
            tok_outdma = KY.dma(out_d[dm * 128:(dm + 1) * 128, g0:g0 + GRP], gout[:])
    KY.w(("sq", kb.cnt["sq"]))
    KG.w(("gq", kb.cnt["gq"]))
    if kb.cnt["cq"]:
        KS.w(("cq", kb.cnt["cq"]))

    with nc.Block() as block:
        @block.tensor
        def _(t):
            for fn in kb.q["t"]:
                fn(t)

        @block.vector
        def _(v):
            for fn in kb.q["v"]:
                fn(v)

        @block.scalar
        def _(s):
            for fn in kb.q["s"]:
                fn(s)

        @block.gpsimd
        def _(g):
            for fn in kb.q["g"]:
                fn(g)

        @block.sync
        def _(y):
            for fn in kb.q["y"]:
                fn(y)

    es.close()
    return nc


def _prep_inputs(inputs):
    u = np.asarray(inputs["L0_attn_u"])
    ipw = np.asarray(inputs["L0_attn_in_proj_w"])
    ipb = np.asarray(inputs["L0_attn_in_proj_b"])
    cvw = np.asarray(inputs["L0_attn_conv_w"])
    cvb = np.asarray(inputs["L0_attn_conv_b"])
    dtb = np.asarray(inputs["L0_attn_dt_bias"])
    alog = np.asarray(inputs["L0_attn_A_log"])
    lnw = np.asarray(inputs["L0_attn_ln_w"])
    lnb = np.asarray(inputs["L0_attn_ln_b"])
    opw = np.asarray(inputs["L0_attn_out_proj_w"])
    opb = np.asarray(inputs["L0_attn_out_proj_b"])
    Wc = np.asarray(inputs["L0_ffn_Wc"])
    temp = np.asarray(inputs["L0_ffn_temperature"])
    gen = np.asarray(inputs["L0_ffn_genes"])
    Wd = np.asarray(inputs["L0_ffn_Wd"])
    Wu = np.asarray(inputs["L0_ffn_Wu"])
    nw = np.asarray(inputs["L0_ffn_norm_w"])
    scale = np.asarray(inputs["L0_ffn_scale"])

    f = np.float32
    WinT = np.zeros((D, DIP), f)
    WinT[:, :8352] = ipw.T
    WinT = np.ascontiguousarray(WinT)
    inb_a = np.zeros((DIP,), f)
    inb_a[:8352] = ipb
    inb_a = np.ascontiguousarray(inb_a.reshape(66, 128).T)
    cw_a = np.ascontiguousarray(
        cvw.reshape(33, 128, 4).transpose(1, 0, 2).reshape(128, 132)).astype(f)
    cb_a = np.ascontiguousarray(cvb.reshape(33, 128).T).astype(f)
    lnw_a = np.ascontiguousarray(lnw.reshape(32, 128).T).astype(f)
    lnb_a = np.ascontiguousarray(lnb.reshape(32, 128).T).astype(f)
    WoutT = np.ascontiguousarray(opw.T).astype(f)
    outb_a = np.ascontiguousarray(opb.reshape(16, 128).T).astype(f)
    WcT = np.ascontiguousarray(Wc.T).astype(f)
    nw_a = np.ascontiguousarray(nw.reshape(16, 128).T).astype(f)
    WdT = np.ascontiguousarray(Wd.T).astype(f)
    WuT = np.ascontiguousarray(Wu.T).astype(f)
    temp_a = np.broadcast_to(np.asarray(temp, f).reshape(1, 1), (1, 1)).copy()
    scl_a = np.broadcast_to(np.asarray(scale, f).reshape(1, 1), (128, 1)).copy()

    self32 = np.zeros((32, 32 * 128), f)
    for h in range(32):
        self32[h, h * 128:(h + 1) * 128] = 1.0
    sel64 = np.zeros((32, 32 * 64), np.float32)
    for h in range(32):
        sel64[h, h * 64:(h + 1) * 64] = 1.0
    sel64 = sel64.astype(ml_dtypes.bfloat16)
    selP = np.zeros((32, 16 * 128), f)
    for j in range(16):
        for p in range(128):
            selP[2 * j + (1 if p >= 64 else 0), j * 128 + p] = 1.0
    ones_rf_a = np.ones((1, 128), f)
    ones_cb_a = np.ones((128, 1), np.float32).astype(ml_dtypes.bfloat16)
    id_b_a = np.eye(128, dtype=np.float32).astype(ml_dtypes.bfloat16)
    id_f_a = np.eye(128, dtype=f)

    shared = dict(
        WinT=WinT, inb=inb_a, cwd=cw_a, cbd=cb_a,
        dtbd=np.ascontiguousarray(dtb.reshape(32, 1)).astype(f),
        alogd=np.ascontiguousarray(alog.reshape(32, 1)).astype(f),
        lnwd=lnw_a, lnbd=lnb_a, WoutT=WoutT, outbd=outb_a, WcT=WcT, nwd=nw_a,
        genesd=gen.astype(f), WdTd=WdT, WuTd=WuT, tempd=temp_a, scld=scl_a,
        self32d=self32, sel64d=sel64, selPd=selP, onesrfd=ones_rf_a,
        onescbd=ones_cb_a, idbd=id_b_a, idfd=id_f_a,
    )

    in_maps = []
    for core in range(8):
        b, r = core // 4, core % 4
        s0 = r * T
        useq = u[b]
        pad = np.zeros((4, D), f)
        if s0 >= 4:
            blk = useq[s0 - 4:s0 + T]
        else:
            blk = np.concatenate([pad[:4 - 0], useq[0:s0 + T]], 0) if s0 == 0 else None
        if s0 == 0:
            blk = np.concatenate([pad, useq[0:T]], 0)
        u_c = np.ascontiguousarray(blk.T)
        pm_a = np.zeros((128, 4), f)
        pm_a[:, :r] = 1.0
        pm1_a = 1.0 - pm_a
        m = dict(shared)
        m["u_fm"] = u_c
        m["pmd"] = pm_a
        m["pm1d"] = pm1_a
        in_maps.append(m)
    return in_maps


def kernel(**inputs):
    if "nc" not in _CACHE:
        _CACHE["nc"] = build_core()
    nc = _CACHE["nc"]
    in_maps = _prep_inputs(inputs)
    res = run_bass_kernel_spmd(nc, in_maps, core_ids=list(range(8)))
    out = np.empty((2, 4096, D), np.float32)
    for core in range(8):
        b, r = core // 4, core % 4
        out[b, r * T:(r + 1) * T, :] = res.results[core]["out"].T
    return out


# version 02
# speedup vs baseline: 1.0567x; 1.0567x over previous
import numpy as np
import ml_dtypes
import concourse.bass as bass
import concourse.mybir as mybir
from concourse.bass_utils import run_bass_kernel_spmd

F32 = mybir.dt.float32
BF16 = mybir.dt.bfloat16
AF = mybir.ActivationFunctionType
ALU = mybir.AluOpType

D = 2048
DI = 4096
NH = 32
NS = 64
DIP = 8448
T = 1024
TH = 1028
NCH = 8
G = 4096
GD = 256
GRP = 256
NGRP = T // GRP
BIG = 1.0e30
DBG = False
DBGCC = False
DBG3 = False
LNEPS = 1e-5
F32EPS = float(np.finfo(np.float32).eps)

_CACHE = {}


class Eng:
    def __init__(self, kb, key, semkey, dmakey):
        self.kb, self.key, self.semkey, self.dmakey = kb, key, semkey, dmakey

    def w(self, tok):
        if tok is None:
            return
        k, v = tok
        self.kb.q[self.key].append(lambda e, k=k, v=v: e.wait_ge(self.kb.sem[k], v))

    def ws(self, toks):
        best = {}
        for t in toks:
            if t is None:
                continue
            k, v = t
            best[k] = max(best.get(k, 0), v)
        for k, v in best.items():
            self.w((k, v))

    def d(self):
        self.kb.q[self.key].append(lambda e: e.drain())

    def o(self, fname, *a, inc=False, **kw):
        tok = None
        if inc:
            v = self.kb.bump(self.semkey)
            tok = (self.semkey, v)
            self.kb.q[self.key].append(
                lambda e, a=a, kw=kw: getattr(e, fname)(*a, **kw).then_inc(
                    self.kb.sem[self.semkey], 1))
        else:
            self.kb.q[self.key].append(
                lambda e, a=a, kw=kw: getattr(e, fname)(*a, **kw))
        return tok

    def dma(self, dst, src, **kw):
        v = self.kb.bump(self.dmakey, 16)
        tok = (self.dmakey, v)
        self.kb.q[self.key].append(
            lambda e, kw=kw: e.dma_start(out=dst, in_=src, **kw).then_inc(
                self.kb.sem[self.dmakey], 16))
        return tok

    def cc(self, kind, op, groups, ins, outs):
        v = self.kb.bump("cc", 1)
        self.kb.q[self.key].append(
            lambda e: e.collective_compute(kind, op, replica_groups=groups,
                                           ins=ins, outs=outs).then_inc(
                self.kb.sem["cc"]))
        return ("cc", v)


class Kb:
    def __init__(self, nc):
        self.nc = nc
        self.q = {k: [] for k in "tvsgy"}
        self.cnt = {k: 0 for k in ["et", "ev", "es", "eg", "sq", "cq", "gq",
                                   "cc", "bgc", "bgq"]}
        self.sem = {}
        self.t = Eng(self, "t", "et", None)
        self.v = Eng(self, "v", "ev", None)
        self.s = Eng(self, "s", "es", "cq")
        self.g = Eng(self, "g", "eg", "gq")
        self.y = Eng(self, "y", None, "sq")
        self.all = [self.t, self.v, self.s, self.g, self.y]

    def bump(self, s, n=1):
        self.cnt[s] += n
        return self.cnt[s]

    def barrier_all(self, exclude=()):
        toks = [(k, self.cnt[k]) for k in self.cnt
                if self.cnt[k] > 0 and k not in exclude]
        for e in self.all:
            e.ws(toks)


def build_core():
    nc = bass.Bass("TRN2", target_bir_lowering=False)
    kb = Kb(nc)
    KT, KV, KS, KG, KY = kb.t, kb.v, kb.s, kb.g, kb.y

    def din(name, shape, dt=F32):
        return nc.dram_tensor(name, shape, dt, kind="ExternalInput").ap()

    u_fm = din("u_fm", [D, TH])
    WinT = din("WinT", [D, DIP])
    inb = din("inb", [128, 66])
    cwd = din("cwd", [128, 33 * 4])
    cbd = din("cbd", [128, 33])
    dtbd = din("dtbd", [32, 1])
    alogd = din("alogd", [32, 1])
    lnwd = din("lnwd", [128, 32])
    lnbd = din("lnbd", [128, 32])
    WoutT = din("WoutT", [DI, D])
    outbd = din("outbd", [128, 16])
    WcT = din("WcT", [D, G])
    nwd = din("nwd", [128, 16])
    genesd = din("genesd", [G, GD])
    WdTd = din("WdTd", [GD, D])
    WuTd = din("WuTd", [GD, D])
    tempd = din("tempd", [1, 1])
    scld = din("scld", [128, 1])
    pmd = din("pmd", [128, 4])
    pm1d = din("pm1d", [128, 4])
    self32d = din("self32d", [32, 32 * 128])
    sel64d = din("sel64d", [32, 32 * 64], BF16)
    selPd = din("selPd", [32, 16 * 128])
    onesrfd = din("onesrfd", [1, 128])
    onescbd = din("onescbd", [128, 1], BF16)
    idbd = din("idbd", [128, 128], BF16)
    idfd = din("idfd", [128, 128])

    out_d = nc.dram_tensor("out", [D, T], F32, kind="ExternalOutput").ap()

    KI = "ExternalOutput" if DBG else "Internal"
    z_hbm = nc.dram_tensor("z_hbm", [DI, T], F32, kind=KI).ap()
    x_hbm = nc.dram_tensor("x_hbm", [T, DI], BF16, kind=KI).ap()
    r1_hbm = nc.dram_tensor("r1_hbm", [D, T], F32, kind=KI).ap()
    s_hbm = nc.dram_tensor("s_hbm", [128, NCH * 16 * 128], BF16).ap()
    wout_bf = nc.dram_tensor("wout_bf", [DI, D], BF16).ap()
    wc_bf = nc.dram_tensor("wc_bf", [D, G], BF16).ap()
    dbgA = dbgB = dbgC = dbgD = dbgE = dbgF = dbgG = dbgH = None
    if DBG or DBG3:
        dbgA = nc.dram_tensor("dbgA", [32, T], F32, kind="ExternalOutput").ap()
        dbgB = nc.dram_tensor("dbgB", [128, 4096], F32, kind="ExternalOutput").ap()
        dbgC = nc.dram_tensor("dbgC", [128, 64], BF16, kind="ExternalOutput").ap()
        dbgD = nc.dram_tensor("dbgD", [128, TH], BF16, kind="ExternalOutput").ap()
        dbgE = nc.dram_tensor("dbgE", [128, T], BF16, kind="ExternalOutput").ap()
        dbgF = nc.dram_tensor("dbgF", [8, T], F32, kind="ExternalOutput").ap()
        dbgG = nc.dram_tensor("dbgG", [128, T], BF16, kind="ExternalOutput").ap()
        dbgH = nc.dram_tensor("dbgH", [128, 2064], F32, kind="ExternalOutput").ap()
    cc_in = nc.dram_tensor("cc_in", [128, 16 * 128 + 16], F32).ap()
    cc_out = nc.dram_tensor("cc_out", [4 * 128, 16 * 128 + 16], F32).ap()
    dbgI = (nc.dram_tensor("dbgI", [4 * 128, 16 * 128 + 16], F32,
                           kind="ExternalOutput").ap() if (DBG or DBGCC) else None)

    import math as _math
    _lo = nc.s._sbuf_addr_for_side("left") if hasattr(nc, "s") else None
    try:
        _lo = nc.m.bass._sbuf_addr_for_side("left")
    except Exception:
        _lo = None
    _ALO = [None, None]

    def _init_region():
        lo = nc._sbuf_addr_for_side("left")
        hi = nc._sbuf_addr_for_side("right")
        _ALO[0] = [(lo, hi)]
        _ALO[1] = {}

    def sbA(name, shape, dt=F32, hi=False):
        if _ALO[0] is None:
            _init_region()
        nb = int(np.prod(shape[1:])) * mybir.dt.size(dt)
        nb = (nb + 31) // 32 * 32
        free = _ALO[0]
        order = reversed(list(enumerate(free))) if hi else list(enumerate(free))
        for i, (a, b) in order:
            if hi:
                addr = (b - nb) // 32 * 32
                if addr < a:
                    continue
                if addr + nb == b:
                    if addr == a:
                        free.pop(i)
                    else:
                        free[i] = (a, addr)
                else:
                    free[i] = (a, addr)
                    if addr + nb < b:
                        free.insert(i + 1, (addr + nb, b))
                _ALO[1][name] = (addr, nb)
                return nc.alloc_sbuf_tensor_at(name, list(shape), dt, offset=addr)
            a0 = (a + 31) // 32 * 32
            if b - a0 >= nb:
                addr = a0
                if addr + nb == b and a0 == a:
                    free.pop(i)
                else:
                    free[i] = (addr + nb, b)
                _ALO[1][name] = (addr, nb)
                return nc.alloc_sbuf_tensor_at(name, list(shape), dt, offset=addr)
        livel = sorted(((v[0], v[1], k) for k, v in _ALO[1].items()))
        raise AssertionError(
            f"manual sbuf OOM for {name} {shape} need {nb} free={free}\n" +
            "\n".join(f"  {a:7d} +{n:6d} {k}" for a, n, k in livel))

    def sbF(*names):
        free = _ALO[0]
        for n in names:
            a, nb = _ALO[1].pop(n)
            free.append((a, a + nb))
        free.sort()
        out = []
        for a, b in free:
            if out and out[-1][1] == a:
                out[-1] = (out[-1][0], b)
            else:
                out.append((a, b))
        _ALO[0] = [tuple(x) for x in out]
        _ALO[0] = out

    from contextlib import ExitStack
    es = ExitStack()

    def pst(name, shape, dt=F32):
        return es.enter_context(nc.psum_tensor(name, shape, dt))

    for s in ["et", "ev", "es", "eg", "sq", "cq", "gq", "cc", "bgc", "bgq"]:
        kb.sem[s] = es.enter_context(nc.semaphore(s))

    p_big = pst("p_big", [128, 1024])
    p_big2 = pst("p_big2", [128, 1024])
    p_a = pst("p_a", [128, 128])
    p_b = pst("p_b", [128, 128])
    p_c = pst("p_c", [128, 128], BF16)
    p_ra = pst("p_ra", [1, 512])

    cw_s = sbA("cw_s", [128, 33 * 4])
    cb_s = sbA("cb_s", [128, 33])
    inb_s = sbA("inb_s", [128, 66])
    dtb_s = sbA("dtb_s", [32, 1])
    alog_s = sbA("alog_s", [32, 1])
    nAcol = sbA("nAcol", [32, 1])
    lnw_s = sbA("lnw_s", [128, 32])
    lnb_s = sbA("lnb_s", [128, 32])
    lnwb2 = sbA("lnwb2", [128, 32, 2], BF16)
    outb_s = sbA("outb_s", [128, 16])
    nw_s = sbA("nw_s", [128, 16])
    nw_bs = sbA("nw_bs", [128, 16], BF16)
    temp_s = sbA("temp_s", [1, 1])
    scl_s = sbA("scl_s", [128, 1])
    pm_s = sbA("pm_s", [128, 4])
    pm1_s = sbA("pm1_s", [128, 4])
    self32_s = sbA("self32_s", [32, 32 * 128])
    sel64_s = sbA("sel64_s", [32, 32 * 64], BF16)
    selP_s = sbA("selP_s", [32, 16 * 128])
    ones_rf = sbA("ones_rf", [1, 128])
    ones_cb = sbA("ones_cb", [128, 1], BF16)
    id_bs = sbA("id_bs", [128, 128], BF16)
    id_fs = sbA("id_fs", [128, 128])
    mtri = sbA("mtri", [128, 128])
    zc32 = sbA("zc32", [32, 1])
    zcol128 = sbA("zcol128", [128, 1])
    onec32 = sbA("onec32", [32, 1])
    el1 = sbA("el1", [1, 1])
    ef1 = sbA("ef1", [1, 1])
    nwsc = sbA("nwsc", [128, 16])
    tcl_r = sbA("tcl_r", [1, 1])

    def rowt(name, n=T):
        return sbA(name, [1, n])

    dtraw = sbA("dtraw", [32, TH])
    wst_f = sbA("wst_f", [128, 16, 128])
    wst_f2 = sbA("wst_f2", [128, 16, 128])
    wst_b = sbA("wst_b", [128, 16, 128], BF16)
    wst_b2 = sbA("wst_b2", [128, 16, 128], BF16)
    trst = sbA("trst", [128, 128], BF16)

    t_l = None
    for dst, src in [
        (cw_s[:], cwd), (cb_s[:], cbd), (inb_s[:], inb), (dtb_s[:], dtbd),
        (alog_s[:], alogd), (lnw_s[:], lnwd), (lnb_s[:], lnbd), (outb_s[:], outbd),
        (nw_s[:], nwd), (temp_s[:], tempd), (scl_s[:], scld), (pm_s[:], pmd),
        (pm1_s[:], pm1d), (self32_s[:], self32d), (sel64_s[:], sel64d),
        (selP_s[:], selPd), (ones_rf[:], onesrfd), (ones_cb[:], onescbd),
        (id_bs[:], idbd), (id_fs[:], idfd),
    ]:
        t_l = KY.dma(dst, src)
    tok_small = t_l

    KG.o("memset", mtri[:], 0.0)
    KG.d()
    KG.o("affine_select", mtri[:], mtri[:], pattern=[[1, 128]],
         compare_op=ALU.is_ge, fill=-BIG, base=0, channel_multiplier=-1, inc=True)

    KV.w(tok_small)
    KV.o("memset", zc32[:], 0.0)
    KV.o("memset", zcol128[:], 0.0)
    KV.o("memset", onec32[:], 1.0)
    KV.o("memset", el1[:], LNEPS)
    KV.o("memset", ef1[:], F32EPS)
    KV.o("tensor_scalar", nwsc[:], nw_s[:], scl_s[:, 0:1], None, ALU.mult)
    KV.o("tensor_copy", nw_bs[:], nw_s[:])
    KV.o("tensor_copy", lnwb2[:, :, 0], lnw_s[:])
    KV.o("tensor_copy", lnwb2[:, :, 1], lnb_s[:])
    KV.o("tensor_scalar", tcl_r[:], temp_s[:], 0.1, None, ALU.max, inc=True)

    KS.w(tok_small)
    KS.o("activation", nAcol[:], alog_s[:], AF.Exp, bias=zc32[:], scale=1.0)
    KS.d()
    KS.o("mul", nAcol[:], nAcol[:], -1.0, inc=True)
    kb.barrier_all()

    uf_st = sbA("uf_st", [128, 16, 257])
    u_b = sbA("u_b", [128, 16, TH], BF16)
    zst = sbA("zst", [128, 2, 1024])
    tok_uc = None
    for h in range(4):
        t_u = KY.dma(uf_st[:], u_fm.rearrange("(kc k) t -> k kc t", k=128)
                     [:, :, h * 257:(h + 1) * 257])
        KG.w(t_u)
        tok_uc = KG.o("tensor_copy", u_b[:, :, h * 257:(h + 1) * 257], uf_st[:],
                      inc=True)
        if h < 3:
            KY.w(tok_uc)
    sbF("uf_st")

    xBC = sbA("xBC", [128, 33, TH], BF16)
    sweeps = [(0, 512, p_big, 0), (512, 512, p_big, 512), (1024, 4, p_big2, 0)]
    tok_wbuf = [None, None]
    tok_wcast = [None, None]
    tok_zdma = [None, None]
    tok_evac_prev = None
    for oc in range(66):
        buf_f = wst_f if oc % 2 == 0 else wst_f2
        buf_b = wst_b if oc % 2 == 0 else wst_b2
        KY.ws([tok_wcast[oc % 2]])
        t_wl = KY.dma(buf_f[:], WinT[:, oc * 128:(oc + 1) * 128]
                      .rearrange("(kc k) m -> k kc m", k=128))
        KG.ws([t_wl, tok_wbuf[oc % 2], tok_uc])
        t_wc = KG.o("tensor_copy", buf_b[:], buf_f[:], inc=True)
        tok_wcast[oc % 2] = t_wc
        KT.w(t_wc)
        if tok_evac_prev is not None:
            KT.w(tok_evac_prev)
        last_mm = None
        for (t0, n, pp, po) in sweeps:
            for kc in range(16):
                tokm = KT.o("matmul", pp[:, po:po + n], buf_b[:, kc, :],
                            u_b[:, kc, t0:t0 + n], start=(kc == 0), stop=(kc == 15),
                            inc=(kc == 15))
                if kc == 15:
                    last_mm = tokm
        tok_wbuf[oc % 2] = last_mm
        KS.w(last_mm)
        if oc < 32:
            sl = oc % 2
            KS.w(tok_zdma[sl])
            KS.o("activation", zst[:, sl, 0:1020], p_big[:, 4:1024], AF.Silu,
                 bias=inb_s[:, oc:oc + 1], scale=1.0)
            KS.d()
            t_ev = KS.o("activation", zst[:, sl, 1020:1024], p_big2[:, 0:4], AF.Silu,
                        bias=inb_s[:, oc:oc + 1], scale=1.0, inc=True)
            KY.w(t_ev)
            tok_zdma[sl] = KY.dma(z_hbm[oc * 128:(oc + 1) * 128, :], zst[:, sl, :])
        elif oc < 65:
            c = oc - 32
            KV.w(last_mm)
            KV.o("tensor_scalar", xBC[:, c, 0:1024], p_big[:, 0:1024],
                 inb_s[:, oc:oc + 1], None, ALU.add)
            t_ev = KV.o("tensor_scalar", xBC[:, c, 1024:1028], p_big2[:, 0:4],
                        inb_s[:, oc:oc + 1], None, ALU.add, inc=True)
        else:
            KV.w(last_mm)
            KV.o("tensor_scalar", dtraw[:, 0:1024], p_big[0:32, 0:1024],
                 inb_s[0:32, oc:oc + 1], None, ALU.add)
            t_ev = KV.o("tensor_scalar", dtraw[:, 1024:1028], p_big2[0:32, 0:4],
                        inb_s[0:32, oc:oc + 1], None, ALU.add, inc=True)
        tok_evac_prev = t_ev
    kb.barrier_all()
    sbF("u_b", "zst", "wst_f", "wst_f2", "wst_b", "wst_b2")

    bgf = [sbA("bgf0", [128, 2048]), sbA("bgf1", [128, 2048])]
    bgb = [sbA("bgb0", [128, 2048], BF16), sbA("bgb1", [128, 2048], BF16)]
    bg_jobs = []
    for r0 in range(0, DI, 128):
        bg_jobs.append((WoutT[r0:r0 + 128, :], wout_bf[r0:r0 + 128, :]))
    for r0 in range(0, D, 128):
        bg_jobs.append((WcT[r0:r0 + 128, 0:2048], wc_bf[r0:r0 + 128, 0:2048]))
        bg_jobs.append((WcT[r0:r0 + 128, 2048:4096], wc_bf[r0:r0 + 128, 2048:4096]))
    tok_bgo = [None, None]
    for i, (srcw, dstw) in enumerate(bg_jobs):
        p = i % 2
        KG.ws([tok_bgo[p]])
        v = kb.bump("bgq", 16)
        kb.q["g"].append(lambda e, s=srcw, p=p: e.dma_start(
            out=bgf[p][:], in_=s).then_inc(kb.sem["bgq"], 16))
        KG.w(("bgq", v))
        vc = kb.bump("bgc")
        kb.q["g"].append(lambda e, p=p: e.tensor_copy(
            bgb[p][:], bgf[p][:]).then_inc(kb.sem["bgc"], 1))
        KG.w(("bgc", vc))
        v2 = kb.bump("bgq", 16)
        kb.q["g"].append(lambda e, d=dstw, p=p: e.dma_start(
            out=d, in_=bgb[p][:]).then_inc(kb.sem["bgq"], 16))
        tok_bgo[p] = ("bgq", v2)
    tok_bg_done = ("bgq", kb.cnt["bgq"])

    dt_sp = sbA("dt_sp", [32, T])
    Acum = sbA("Acum", [32, T])
    nAc = sbA("nAc", [32, T], hi=True)
    AcumT = sbA("AcumT", [128, NCH, 32], hi=True)
    dtT = sbA("dtT", [128, NCH, 32], hi=True)
    dtdsT = sbA("dtdsT", [128, NCH, 32], hi=True)
    sdo_b = sbA("sdo_b", [32, T], BF16, hi=True)
    Dmat = sbA("Dmat", [32, NCH])
    eLast = sbA("eLast", [32, NCH])
    eStart = sbA("eStart", [32, NCH])
    nStart = sbA("nStart", [32, NCH])
    Dagg = sbA("Dagg", [32, 1])
    Dp = sbA("Dp", [128, NCH, 16])
    zero32 = sbA("zero32", [32, T])
    sp_a = sbA("sp_a", [32, T])
    sp_b = sbA("sp_b", [32, T])
    Bn = sbA("Bn", [64, T], BF16, hi=True)
    Ctn = sbA("Ctn", [64, T], BF16, hi=True)
    Btok = sbA("Btok", [128, NCH, 64], BF16, hi=True)
    KV.o("memset", zero32[:], 0.0)
    KV.d()
    KS.o("activation", sp_a[:], dtraw[0:32, 4:1028], AF.Relu, bias=dtb_s[:], scale=1.0)
    KS.o("activation", sp_b[:], dtraw[0:32, 4:1028], AF.Abs, bias=dtb_s[:], scale=1.0)
    KS.d()
    KS.o("activation", sp_b[:], sp_b[:], AF.Exp, bias=zc32[:], scale=-1.0)
    KS.d()
    t_sp = KS.o("activation", sp_b[:], sp_b[:], AF.Ln, bias=onec32[:], scale=1.0,
                inc=True)
    KV.w(t_sp)
    KV.o("tensor_add", dt_sp[:], sp_a[:], sp_b[:])
    KV.d()
    KV.o("tensor_scalar", sp_a[:], dt_sp[:], nAcol[:], None, ALU.mult)
    KV.d()
    KV.o("tensor_tensor_scan", Acum[:], sp_a[:], zero32[:], 0.0, ALU.add, ALU.add)
    KV.d()
    KV.o("tensor_scalar", nAc[:], Acum[:], -1.0, None, ALU.mult)
    KV.o("tensor_copy", eLast[:], Acum.ap().rearrange("p (c l) -> p c l", l=128)[:, :, 127])
    KV.d()
    KV.o("memset", eStart[:, 0:1], 0.0)
    KV.o("tensor_copy", eStart[:, 1:8], eLast[:, 0:7])
    KV.d()
    KV.o("tensor_sub", eLast[:], eLast[:], eStart[:])
    KV.d()
    t_nst = KV.o("tensor_scalar", nStart[:], eStart[:], -1.0, None, ALU.mult, inc=True)
    KS.w(t_nst)
    KS.o("activation", Dmat[:], eLast[:], AF.Exp, bias=zc32[:], scale=1.0)
    KS.o("activation", Dagg[:], Acum[:, 1023:1024], AF.Exp, bias=zc32[:], scale=1.0)
    for ct in range(NCH):
        KS.o("activation", sdo_b[:, ct * 128:(ct + 1) * 128],
             Acum[:, ct * 128:(ct + 1) * 128], AF.Exp,
             bias=nStart[:, ct:ct + 1], scale=1.0)
    KS.d()
    t_ds = None
    for ct in range(NCH):
        t_ds = KS.o("activation", sp_b[:, ct * 128:(ct + 1) * 128],
                    nAc[:, ct * 128:(ct + 1) * 128], AF.Exp,
                    bias=Acum[:, ct * 128 + 127:ct * 128 + 128], scale=1.0,
                    inc=(ct == NCH - 1))
    KV.w(t_ds)
    t_dtds = KV.o("tensor_mul", sp_b[:], sp_b[:], dt_sp[:], inc=True)

    KT.w(t_dtds)
    for ct in range(NCH):
        for srcr, dstt in [(nAc, AcumT), (dt_sp, dtT), (sp_b, dtdsT)]:
            t_tr = KT.o("transpose", p_a[:, 0:32], srcr[:, ct * 128:(ct + 1) * 128],
                        id_fs[0:32, 0:32], inc=True)
            KV.w(t_tr)
            t_ev = KV.o("tensor_copy", dstt[:, ct, :], p_a[:, 0:32], inc=True)
            KT.w(t_ev)

    convq = sbA("convq", [128, T])
    xc_b = sbA("xc_b", [128, T], BF16)
    xtr0 = sbA("xtr0", [128, 8, 128], BF16)
    xtr1 = sbA("xtr1", [128, 8, 128], BF16)
    tok_xstore = None
    tok_trev = None
    tok_xdma = [None, None]
    tok_bt = None
    for c in range(33):
        KV.o("tensor_scalar", convq[:], xBC[:, c, 1:1025],
             cw_s[:, c * 4:c * 4 + 1], None, ALU.mult)
        for k in range(1, 3):
            KV.d()
            KV.o("scalar_tensor_tensor", convq[:], xBC[:, c, 1 + k:1025 + k],
                 cw_s[:, c * 4 + k:c * 4 + k + 1], convq[:], ALU.mult, ALU.add)
        KV.d()
        t_cq = KV.o("scalar_tensor_tensor", convq[:], xBC[:, c, 4:1028],
                    cw_s[:, c * 4 + 3:c * 4 + 4], convq[:], ALU.mult, ALU.add,
                    inc=True)
        KS.ws([t_cq, tok_trev, tok_bt])
        t_sil = KS.o("activation", xc_b[:], convq[:], AF.Silu,
                     bias=cb_s[:, c:c + 1], scale=1.0, inc=True)
        if DBG and c == 0:
            KY.w(t_sil)
            t_d = KY.dma(dbgE, xc_b[:])
            KY.w(t_d)
            t_d2 = KY.dma(dbgD, xBC[:, 0, :])
            KY.w(t_d2)
        KV.w(t_sil)
        if c < 32:
            pp = c % 2
            xb = xtr0 if pp == 0 else xtr1
            for ct in range(NCH):
                KT.ws([t_sil, tok_trev])
                t_tr = KT.o("transpose", p_c[:], xc_b[:, ct * 128:(ct + 1) * 128],
                            id_bs[:], inc=True)
                KV.ws([t_tr, tok_xdma[pp]])
                tok_trev = KV.o("tensor_copy", xb[:, ct, :], p_c[:], inc=True)
            KY.w(tok_trev)
            tok_xdma[pp] = KY.dma(
                x_hbm[:, c * 128:(c + 1) * 128]
                .rearrange("(ct q) m -> q ct m", q=128), xb[:])
            tok_xstore = tok_xdma[pp]
        else:
            KV.w(t_sil)
            KV.o("tensor_copy", Bn[:], xc_b[0:64, :])
            KV.d()
            t_bn = KV.o("tensor_copy", Ctn[:], xc_b[64:128, :], inc=True)
            KT.w(t_bn)
            for ct in range(NCH):
                t_tr = KT.o("transpose", p_c[:, 0:64],
                            Bn[:, ct * 128:(ct + 1) * 128], id_bs[0:64, 0:64],
                            inc=True)
                KV.w(t_tr)
                tok_bt = KV.o("tensor_copy", Btok[:, ct, :], p_c[:, 0:64], inc=True)
                KT.w(tok_bt)
    if DBG or DBG3:
        t_d = KY.dma(dbgA, Acum[:])
        KY.w(t_d)
    kb.barrier_all(exclude=("bgq", "bgc"))
    sbF("xBC", "convq", "xc_b", "xtr0", "xtr1", "dt_sp", "zero32", "sp_a",
        "sp_b", "dtraw")

    StIn = sbA("StIn", [128, NCH, 16, 128], BF16, hi=True)
    tmat = sbA("tmat", [128, 32, T], BF16, hi=True)
    agg = sbA("agg", [128, 16, 128])
    xchk = sbA("xchk", [128, 32, 128], BF16)
    xdt_ds = sbA("xdt_ds", [128, 128], BF16)
    sstg = sbA("sstg", [128, 128], BF16)
    sslice = sbA("sslice", [128, 16, 128], BF16)
    Dap = sbA("Dap", [128, 16])

    for j in range(16):
        t_mm = KT.o("matmul", p_a[:, 0:8], selP_s[:, j * 128:(j + 1) * 128],
                    Dmat[:], start=True, stop=True, inc=True)
        KV.w(t_mm)
        t_dp = KV.o("tensor_copy", Dp[:, :, j], p_a[:, 0:8], inc=True)
        KT.w(t_dp)
        t_mm = KT.o("matmul", p_a[:, 0:1], selP_s[:, j * 128:(j + 1) * 128],
                    Dagg[:], start=True, stop=True, inc=True)
        KV.w(t_mm)
        t_dap = KV.o("tensor_copy", Dap[:, j:j + 1], p_a[:, 0:1], inc=True)
        KT.w(t_dap)

    tok_schunk = None
    tok_sdma = None
    tok_xwar = None
    for ct in range(NCH):
        KY.w(tok_xwar)
        t_x = KY.dma(xchk[:], x_hbm[ct * 128:(ct + 1) * 128, :]
                     .rearrange("p (hc q) -> p hc q", q=128))
        KV.w(t_x)
        for h in range(NH):
            t_xs = KV.o("tensor_scalar", xdt_ds[:], xchk[:, h, :],
                        dtdsT[:, ct, h:h + 1], None, ALU.mult, inc=True)
            KT.w(t_xs)
            half = 64 * (h % 2)
            t_mm = KT.o("matmul", p_b[0:64, :], Btok[:, ct, :], xdt_ds[:],
                        start=True, stop=True, inc=True)
            KV.ws([t_mm, tok_sdma])
            tok_schunk = KV.o("tensor_copy", sstg[half:half + 64, :],
                              p_b[0:64, :], inc=True)
            KT.w(tok_schunk)
            KV.d()
            KY.w(tok_schunk)
            tok_sdma = KY.dma(
                s_hbm[half:half + 64,
                      (ct * 16 + h // 2) * 128:(ct * 16 + h // 2 + 1) * 128],
                sstg[half:half + 64, :])
        tok_xwar = tok_schunk
    t_ss = None
    for ct in range(NCH):
        KY.ws([tok_sdma, t_ss])
        t_sl = KY.dma(sslice[:].rearrange("p a b -> p (a b)"),
                      s_hbm[:, ct * 2048:(ct + 1) * 2048])
        KV.w(t_sl)
        if ct == 0:
            KV.o("tensor_copy", agg[:].rearrange("p a b -> p (a b)"),
                 sslice[:].rearrange("p a b -> p (a b)"))
        else:
            for j in range(16):
                KV.o("scalar_tensor_tensor", agg[:, j, :], agg[:, j, :],
                     Dp[:, ct, j:j + 1], sslice[:, j, :], ALU.mult, ALU.add)
        KV.d()
        t_ss = KV.o("tensor_copy", Dap[0:1, 0:1], Dap[0:1, 0:1], inc=True)
    ccstg = sbA("ccstg", [128, 16 * 128 + 16])
    KV.o("tensor_copy", ccstg[:, 0:2048], agg[:].rearrange("p a b -> p (a b)"))
    KV.d()
    t_ccs = KV.o("tensor_copy", ccstg[:, 2048:2064], Dap[:], inc=True)
    KG.w(t_ccs)
    if DBG or DBG3:
        KY.w(t_ccs)
        t_d = KY.dma(dbgH, ccstg[:])
        KY.w(t_d)
    t_ci = KG.dma(cc_in, ccstg[:])
    KG.w(t_ci)
    t_cc = KG.cc("AllGather", ALU.bypass, [[0, 1, 2, 3], [4, 5, 6, 7]],
                 [cc_in.opt()], [cc_out.opt()])
    KG.w(t_cc)
    if DBG or DBGCC:
        t_d = KG.dma(dbgI, cc_out)
        KG.w(t_d)
    gsl = sbA("gsl", [128, 1024])
    Dg4 = sbA("Dg4", [128, 4, 16])
    Ecf = sbA("Ecf", [128, 4, 16])
    Wcf = sbA("Wcf", [128, 4, 16])
    t_go = None
    for r in range(4):
        t_go = KG.dma(Dg4[:, r, :], cc_out[r * 128:(r + 1) * 128, 2048:2064])
    KV.w(t_go)
    for r in range(4):
        KV.o("tensor_scalar", Ecf[:, r, :], Dg4[:, r, :],
             pm_s[:, r:r + 1], None, ALU.mult)
        KV.d()
        KV.o("tensor_scalar", Ecf[:, r, :], Ecf[:, r, :], pm1_s[:, r:r + 1],
             None, ALU.add)
        KV.d()
    KV.o("memset", Wcf[:, 3, :], 1.0)
    KV.d()
    KV.o("tensor_copy", Wcf[:, 2, :], Ecf[:, 3, :])
    KV.d()
    KV.o("tensor_mul", Wcf[:, 1, :], Wcf[:, 2, :], Ecf[:, 2, :])
    KV.d()
    KV.o("tensor_mul", Wcf[:, 0, :], Wcf[:, 1, :], Ecf[:, 1, :])
    KV.d()
    for r in range(4):
        KV.o("tensor_scalar", Wcf[:, r, :], Wcf[:, r, :], pm_s[:, r:r + 1],
             None, ALU.mult)
    KV.d()
    KV.o("memset", agg[:].rearrange("p a b -> p (a b)"), 0.0)
    t_gz = KV.o("tensor_copy", Dap[0:1, 0:1], Dap[0:1, 0:1], inc=True)
    for r in range(4):
        for hf in range(2):
            KG.ws([t_gz])
            t_gs = KG.dma(gsl[:], cc_out[r * 128:(r + 1) * 128,
                                         hf * 1024:(hf + 1) * 1024])
            KV.w(t_gs)
            for j in range(8):
                jj = hf * 8 + j
                KV.o("scalar_tensor_tensor", agg[:, jj, :],
                     gsl[:, j * 128:(j + 1) * 128], Wcf[:, r, jj:jj + 1],
                     agg[:, jj, :], ALU.mult, ALU.add)
            KV.d()
            t_gz = KV.o("tensor_copy", Dap[0:1, 0:1], Dap[0:1, 0:1], inc=True)
    for ct in range(NCH):
        for j in range(16):
            KV.o("tensor_copy", StIn[:, ct, j, :], agg[:, j, :])
        KV.d()
        if ct < NCH - 1:
            KY.w(t_ss)
            t_sl = KY.dma(sslice[:].rearrange("p a b -> p (a b)"),
                          s_hbm[:, ct * 2048:(ct + 1) * 2048])
            KV.w(t_sl)
            for j in range(16):
                KV.o("scalar_tensor_tensor", agg[:, j, :], agg[:, j, :],
                     Dp[:, ct, j:j + 1], sslice[:, j, :], ALU.mult, ALU.add)
            KV.d()
            t_ss = KV.o("tensor_copy", Dap[0:1, 0:1], Dap[0:1, 0:1], inc=True)
    t_dbgs = KV.o("tensor_copy", Dap[0:1, 0:1], Dap[0:1, 0:1], inc=True)
    if DBG or DBG3:
        KY.w(t_dbgs)
        t_d = KY.dma(dbgC, StIn[:, 0, 0, 0:64])
        KY.w(t_d)
    kb.barrier_all(exclude=("bgq", "bgc"))
    sbF("agg", "ccstg", "gsl", "Dg4", "Ecf", "Wcf", "Dap", "Btok", "selP_s",
        "Dmat", "eLast", "eStart", "nStart", "Dagg", "Dp", "sstg", "sslice",
        "bgf0", "bgf1", "bgb0", "bgb1")

    Pm = sbA("Pm", [128, 128], BF16)
    Eh = sbA("Eh", [128, 128], BF16)
    Th_ = sbA("Th_", [128, 128], BF16)
    ygate = sbA("ygate", [128, 128])
    ybf = sbA("ybf", [128, 128], BF16)
    ysq = sbA("ysq", [128, 128], BF16)
    xdt_p = sbA("xdt_p", [128, 128], BF16)
    CSs = sbA("CSs", [128, 128], BF16)
    zt0 = sbA("zt0", [128, 16, 128])
    zt1 = sbA("zt1", [128, 16, 128])
    sry_r = rowt("sry_r"); sry2_r = rowt("sry2_r")

    tok_gate = None
    tok_th = None
    tok_statev = None
    tok_xdtp = None
    t_smm = None
    t_ysq = None
    for ct in range(NCH):
        KY.ws([tok_xdtp])
        t_x = KY.dma(xchk[:], x_hbm[ct * 128:(ct + 1) * 128, :]
                     .rearrange("p (hc q) -> p hc q", q=128))
        KY.w(tok_gate)
        t_z = KY.dma(zt0[:], z_hbm[0:2048, ct * 128:(ct + 1) * 128]
                     .rearrange("(hc p) q -> p hc q", p=128))
        t_z2 = KY.dma(zt1[:], z_hbm[2048:4096, ct * 128:(ct + 1) * 128]
                      .rearrange("(hc p) q -> p hc q", p=128))
        KT.ws([tok_th])
        t_pmm = KT.o("matmul", p_b[:], Bn[:, ct * 128:(ct + 1) * 128],
                     Ctn[:, ct * 128:(ct + 1) * 128], start=True, stop=True, inc=True)
        KV.w(t_pmm)
        t_pev = KV.o("tensor_copy", Pm[:], p_b[:], inc=True)
        KT.w(t_pev)
        for h in range(NH):
            KV.ws([t_x, tok_statev])
            tok_xdtp = KV.o("tensor_scalar", xdt_p[:], xchk[:, h, :],
                            dtT[:, ct, h:h + 1], None, ALU.mult, inc=True)
            half = 64 * (h % 2)
            t_csm = KT.o("matmul", p_b[0:64, :128], sel64_s[:, h * 64:(h + 1) * 64],
                         sdo_b[:, ct * 128:(ct + 1) * 128], start=True, stop=True,
                         inc=True)
            KV.w(t_csm)
            t_css = KV.o("tensor_mul", CSs[half:half + 64, :],
                         Ctn[:, ct * 128:(ct + 1) * 128],
                         p_b[0:64, :128], inc=True)
            KT.w(t_css)
            KT.ws([tok_gate])
            KT.o("matmul", p_a[:], self32_s[:, h * 128:(h + 1) * 128],
                 Acum[:, ct * 128:(ct + 1) * 128], start=True, stop=False)
            t_emm = KT.o("matmul", p_a[:], id_fs[:], mtri[:], start=False, stop=True,
                         inc=True)
            KS.w(t_emm)
            t_eh = KS.o("activation", Eh[:], p_a[:], AF.Exp,
                        bias=AcumT[:, ct, h:h + 1], scale=1.0, inc=True)
            KV.ws([t_eh, t_pev])
            tok_th = KV.o("tensor_mul", Th_[:], Pm[:], Eh[:], inc=True)
            KT.ws([tok_th, tok_xdtp, t_eh])
            KT.o("matmul", p_a[:], xdt_p[:], Th_[:], start=True, stop=False)
            t_ymm = KT.o("matmul", p_a[:], StIn[half:half + 64, ct, h // 2, :],
                         CSs[half:half + 64, :], start=False, stop=True, inc=True)
            KV.ws([t_ymm, t_z, t_z2, t_smm, t_ysq])
            zsl = zt0[:, h, :] if h < 16 else zt1[:, h - 16, :]
            tok_gate = KV.o("tensor_mul", ygate[:], p_a[:], zsl, inc=True)
            KV.d()
            t_ybf = KV.o("tensor_copy", ybf[:], ygate[:], inc=True)
            KS.ws([tok_gate, t_smm])
            t_ysq = KS.o("square", ysq[:], ygate[:], inc=True)
            t_tm = KV.o("tensor_scalar", tmat[:, h, ct * 128:(ct + 1) * 128],
                        ygate[:], lnw_s[:, h:h + 1], lnb_s[:, h:h + 1],
                        ALU.mult, op1=ALU.add, inc=True)
            q0 = 256 * (ct % 2)
            KT.ws([t_ybf, t_ysq, tok_statev])
            KT.o("matmul", p_ra[0:1, q0:q0 + 128], ones_cb[:], ybf[:],
                 start=(h == 0), stop=(h == NH - 1))
            t_smm = KT.o("matmul", p_ra[0:1, q0 + 128:q0 + 256], ones_cb[:], ysq[:],
                         start=(h == 0), stop=(h == NH - 1), inc=True)
            if h == NH - 1:
                KV.w(t_smm)
                KV.o("tensor_copy", sry_r[0:1, ct * 128:(ct + 1) * 128],
                     p_ra[0:1, q0:q0 + 128])
                KV.d()
                tok_statev = KV.o("tensor_copy", sry2_r[0:1, ct * 128:(ct + 1) * 128],
                                  p_ra[0:1, q0 + 128:q0 + 256], inc=True)
    kb.barrier_all()
    sbF("CSs", "xchk", "xdt_ds", "zt0", "zt1", "Pm", "Eh", "Th_", "ygate", "ybf", "ysq",
        "xdt_p", "StIn", "self32_s", "sel64_s", "nAc", "AcumT", "dtT", "dtdsT",
        "sdo_b", "Bn", "Ctn", "Acum")

    muy_r = rowt("muy_r"); ay_r = rowt("ay_r")
    f1_r = rowt("f1_r"); f2_r = rowt("f2_r")
    f12 = sbA("f12", [2, T])
    tmp_r = rowt("tmp_r"); tmp2_r = rowt("tmp2_r"); tmp3_r = rowt("tmp3_r")
    KV.o("tensor_scalar", muy_r[:], sry_r[:], 1.0 / DI, None, ALU.mult)
    KV.o("tensor_scalar", tmp_r[:], sry2_r[:], 1.0 / DI, None, ALU.mult)
    KV.d()
    KV.o("tensor_mul", tmp2_r[:], muy_r[:], muy_r[:])
    KV.d()
    t_var = KV.o("tensor_sub", tmp_r[:], tmp_r[:], tmp2_r[:], inc=True)
    KS.w(t_var)
    t_sd = KS.o("activation", tmp2_r[:], tmp_r[:], AF.Sqrt, bias=el1[:], scale=1.0,
                inc=True)
    KV.w(t_sd)
    KV.o("reciprocal", ay_r[:], tmp2_r[:])
    KV.d()
    KV.o("tensor_mul", f1_r[:], muy_r[:], ay_r[:])
    KV.d()
    KV.o("tensor_scalar", f1_r[:], f1_r[:], -1.0, None, ALU.mult)
    KV.o("tensor_scalar", f2_r[:], ay_r[:], -1.0, None, ALU.mult)
    KV.d()
    KV.o("tensor_scalar", f2_r[:], f2_r[:], 1.0, None, ALU.add)
    t_rows = KV.o("tensor_copy", tmp3_r[0:1, 0:1], tmp3_r[0:1, 0:1], inc=True)
    KY.w(t_rows)
    t_f12a = KY.dma(f12[0:1, :], f1_r[:])
    t_f12b = KY.dma(f12[1:2, :], f2_r[:])
    if DBG:
        KY.ws([("ev", kb.cnt["ev"]), ("es", kb.cnt["es"])])
        for i, rr in enumerate([sry_r, sry2_r, muy_r, ay_r, f1_r, f2_r]):
            KY.dma(dbgF[i:i + 1, :], rr[:])
        t_d = KY.dma(dbgG, tmat[:, 0, :])
        KY.w(t_d)
    ay_rep = sbA("ay_rep", [128, T])
    KT.w(t_rows)
    for hh in range(2):
        t_mm = KT.o("matmul", p_big[:, hh * 512:hh * 512 + 512], ones_rf[:],
                    ay_r[0:1, hh * 512:(hh + 1) * 512], start=True, stop=True,
                    inc=True)
        KV.w(t_mm)
        t_ayr = KV.o("tensor_copy", ay_rep[:, hh * 512:(hh + 1) * 512],
                     p_big[:, hh * 512:hh * 512 + 512], inc=True)
        KT.w(t_ayr)
    kb.barrier_all()

    wst_b = sbA("wst5_b", [128, 32, 128], BF16)
    wst_b2 = sbA("wst5_b2", [128, 32, 128], BF16)
    stg1 = sbA("stg1", [128, T])
    stg2 = sbA("stg2", [128, T])
    ust = sbA("ust", [128, T])
    bfst = sbA("bfst", [128, T], BF16)
    sqst = sbA("sqst", [128, T], BF16)
    v12 = sbA("v12", [128, 16, 2])
    r3f = sbA("r3f", [2, 16, 128])
    sr_r = rowt("sr_r"); sr2_r = rowt("sr2_r")
    KV.o("memset", sr_r[:], 0.0)
    KV.o("memset", sr2_r[:], 0.0, inc=True)

    tok_wdone = [None, None]
    tok_r1dma = None
    tok_stadd = None
    tok_r1tok = None
    for dm in range(16):
        pq = dm % 2
        buf_b = wst_b if pq == 0 else wst_b2
        KY.ws([tok_wdone[pq]])
        t_wl = KY.dma(buf_b[:], wout_bf[:, dm * 128:(dm + 1) * 128]
                      .rearrange("(kc k) m -> k kc m", k=128))
        KT.w(t_wl)
        for kcg in range(32):
            t_vmm = KT.o("matmul", p_b[:, 0:2], buf_b[:, kcg, :],
                         lnwb2[:, kcg, :], start=(kcg == 0), stop=(kcg == 31),
                         inc=(kcg == 31))
        KV.w(t_vmm)
        t_v12 = KV.o("tensor_copy", v12[:, dm, :], p_b[:, 0:2], inc=True)
        KT.w(t_v12)
        t_tr = KT.o("transpose", p_a[0:2, 0:128], v12[:, dm, :], id_fs[:], inc=True)
        KV.w(t_tr)
        t_r3 = KV.o("tensor_copy", r3f[:, dm, :], p_a[0:2, 0:128], inc=True)
        KT.ws([t_r3, ("sq", kb.cnt["sq"])])
        for hh in range(2):
            t_r2m = KT.o("matmul", p_big2[:, hh * 512:hh * 512 + 512], r3f[:, dm, :],
                         f12[:, hh * 512:(hh + 1) * 512], start=True, stop=True,
                         inc=(hh == 1))
        for kcg in range(32):
            for hh in range(2):
                t_mmm = KT.o("matmul", p_big[:, hh * 512:(hh + 1) * 512],
                             buf_b[:, kcg, :],
                             tmat[:, kcg, hh * 512:(hh + 1) * 512],
                             start=(kcg == 0), stop=(kcg == 31),
                             inc=(kcg == 31 and hh == 1))
        tok_wdone[pq] = t_mmm
        KY.ws([tok_r1tok])
        t_ul = KY.dma(ust[:], u_fm[dm * 128:(dm + 1) * 128, 4:1028])
        KV.ws([t_r2m, t_mmm, t_ul, tok_r1dma])
        t_s1 = KV.o("tensor_scalar", stg1[:], p_big2[:],
                    outb_s[:, dm:dm + 1], None, ALU.add, inc=True)
        KV.o("tensor_mul", stg2[:], p_big[:], ay_rep[:])
        KV.d()
        KV.o("tensor_add", stg2[:], stg2[:], stg1[:])
        KV.d()
        t_r1 = KV.o("tensor_add", stg2[:], stg2[:], ust[:], inc=True)
        tok_r1tok = t_r1
        KY.w(t_r1)
        tok_r1dma = KY.dma(r1_hbm[dm * 128:(dm + 1) * 128, :], stg2[:])
        KV.d()
        t_bf = KV.o("tensor_copy", bfst[:], stg2[:], inc=True)
        KS.w(t_r1)
        t_sq = KS.o("square", sqst[:], stg2[:], inc=True)
        for hh in range(2):
            KT.ws([t_bf, t_sq, tok_stadd])
            t_sa = KT.o("matmul", p_ra[0:1, 0:512], ones_cb[:],
                        bfst[:, hh * 512:(hh + 1) * 512], start=True, stop=True,
                        inc=True)
            KV.w(t_sa)
            KV.o("tensor_add", sr_r[0:1, hh * 512:(hh + 1) * 512],
                 sr_r[0:1, hh * 512:(hh + 1) * 512], p_ra[0:1, 0:512])
            KV.d()
            t_sb_tok = KV.o("tensor_copy", tmp_r[0:1, 0:1], tmp_r[0:1, 0:1], inc=True)
            KT.w(t_sb_tok)
            t_sa = KT.o("matmul", p_ra[0:1, 0:512], ones_cb[:],
                        sqst[:, hh * 512:(hh + 1) * 512], start=True, stop=True,
                        inc=True)
            KV.w(t_sa)
            KV.o("tensor_add", sr2_r[0:1, hh * 512:(hh + 1) * 512],
                 sr2_r[0:1, hh * 512:(hh + 1) * 512], p_ra[0:1, 0:512])
            KV.d()
            tok_stadd = KV.o("tensor_copy", tmp_r[0:1, 0:1], tmp_r[0:1, 0:1],
                             inc=True)
    kb.barrier_all()
    sbF("stg1", "ust", "v12", "r3f", "ay_rep", "tmat", "sry_r", "sry2_r",
        "muy_r", "ay_r", "f1_r", "f2_r", "f12", "wst5_b", "wst5_b2")

    mu1_r = rowt("mu1_r"); a1_r = rowt("a1_r"); b1_r = rowt("b1_r")
    c_r = rowt("c_r"); ca_r = rowt("ca_r"); cb_r = rowt("cb_r")
    KV.o("tensor_scalar", mu1_r[:], sr_r[:], 1.0 / D, None, ALU.mult)
    KV.o("tensor_scalar", tmp_r[:], sr2_r[:], 1.0 / D, None, ALU.mult)
    KV.d()
    KV.o("tensor_mul", tmp2_r[:], mu1_r[:], mu1_r[:])
    KV.d()
    t_var = KV.o("tensor_sub", tmp_r[:], tmp_r[:], tmp2_r[:], inc=True)
    KS.w(t_var)
    t_sd = KS.o("activation", tmp2_r[:], tmp_r[:], AF.Sqrt, bias=el1[:], scale=1.0,
                inc=True)
    KV.w(t_sd)
    KV.o("reciprocal", a1_r[:], tmp2_r[:])
    KV.d()
    KV.o("tensor_mul", b1_r[:], mu1_r[:], a1_r[:])
    KV.d()
    KV.o("tensor_scalar", b1_r[:], b1_r[:], -1.0, None, ALU.mult)
    KV.d()
    KV.o("tensor_mul", tmp_r[:], a1_r[:], a1_r[:])
    KV.d()
    KV.o("tensor_mul", tmp_r[:], tmp_r[:], sr2_r[:])
    KV.d()
    KV.o("tensor_scalar", tmp_r[:], tmp_r[:], 1.0 / D, None, ALU.mult)
    KV.o("tensor_mul", tmp2_r[:], a1_r[:], b1_r[:])
    KV.d()
    KV.o("tensor_mul", tmp2_r[:], tmp2_r[:], sr_r[:])
    KV.d()
    KV.o("tensor_scalar", tmp2_r[:], tmp2_r[:], 2.0 / D, None, ALU.mult)
    KV.d()
    KV.o("tensor_add", tmp_r[:], tmp_r[:], tmp2_r[:])
    KV.o("tensor_mul", tmp3_r[:], b1_r[:], b1_r[:])
    KV.d()
    t_msx = KV.o("tensor_add", tmp_r[:], tmp_r[:], tmp3_r[:], inc=True)
    KS.w(t_msx)
    t_sd = KS.o("activation", tmp2_r[:], tmp_r[:], AF.Sqrt, bias=ef1[:], scale=1.0,
                inc=True)
    KV.w(t_sd)
    KV.o("reciprocal", c_r[:], tmp2_r[:])
    KV.d()
    KV.o("reciprocal", tmp3_r[0:1, 0:1], tcl_r[:])
    KV.d()
    KV.o("tensor_mul", ca_r[:], c_r[:], a1_r[:])
    KV.d()
    KV.o("tensor_scalar", ca_r[:], ca_r[:], tmp3_r[0:1, 0:1], None, ALU.mult)
    KV.o("tensor_mul", cb_r[:], c_r[:], b1_r[:])
    KV.d()
    t_cb = KV.o("tensor_scalar", cb_r[:], cb_r[:], tmp3_r[0:1, 0:1], None, ALU.mult,
                inc=True)
    a1_rep = sbA("a1_rep", [128, T])
    b1_rep = sbA("b1_rep", [128, T])
    c_rep = sbA("c_rep", [128, T])
    ca_rep = sbA("ca_rep", [128, T])
    KT.w(t_cb)
    tok_repev = None
    for src_r, dst in [(a1_r, a1_rep), (b1_r, b1_rep), (c_r, c_rep), (ca_r, ca_rep)]:
        for hh in range(2):
            KT.w(tok_repev)
            t_mm = KT.o("matmul", p_big[:, 0:512], ones_rf[:],
                        src_r[0:1, hh * 512:(hh + 1) * 512], start=True, stop=True,
                        inc=True)
            KV.w(t_mm)
            tok_repev = KV.o("tensor_copy", dst[:, hh * 512:(hh + 1) * 512],
                             p_big[:, 0:512], inc=True)
    kb.barrier_all()
    sbF("sr_r", "sr2_r", "mu1_r", "a1_r", "b1_r", "c_r", "ca_r", "tmp_r",
        "tmp2_r", "tmp3_r")

    genes_f = sbA("genes_f", [128, 32, GD])
    genes_b = sbA("genes_b", [128, 32, GD], BF16)
    WdT_b = sbA("WdT_b", [128, 2, 16, 128], BF16)
    WuT_b = sbA("WuT_b", [128, 2, 16, 128], BF16)
    t_gl = KY.dma(genes_f[:], genesd.rearrange("(gc p) n -> p gc n", p=128))
    KG.w(t_gl)
    t_gc = KG.o("tensor_copy", genes_b[:].rearrange("p a b -> p (a b)"),
                genes_f[:].rearrange("p a b -> p (a b)"), inc=True)
    for dst, srcw in [(WdT_b, WdTd), (WuT_b, WuTd)]:
        KY.w(t_gc)
        stview = genes_f[:].rearrange("p a b -> p (a b)")[:, 0:4096]
        t_wl = KY.dma(stview.rearrange("p (kk m) -> p kk m", kk=2),
                      srcw.rearrange("(kk p) m -> p kk m", p=128))
        KG.w(t_wl)
        t_gc = KG.o("tensor_copy", dst[:].rearrange("p a b c -> p (a b c)"),
                    stview, inc=True)
    kb.barrier_all()
    sbF("genes_f")

    lgT = sbA("lgT", [128, 2, 32, 128])
    scr = sbA("scr", [128, G])
    tv = sbA("tv", [128, 32])
    tvE = sbA("tvE", [128, 32])
    negm = sbA("negm", [128, 1])
    sw = sbA("sw", [128, 1])
    rden = sbA("rden", [128, 1])
    wm = sbA("wm", [128, G], BF16)
    wT = sbA("wT", [128, 32, 128], BF16)
    ex_b = sbA("ex_b", [128, GD], BF16)
    eT = sbA("eT", [128, 2, 128], BF16)
    r2b = sbA("r2b", [128, 16, GRP])
    r1s = sbA("r1s", [128, GRP])
    xnb = sbA("xnb", [128, 16, GRP], BF16)
    lgst = sbA("lgst", [128, GRP])
    xtmp = sbA("xtmp", [128, 128])
    tmp_r = rowt("tmp_r", GRP); tmp2_r = rowt("tmp2_r", GRP)
    dtile = sbA("dtile", [128, 128])
    utile = sbA("utile", [128, 128])
    m1t = sbA("m1t", [128, 128])
    g2t = sbA("g2t", [128, 128])
    m2_rep = sbA("m2_rep", [128, GRP])
    s2_rep = sbA("s2_rep", [128, GRP])
    gout = sbA("gout", [128, GRP])
    wh_b = sbA("wh_b", [128, 16, 128], BF16)
    wh_b2 = sbA("wh_b2", [128, 16, 128], BF16)
    wcnw_b = sbA("wcnw_b", [1, G], BF16)
    cb_b = sbA("cb_b", [1, T], BF16)
    mu2_r = rowt("mu2_r", GRP); rs2_r = rowt("rs2_r", GRP)
    t2a_r = rowt("t2a_r", GRP); t2b_r = rowt("t2b_r", GRP)

    KV.o("tensor_copy", cb_b[:], cb_r[:])
    KV.o("tensor_copy", negm[:], negm[:], inc=True)
    tok_outdma = None
    tok_lgev = None
    tok_lgtr = None
    tok_xnwar = None
    tok_wcbuf = [None, None]
    for g in range(NGRP):
        g0 = g * GRP
        t_x4 = None
        for dm in range(16):
            KY.ws([tok_xnwar])
            t_r1g = KY.dma(r1s[:], r1_hbm[dm * 128:(dm + 1) * 128, g0:g0 + GRP])
            KV.w(t_r1g)
            t_x4 = KV.o("tensor_scalar", xnb[:, dm, :], r1s[:],
                        nw_s[:, dm:dm + 1], None, ALU.mult, inc=True)
            tok_xnwar = t_x4
        for gc in range(32):
            pq = gc % 2
            buf_b = wh_b if pq == 0 else wh_b2
            KY.ws([tok_wcbuf[pq]])
            t_wl = KY.dma(buf_b[:], wc_bf[:, gc * 128:(gc + 1) * 128]
                          .rearrange("(kc k) m -> k kc m", k=128))
            KT.ws([t_wl, t_x4, tok_lgev])
            t_lmm = None
            for kc in range(16):
                t_lmm = KT.o("matmul", p_big[:, 0:GRP], buf_b[:, kc, :],
                             xnb[:, kc, :], start=(kc == 0), stop=(kc == 15),
                             inc=(kc == 15))
            tok_wcbuf[pq] = t_lmm
            if g == 0:
                for kc in range(16):
                    t_wv = KT.o("matmul", p_b[:, 0:1], buf_b[:, kc, :],
                                nw_bs[:, kc:kc + 1], start=(kc == 0),
                                stop=(kc == 15), inc=(kc == 15))
                tok_wcbuf[pq] = t_wv
                KV.w(t_wv)
                t_wcol = KV.o("tensor_copy", negm[:], p_b[:, 0:1], inc=True)
                KT.w(t_wcol)
                t_wt = KT.o("transpose", p_a[0:1, 0:128], negm[:], id_fs[:],
                            inc=True)
                KV.w(t_wt)
                t_wrow = KV.o("tensor_copy", wcnw_b[0:1, gc * 128:(gc + 1) * 128],
                              p_a[0:1, 0:128], inc=True)
                KT.w(t_wrow)
            t_r1m = KT.o("matmul", p_big2[:, 0:GRP],
                         wcnw_b[0:1, gc * 128:(gc + 1) * 128],
                         cb_b[0:1, g0:g0 + GRP], start=True, stop=True, inc=True)
            KV.ws([t_lmm, t_r1m, tok_lgtr])
            KV.o("tensor_mul", lgst[:], p_big[:, 0:GRP], ca_rep[:, g0:g0 + GRP])
            KV.d()
            t_lg = KV.o("tensor_add", lgst[:], lgst[:], p_big2[:, 0:GRP], inc=True)
            for tc in range(2):
                KT.ws([t_lg, tok_lgev])
                t_tr = KT.o("transpose", p_a[:], lgst[:, tc * 128:(tc + 1) * 128],
                            id_fs[:], inc=True)
                tok_lgtr = t_tr
                KV.w(t_tr)
                tok_lgev = KV.o("tensor_copy", lgT[:, tc, gc, :], p_a[:], inc=True)
        if DBG and g == 0:
            KY.ws([tok_lgev])
            t_d = KY.dma(dbgB, lgT[:, 0, :, :].rearrange("p a b -> p (a b)"))
            KY.w(t_d)
        t2z = KV.o("memset", t2a_r[:], 0.0)
        KV.o("memset", t2b_r[:], 0.0)
        KV.d()
        tok_r2 = None
        tok_l2 = None
        tok_r1war = None
        for tc in range(2):
            lg = lgT[:, tc, :, :].rearrange("p a b -> p (a b)")
            KV.ws([tok_lgev])
            KV.o("max", out=tv[:, 0:8], in_=lg)
            KV.d()
            KV.o("match_replace", out=scr[:], in_to_replace=tv[:, 0:8],
                 in_values=lg, imm_value=-BIG)
            KV.d()
            KV.o("max", out=tv[:, 8:16], in_=scr[:])
            KV.d()
            KV.o("match_replace", out=scr[:], in_to_replace=tv[:, 8:16],
                 in_values=scr[:], imm_value=-BIG)
            KV.d()
            KV.o("max", out=tv[:, 16:24], in_=scr[:])
            KV.d()
            KV.o("match_replace", out=scr[:], in_to_replace=tv[:, 16:24],
                 in_values=scr[:], imm_value=-BIG)
            KV.d()
            KV.o("max", out=tv[:, 24:32], in_=scr[:])
            KV.d()
            t_negm = KV.o("tensor_scalar", negm[:], tv[:, 0:1], -1.0, None, ALU.mult,
                          inc=True)
            KS.w(t_negm)
            t_sw = KS.o("activation", tvE[:], tv[:], AF.Exp, bias=negm[:], scale=1.0,
                        accum_out=sw[:], inc=True)
            KV.w(t_sw)
            t_rden = KV.o("reciprocal", rden[:], sw[:], inc=True)
            t_ef = KS.o("activation", wm[:], lg, AF.Exp, bias=negm[:], scale=1.0,
                        inc=True)
            KV.w(t_ef)
            t_wm = KV.o("scalar_tensor_tensor", wm[:], lg, tv[:, 31:32], wm[:],
                        ALU.is_ge, ALU.mult, inc=True)
            tok_wtev = None
            for gc in range(32):
                KT.ws([t_wm, tok_wtev])
                t_tr = KT.o("transpose", p_c[:], wm[:, gc * 128:(gc + 1) * 128],
                            id_bs[:], inc=True)
                KV.w(t_tr)
                tok_wtev = KV.o("tensor_copy", wT[:, gc, :], p_c[:], inc=True)
            KT.ws([tok_wtev, tok_r2])
            for gc in range(32):
                t_xmm = KT.o("matmul", p_big[:, 0:GD], wT[:, gc, :],
                             genes_b[:, gc, :], start=(gc == 0), stop=(gc == 31),
                             inc=(gc == 31))
            KV.ws([t_xmm, t_rden])
            t_exb = KV.o("tensor_scalar", ex_b[:], p_big[:, 0:GD], rden[:], None,
                         ALU.mult, inc=True)
            tok_etev = None
            for kk in range(2):
                KT.ws([t_exb, tok_etev])
                t_tr = KT.o("transpose", p_c[:], ex_b[:, kk * 128:(kk + 1) * 128],
                            id_bs[:], inc=True)
                KV.w(t_tr)
                tok_etev = KV.o("tensor_copy", eT[:, kk, :], p_c[:], inc=True)
            t_dt = None
            t_ut = None
            for dm in range(16):
                KT.ws([tok_etev, t_dt])
                KT.o("matmul", p_a[:], WdT_b[:, 0, dm, :], eT[:, 0, :],
                     start=True, stop=False)
                t_dmm = KT.o("matmul", p_a[:], WdT_b[:, 1, dm, :], eT[:, 1, :],
                             start=False, stop=True, inc=True)
                KT.w(t_ut)
                KT.o("matmul", p_b[:], WuT_b[:, 0, dm, :], eT[:, 0, :],
                     start=True, stop=False)
                t_umm = KT.o("matmul", p_b[:], WuT_b[:, 1, dm, :], eT[:, 1, :],
                             start=False, stop=True, inc=True)
                KS.w(t_dmm)
                t_dt = KS.o("activation", dtile[:], p_a[:], AF.Tanh,
                            bias=zcol128[:], scale=1.0, inc=True)
                KV.w(t_umm)
                t_ut = KV.o("tensor_copy", utile[:], p_b[:], inc=True)
                KY.ws([tok_r1war, t_x4])
                t_r1sl = KY.dma(r1s[:, 0:128],
                                r1_hbm[dm * 128:(dm + 1) * 128,
                                       g0 + tc * 128:g0 + (tc + 1) * 128])
                KV.ws([t_dt, t_r1sl])
                KV.o("scalar_tensor_tensor", m1t[:], dtile[:], nwsc[:, dm:dm + 1],
                     utile[:], ALU.mult, ALU.mult)
                KV.o("tensor_mul", xtmp[:], r1s[:, 0:128],
                     a1_rep[:, g0 + tc * 128:g0 + (tc + 1) * 128])
                KV.d()
                KV.o("tensor_add", xtmp[:], xtmp[:],
                     b1_rep[:, g0 + tc * 128:g0 + (tc + 1) * 128])
                KV.o("tensor_mul", g2t[:], m1t[:],
                     c_rep[:, g0 + tc * 128:g0 + (tc + 1) * 128])
                KV.d()
                KV.o("tensor_scalar", g2t[:], g2t[:], 1.0, None, ALU.add)
                KV.d()
                t_r2 = KV.o("tensor_mul", r2b[:, dm, tc * 128:(tc + 1) * 128],
                            xtmp[:], g2t[:], inc=True)
                tok_r1war = t_r2
                tok_r2 = t_r2
                KV.d()
                t_bf = KV.o("tensor_copy", bfst[:, 0:128],
                            r2b[:, dm, tc * 128:(tc + 1) * 128], inc=True)
                KS.w(t_r2)
                t_sq = KS.o("square", sqst[:, 0:128],
                            r2b[:, dm, tc * 128:(tc + 1) * 128], inc=True)
                KT.ws([t_bf, t_sq, tok_l2])
                t_sa = KT.o("matmul", p_ra[0:1, 0:128], ones_cb[:], bfst[:, 0:128],
                            start=True, stop=True)
                t_sb = KT.o("matmul", p_ra[0:1, 128:256], ones_cb[:], sqst[:, 0:128],
                            start=True, stop=True, inc=True)
                KV.w(t_sb)
                KV.o("tensor_add", t2a_r[0:1, tc * 128:(tc + 1) * 128],
                     t2a_r[0:1, tc * 128:(tc + 1) * 128], p_ra[0:1, 0:128])
                KV.d()
                KV.o("tensor_add", t2b_r[0:1, tc * 128:(tc + 1) * 128],
                     t2b_r[0:1, tc * 128:(tc + 1) * 128], p_ra[0:1, 128:256])
                KV.d()
                tok_l2 = KV.o("tensor_copy", tmp_r[0:1, 0:1], tmp_r[0:1, 0:1],
                              inc=True)
        KV.o("tensor_scalar", mu2_r[:], t2a_r[:], 1.0 / D, None, ALU.mult)
        KV.o("tensor_scalar", tmp_r[0:1, 0:GRP], t2b_r[:], 1.0 / D, None, ALU.mult)
        KV.d()
        KV.o("tensor_mul", tmp2_r[0:1, 0:GRP], mu2_r[:], mu2_r[:])
        KV.d()
        t_v2 = KV.o("tensor_sub", tmp_r[0:1, 0:GRP], tmp_r[0:1, 0:GRP],
                    tmp2_r[0:1, 0:GRP], inc=True)
        KS.w(t_v2)
        t_sd2 = KS.o("activation", tmp2_r[0:1, 0:GRP], tmp_r[0:1, 0:GRP], AF.Sqrt,
                     bias=el1[:], scale=1.0, inc=True)
        KV.w(t_sd2)
        t_rs2 = KV.o("reciprocal", rs2_r[:], tmp2_r[0:1, 0:GRP], inc=True)
        KT.w(t_rs2)
        t_mm = KT.o("matmul", p_big[:, 0:GRP], ones_rf[:], mu2_r[:], start=True,
                    stop=True, inc=True)
        KV.w(t_mm)
        t_m2 = KV.o("tensor_copy", m2_rep[:], p_big[:, 0:GRP], inc=True)
        KT.w(t_m2)
        t_mm = KT.o("matmul", p_big[:, 0:GRP], ones_rf[:], rs2_r[:], start=True,
                    stop=True, inc=True)
        KV.w(t_mm)
        t_s2 = KV.o("tensor_copy", s2_rep[:], p_big[:, 0:GRP], inc=True)
        KT.w(t_s2)
        tok_xnwar = tok_r1war
        for dm in range(16):
            KV.ws([t_s2, tok_outdma])
            KV.o("tensor_sub", gout[:], r2b[:, dm, :], m2_rep[:])
            KV.d()
            t_o = KV.o("tensor_mul", gout[:], gout[:], s2_rep[:], inc=True)
            KY.w(t_o)
            tok_outdma = KY.dma(out_d[dm * 128:(dm + 1) * 128, g0:g0 + GRP], gout[:])
    KY.w(("sq", kb.cnt["sq"]))
    KG.w(("gq", kb.cnt["gq"]))
    if kb.cnt["cq"]:
        KS.w(("cq", kb.cnt["cq"]))

    with nc.Block() as block:
        @block.tensor
        def _(t):
            for fn in kb.q["t"]:
                fn(t)

        @block.vector
        def _(v):
            for fn in kb.q["v"]:
                fn(v)

        @block.scalar
        def _(s):
            for fn in kb.q["s"]:
                fn(s)

        @block.gpsimd
        def _(g):
            for fn in kb.q["g"]:
                fn(g)

        @block.sync
        def _(y):
            for fn in kb.q["y"]:
                fn(y)

    es.close()
    return nc


def _prep_inputs(inputs):
    u = np.asarray(inputs["L0_attn_u"])
    ipw = np.asarray(inputs["L0_attn_in_proj_w"])
    ipb = np.asarray(inputs["L0_attn_in_proj_b"])
    cvw = np.asarray(inputs["L0_attn_conv_w"])
    cvb = np.asarray(inputs["L0_attn_conv_b"])
    dtb = np.asarray(inputs["L0_attn_dt_bias"])
    alog = np.asarray(inputs["L0_attn_A_log"])
    lnw = np.asarray(inputs["L0_attn_ln_w"])
    lnb = np.asarray(inputs["L0_attn_ln_b"])
    opw = np.asarray(inputs["L0_attn_out_proj_w"])
    opb = np.asarray(inputs["L0_attn_out_proj_b"])
    Wc = np.asarray(inputs["L0_ffn_Wc"])
    temp = np.asarray(inputs["L0_ffn_temperature"])
    gen = np.asarray(inputs["L0_ffn_genes"])
    Wd = np.asarray(inputs["L0_ffn_Wd"])
    Wu = np.asarray(inputs["L0_ffn_Wu"])
    nw = np.asarray(inputs["L0_ffn_norm_w"])
    scale = np.asarray(inputs["L0_ffn_scale"])

    f = np.float32
    WinT = np.zeros((D, DIP), f)
    WinT[:, :8352] = ipw.T
    WinT = np.ascontiguousarray(WinT)
    inb_a = np.zeros((DIP,), f)
    inb_a[:8352] = ipb
    inb_a = np.ascontiguousarray(inb_a.reshape(66, 128).T)
    cw_a = np.ascontiguousarray(
        cvw.reshape(33, 128, 4).transpose(1, 0, 2).reshape(128, 132)).astype(f)
    cb_a = np.ascontiguousarray(cvb.reshape(33, 128).T).astype(f)
    lnw_a = np.ascontiguousarray(lnw.reshape(32, 128).T).astype(f)
    lnb_a = np.ascontiguousarray(lnb.reshape(32, 128).T).astype(f)
    WoutT = np.ascontiguousarray(opw.T).astype(f)
    outb_a = np.ascontiguousarray(opb.reshape(16, 128).T).astype(f)
    WcT = np.ascontiguousarray(Wc.T).astype(f)
    nw_a = np.ascontiguousarray(nw.reshape(16, 128).T).astype(f)
    WdT = np.ascontiguousarray(Wd.T).astype(f)
    WuT = np.ascontiguousarray(Wu.T).astype(f)
    temp_a = np.broadcast_to(np.asarray(temp, f).reshape(1, 1), (1, 1)).copy()
    scl_a = np.broadcast_to(np.asarray(scale, f).reshape(1, 1), (128, 1)).copy()

    self32 = np.zeros((32, 32 * 128), f)
    for h in range(32):
        self32[h, h * 128:(h + 1) * 128] = 1.0
    sel64 = np.zeros((32, 32 * 64), np.float32)
    for h in range(32):
        sel64[h, h * 64:(h + 1) * 64] = 1.0
    sel64 = sel64.astype(ml_dtypes.bfloat16)
    selP = np.zeros((32, 16 * 128), f)
    for j in range(16):
        for p in range(128):
            selP[2 * j + (1 if p >= 64 else 0), j * 128 + p] = 1.0
    ones_rf_a = np.ones((1, 128), f)
    ones_cb_a = np.ones((128, 1), np.float32).astype(ml_dtypes.bfloat16)
    id_b_a = np.eye(128, dtype=np.float32).astype(ml_dtypes.bfloat16)
    id_f_a = np.eye(128, dtype=f)

    shared = dict(
        WinT=WinT, inb=inb_a, cwd=cw_a, cbd=cb_a,
        dtbd=np.ascontiguousarray(dtb.reshape(32, 1)).astype(f),
        alogd=np.ascontiguousarray(alog.reshape(32, 1)).astype(f),
        lnwd=lnw_a, lnbd=lnb_a, WoutT=WoutT, outbd=outb_a, WcT=WcT, nwd=nw_a,
        genesd=gen.astype(f), WdTd=WdT, WuTd=WuT, tempd=temp_a, scld=scl_a,
        self32d=self32, sel64d=sel64, selPd=selP, onesrfd=ones_rf_a,
        onescbd=ones_cb_a, idbd=id_b_a, idfd=id_f_a,
    )

    in_maps = []
    for core in range(8):
        b, r = core // 4, core % 4
        s0 = r * T
        useq = u[b]
        pad = np.zeros((4, D), f)
        if s0 >= 4:
            blk = useq[s0 - 4:s0 + T]
        else:
            blk = np.concatenate([pad[:4 - 0], useq[0:s0 + T]], 0) if s0 == 0 else None
        if s0 == 0:
            blk = np.concatenate([pad, useq[0:T]], 0)
        u_c = np.ascontiguousarray(blk.T)
        pm_a = np.zeros((128, 4), f)
        pm_a[:, :r] = 1.0
        pm1_a = 1.0 - pm_a
        m = dict(shared)
        m["u_fm"] = u_c
        m["pmd"] = pm_a
        m["pm1d"] = pm1_a
        in_maps.append(m)
    return in_maps


def kernel(**inputs):
    if "nc" not in _CACHE:
        _CACHE["nc"] = build_core()
    nc = _CACHE["nc"]
    in_maps = _prep_inputs(inputs)
    res = run_bass_kernel_spmd(nc, in_maps, core_ids=list(range(8)))
    out = np.empty((2, 4096, D), np.float32)
    for core in range(8):
        b, r = core // 4, core % 4
        out[b, r * T:(r + 1) * T, :] = res.results[core]["out"].T
    return out


# version 04
# speedup vs baseline: 1.0751x; 1.0173x over previous
import numpy as np
import ml_dtypes
import concourse.bass as bass
import concourse.mybir as mybir
from concourse.bass_utils import run_bass_kernel_spmd

F32 = mybir.dt.float32
BF16 = mybir.dt.bfloat16
AF = mybir.ActivationFunctionType
ALU = mybir.AluOpType

D = 2048
DI = 4096
NH = 32
NS = 64
DIP = 8448
T = 1024
TH = 1028
NCH = 8
G = 4096
GD = 256
GRP = 256
NGRP = T // GRP
BIG = 1.0e30
DBG = False
DBGCC = False
DBG3 = False
LNEPS = 1e-5
F32EPS = float(np.finfo(np.float32).eps)

_CACHE = {}


class Eng:
    DSEM = {"v": "dv", "s": "ds", "g": "dg", "t": "dt_", "y": "dy"}

    def __init__(self, kb, key, semkey, dmakey):
        self.kb, self.key, self.semkey, self.dmakey = kb, key, semkey, dmakey
        self.dsem = self.DSEM[key]
        self.cell = [None]

    def w(self, tok):
        if tok is None:
            return
        k, v = tok
        self.kb.q[self.key].append(lambda e, k=k, v=v: e.wait_ge(self.kb.sem[k], v))

    def ws(self, toks):
        best = {}
        for t in toks:
            if t is None:
                continue
            k, v = t
            best[k] = max(best.get(k, 0), v)
        for k, v in best.items():
            self.w((k, v))

    def d(self):
        if self.cell[0] is None:
            self.kb.q[self.key].append(lambda e: e.drain())
            return
        marker = self.cell[0]
        self.cell[0] = None
        if marker.get("tok") is not None:
            self.w(marker["tok"])
            return
        sem = self.dsem
        v = self.kb.bump(sem)
        self.kb.q[self.key].append(
            lambda e, m=marker: m["inst"].then_inc(self.kb.sem[sem], 1))
        self.kb.q[self.key].append(lambda e, v=v: e.wait_ge(self.kb.sem[sem], v))

    def o(self, fname, *a, inc=False, **kw):
        tok = None
        marker = {"inst": None, "tok": None}
        self.cell[0] = marker
        if inc:
            v = self.kb.bump(self.semkey)
            tok = (self.semkey, v)
            marker["tok"] = tok

            def fn(e, a=a, kw=kw, m=marker):
                m["inst"] = getattr(e, fname)(*a, **kw).then_inc(
                    self.kb.sem[self.semkey], 1)
            self.kb.q[self.key].append(fn)
        else:
            def fn(e, a=a, kw=kw, m=marker):
                m["inst"] = getattr(e, fname)(*a, **kw)
            self.kb.q[self.key].append(fn)
        return tok

    def dma(self, dst, src, **kw):
        v = self.kb.bump(self.dmakey, 16)
        tok = (self.dmakey, v)
        self.kb.q[self.key].append(
            lambda e, kw=kw: e.dma_start(out=dst, in_=src, **kw).then_inc(
                self.kb.sem[self.dmakey], 16))
        return tok

    def cc(self, kind, op, groups, ins, outs):
        v = self.kb.bump("cc", 1)
        self.kb.q[self.key].append(
            lambda e: e.collective_compute(kind, op, replica_groups=groups,
                                           ins=ins, outs=outs).then_inc(
                self.kb.sem["cc"]))
        return ("cc", v)


class Kb:
    def __init__(self, nc):
        self.nc = nc
        self.q = {k: [] for k in "tvsgy"}
        self.cnt = {k: 0 for k in ["et", "ev", "es", "eg", "sq", "cq", "gq",
                                   "cc", "bgc", "bgq", "dv", "ds", "dg",
                                   "dt_", "dy"]}
        self.sem = {}
        self.t = Eng(self, "t", "et", None)
        self.v = Eng(self, "v", "ev", None)
        self.s = Eng(self, "s", "es", "cq")
        self.g = Eng(self, "g", "eg", "gq")
        self.y = Eng(self, "y", None, "sq")
        self.all = [self.t, self.v, self.s, self.g, self.y]

    def bump(self, s, n=1):
        self.cnt[s] += n
        return self.cnt[s]

    def barrier_all(self, exclude=()):
        toks = [(k, self.cnt[k]) for k in self.cnt
                if self.cnt[k] > 0 and k not in exclude]
        for e in self.all:
            e.ws(toks)


def build_core():
    nc = bass.Bass("TRN2", target_bir_lowering=False)
    kb = Kb(nc)
    KT, KV, KS, KG, KY = kb.t, kb.v, kb.s, kb.g, kb.y

    def din(name, shape, dt=F32):
        return nc.dram_tensor(name, shape, dt, kind="ExternalInput").ap()

    u_fm = din("u_fm", [D, TH])
    WinT2 = din("WinT2", [128, 66 * 16 * 128])
    inb = din("inb", [128, 66])
    cwd = din("cwd", [128, 33 * 4])
    cbd = din("cbd", [128, 33])
    dtbd = din("dtbd", [32, 1])
    alogd = din("alogd", [32, 1])
    lnwd = din("lnwd", [128, 32])
    lnbd = din("lnbd", [128, 32])
    WoutT = din("WoutT", [DI, D])
    outbd = din("outbd", [128, 16])
    WcT = din("WcT", [D, G])
    nwd = din("nwd", [128, 16])
    genesd = din("genesd", [G, GD])
    WdTd = din("WdTd", [GD, D])
    WuTd = din("WuTd", [GD, D])
    tempd = din("tempd", [1, 1])
    scld = din("scld", [128, 1])
    pmd = din("pmd", [128, 4])
    pm1d = din("pm1d", [128, 4])
    self32d = din("self32d", [32, 32 * 128])
    sel64d = din("sel64d", [32, 32 * 64], BF16)
    selPd = din("selPd", [32, 16 * 128])
    onesrfd = din("onesrfd", [1, 128])
    onescbd = din("onescbd", [128, 1], BF16)
    idbd = din("idbd", [128, 128], BF16)
    idfd = din("idfd", [128, 128])

    out_d = nc.dram_tensor("out", [D, T], F32, kind="ExternalOutput").ap()

    KI = "ExternalOutput" if DBG else "Internal"
    z_hbm = nc.dram_tensor("z_hbm", [DI, T], F32, kind=KI).ap()
    x_hbm = nc.dram_tensor("x_hbm", [T, DI], BF16, kind=KI).ap()
    r1_hbm = nc.dram_tensor("r1_hbm", [D, T], F32, kind=KI).ap()
    s_hbm = nc.dram_tensor("s_hbm", [128, NCH * 16 * 128], BF16).ap()
    wout_bf = nc.dram_tensor("wout_bf", [DI, D], BF16).ap()
    wc_bf = nc.dram_tensor("wc_bf", [D, G], BF16).ap()
    dbgA = dbgB = dbgC = dbgD = dbgE = dbgF = dbgG = dbgH = None
    if DBG or DBG3:
        dbgA = nc.dram_tensor("dbgA", [32, T], F32, kind="ExternalOutput").ap()
        dbgB = nc.dram_tensor("dbgB", [128, 4096], F32, kind="ExternalOutput").ap()
        dbgC = nc.dram_tensor("dbgC", [128, 64], BF16, kind="ExternalOutput").ap()
        dbgD = nc.dram_tensor("dbgD", [128, TH], BF16, kind="ExternalOutput").ap()
        dbgE = nc.dram_tensor("dbgE", [128, T], BF16, kind="ExternalOutput").ap()
        dbgF = nc.dram_tensor("dbgF", [8, T], F32, kind="ExternalOutput").ap()
        dbgG = nc.dram_tensor("dbgG", [128, T], BF16, kind="ExternalOutput").ap()
        dbgH = nc.dram_tensor("dbgH", [128, 2064], F32, kind="ExternalOutput").ap()
    cc_in = nc.dram_tensor("cc_in", [128, 16 * 128 + 16], F32).ap()
    cc_out = nc.dram_tensor("cc_out", [4 * 128, 16 * 128 + 16], F32).ap()
    dbgI = (nc.dram_tensor("dbgI", [4 * 128, 16 * 128 + 16], F32,
                           kind="ExternalOutput").ap() if (DBG or DBGCC) else None)

    import math as _math
    _lo = nc.s._sbuf_addr_for_side("left") if hasattr(nc, "s") else None
    try:
        _lo = nc.m.bass._sbuf_addr_for_side("left")
    except Exception:
        _lo = None
    _ALO = [None, None]

    def _init_region():
        lo = nc._sbuf_addr_for_side("left")
        hi = nc._sbuf_addr_for_side("right")
        _ALO[0] = [(lo, hi)]
        _ALO[1] = {}

    def sbA(name, shape, dt=F32, hi=False):
        if _ALO[0] is None:
            _init_region()
        nb = int(np.prod(shape[1:])) * mybir.dt.size(dt)
        nb = (nb + 31) // 32 * 32
        free = _ALO[0]
        order = reversed(list(enumerate(free))) if hi else list(enumerate(free))
        for i, (a, b) in order:
            if hi:
                addr = (b - nb) // 32 * 32
                if addr < a:
                    continue
                if addr + nb == b:
                    if addr == a:
                        free.pop(i)
                    else:
                        free[i] = (a, addr)
                else:
                    free[i] = (a, addr)
                    if addr + nb < b:
                        free.insert(i + 1, (addr + nb, b))
                _ALO[1][name] = (addr, nb)
                return nc.alloc_sbuf_tensor_at(name, list(shape), dt, offset=addr)
            a0 = (a + 31) // 32 * 32
            if b - a0 >= nb:
                addr = a0
                if addr + nb == b and a0 == a:
                    free.pop(i)
                else:
                    free[i] = (addr + nb, b)
                _ALO[1][name] = (addr, nb)
                return nc.alloc_sbuf_tensor_at(name, list(shape), dt, offset=addr)
        livel = sorted(((v[0], v[1], k) for k, v in _ALO[1].items()))
        raise AssertionError(
            f"manual sbuf OOM for {name} {shape} need {nb} free={free}\n" +
            "\n".join(f"  {a:7d} +{n:6d} {k}" for a, n, k in livel))

    def sbF(*names):
        free = _ALO[0]
        for n in names:
            a, nb = _ALO[1].pop(n)
            free.append((a, a + nb))
        free.sort()
        out = []
        for a, b in free:
            if out and out[-1][1] == a:
                out[-1] = (out[-1][0], b)
            else:
                out.append((a, b))
        _ALO[0] = [tuple(x) for x in out]
        _ALO[0] = out

    from contextlib import ExitStack
    es = ExitStack()

    def pst(name, shape, dt=F32):
        return es.enter_context(nc.psum_tensor(name, shape, dt))

    for s in ["et", "ev", "es", "eg", "sq", "cq", "gq", "cc", "bgc", "bgq",
              "dv", "ds", "dg", "dt_", "dy"]:
        kb.sem[s] = es.enter_context(nc.semaphore(s))

    p_big = pst("p_big", [128, 1024])
    p_big2 = pst("p_big2", [128, 1024])
    p_a = pst("p_a", [128, 128])
    p_b = pst("p_b", [128, 128])
    p_c = pst("p_c", [128, 128], BF16)
    p_ra = pst("p_ra", [1, 512])

    cw_s = sbA("cw_s", [128, 33 * 4])
    cb_s = sbA("cb_s", [128, 33])
    inb_s = sbA("inb_s", [128, 66])
    dtb_s = sbA("dtb_s", [32, 1])
    alog_s = sbA("alog_s", [32, 1])
    nAcol = sbA("nAcol", [32, 1])
    lnw_s = sbA("lnw_s", [128, 32])
    lnb_s = sbA("lnb_s", [128, 32])
    lnwb2 = sbA("lnwb2", [128, 32, 2], BF16)
    outb_s = sbA("outb_s", [128, 16])
    nw_s = sbA("nw_s", [128, 16])
    nw_bs = sbA("nw_bs", [128, 16], BF16)
    temp_s = sbA("temp_s", [1, 1])
    scl_s = sbA("scl_s", [128, 1])
    pm_s = sbA("pm_s", [128, 4])
    pm1_s = sbA("pm1_s", [128, 4])
    self32_s = sbA("self32_s", [32, 32 * 128])
    sel64_s = sbA("sel64_s", [32, 32 * 64], BF16)
    selP_s = sbA("selP_s", [32, 16 * 128])
    ones_rf = sbA("ones_rf", [1, 128])
    ones_cb = sbA("ones_cb", [128, 1], BF16)
    id_bs = sbA("id_bs", [128, 128], BF16)
    id_fs = sbA("id_fs", [128, 128])
    mtri = sbA("mtri", [128, 128])
    zc32 = sbA("zc32", [32, 1])
    zcol128 = sbA("zcol128", [128, 1])
    onec32 = sbA("onec32", [32, 1])
    el1 = sbA("el1", [1, 1])
    ef1 = sbA("ef1", [1, 1])
    nwsc = sbA("nwsc", [128, 16])
    tcl_r = sbA("tcl_r", [1, 1])

    def rowt(name, n=T):
        return sbA(name, [1, n])

    dtraw = sbA("dtraw", [32, TH])
    wst_f = sbA("wst_f", [128, 16, 128])
    wst_f2 = sbA("wst_f2", [128, 16, 128])
    wst_f3 = sbA("wst_f3", [128, 16, 128])
    wst_b = sbA("wst_b", [128, 16, 128], BF16)
    wst_b2 = sbA("wst_b2", [128, 16, 128], BF16)
    wst_b3 = sbA("wst_b3", [128, 16, 128], BF16)
    trst = sbA("trst", [128, 128], BF16)

    t_l = None
    for dst, src in [
        (cw_s[:], cwd), (cb_s[:], cbd), (inb_s[:], inb), (dtb_s[:], dtbd),
        (alog_s[:], alogd), (lnw_s[:], lnwd), (lnb_s[:], lnbd), (outb_s[:], outbd),
        (nw_s[:], nwd), (temp_s[:], tempd), (scl_s[:], scld), (pm_s[:], pmd),
        (pm1_s[:], pm1d), (self32_s[:], self32d), (sel64_s[:], sel64d),
        (selP_s[:], selPd), (ones_rf[:], onesrfd), (ones_cb[:], onescbd),
        (id_bs[:], idbd), (id_fs[:], idfd),
    ]:
        t_l = KY.dma(dst, src)
    tok_small = t_l

    KG.o("memset", mtri[:], 0.0)
    KG.d()
    KG.o("affine_select", mtri[:], mtri[:], pattern=[[1, 128]],
         compare_op=ALU.is_ge, fill=-BIG, base=0, channel_multiplier=-1, inc=True)

    KV.w(tok_small)
    KV.o("memset", zc32[:], 0.0)
    KV.o("memset", zcol128[:], 0.0)
    KV.o("memset", onec32[:], 1.0)
    KV.o("memset", el1[:], LNEPS)
    KV.o("memset", ef1[:], F32EPS)
    KV.o("tensor_scalar", nwsc[:], nw_s[:], scl_s[:, 0:1], None, ALU.mult)
    KV.o("tensor_copy", nw_bs[:], nw_s[:])
    KV.o("tensor_copy", lnwb2[:, :, 0], lnw_s[:])
    KV.o("tensor_copy", lnwb2[:, :, 1], lnb_s[:])
    KV.o("tensor_scalar", tcl_r[:], temp_s[:], 0.1, None, ALU.max, inc=True)

    KS.w(tok_small)
    KS.o("activation", nAcol[:], alog_s[:], AF.Exp, bias=zc32[:], scale=1.0)
    KS.d()
    KS.o("mul", nAcol[:], nAcol[:], -1.0, inc=True)
    kb.barrier_all()

    uf_st = sbA("uf_st", [128, 16, 257])
    u_b = sbA("u_b", [128, 16, TH], BF16)
    zst = sbA("zst", [128, 2, 1024])
    tok_uc = None
    for h in range(4):
        t_u = KY.dma(uf_st[:], u_fm.rearrange("(kc k) t -> k kc t", k=128)
                     [:, :, h * 257:(h + 1) * 257])
        KG.w(t_u)
        tok_uc = KG.o("tensor_copy", u_b[:, :, h * 257:(h + 1) * 257], uf_st[:],
                      inc=True)
        if h < 3:
            KY.w(tok_uc)
    sbF("uf_st")

    xBC = sbA("xBC", [128, 33, TH], BF16)
    sweeps = [(0, 512, p_big, 0), (512, 512, p_big, 512), (1024, 4, p_big2, 0)]
    NWB = 3
    wfb = [wst_f, wst_f2, wst_f3]
    wbb = [wst_b, wst_b2, wst_b3]
    tok_wbuf = [None] * NWB
    tok_wcast = [None] * NWB
    tok_zdma = [None, None]
    tok_evac_prev = None
    for oc in range(66):
        pq = oc % NWB
        buf_f = wfb[pq]
        buf_b = wbb[pq]
        KY.ws([tok_wcast[pq]])
        t_wl = KY.dma(buf_f[:], WinT2[:, oc * 2048:(oc + 1) * 2048]
                      .rearrange("k (kc m) -> k kc m", m=128))
        KV.ws([t_wl, tok_wbuf[pq], tok_uc])
        t_wc = KV.o("tensor_copy", buf_b[:], buf_f[:], inc=True)
        tok_wcast[pq] = t_wc
        KT.w(t_wc)
        if tok_evac_prev is not None:
            KT.w(tok_evac_prev)
        last_mm = None
        for (t0, n, pp, po) in sweeps:
            for kc in range(16):
                tokm = KT.o("matmul", pp[:, po:po + n], buf_b[:, kc, :],
                            u_b[:, kc, t0:t0 + n], start=(kc == 0), stop=(kc == 15),
                            inc=(kc == 15))
                if kc == 15:
                    last_mm = tokm
        tok_wbuf[pq] = last_mm
        KS.w(last_mm)
        if oc < 32:
            sl = oc % 2
            KS.w(tok_zdma[sl])
            KS.o("activation", zst[:, sl, 0:1020], p_big[:, 4:1024], AF.Silu,
                 bias=inb_s[:, oc:oc + 1], scale=1.0)
            KS.d()
            t_ev = KS.o("activation", zst[:, sl, 1020:1024], p_big2[:, 0:4], AF.Silu,
                        bias=inb_s[:, oc:oc + 1], scale=1.0, inc=True)
            KY.w(t_ev)
            tok_zdma[sl] = KY.dma(z_hbm[oc * 128:(oc + 1) * 128, :], zst[:, sl, :])
        elif oc < 65:
            c = oc - 32
            KV.w(last_mm)
            KV.o("tensor_scalar", xBC[:, c, 0:1024], p_big[:, 0:1024],
                 inb_s[:, oc:oc + 1], None, ALU.add)
            t_ev = KV.o("tensor_scalar", xBC[:, c, 1024:1028], p_big2[:, 0:4],
                        inb_s[:, oc:oc + 1], None, ALU.add, inc=True)
        else:
            KV.w(last_mm)
            KV.o("tensor_scalar", dtraw[:, 0:1024], p_big[0:32, 0:1024],
                 inb_s[0:32, oc:oc + 1], None, ALU.add)
            t_ev = KV.o("tensor_scalar", dtraw[:, 1024:1028], p_big2[0:32, 0:4],
                        inb_s[0:32, oc:oc + 1], None, ALU.add, inc=True)
        tok_evac_prev = t_ev
    kb.barrier_all()
    sbF("u_b", "zst", "wst_f", "wst_f2", "wst_f3", "wst_b", "wst_b2", "wst_b3")

    bgf = [sbA("bgf0", [128, 2048]), sbA("bgf1", [128, 2048])]
    bgb = [sbA("bgb0", [128, 2048], BF16), sbA("bgb1", [128, 2048], BF16)]
    bg_jobs = []
    for r0 in range(0, DI, 128):
        bg_jobs.append((WoutT[r0:r0 + 128, :], wout_bf[r0:r0 + 128, :]))
    for r0 in range(0, D, 128):
        bg_jobs.append((WcT[r0:r0 + 128, 0:2048], wc_bf[r0:r0 + 128, 0:2048]))
        bg_jobs.append((WcT[r0:r0 + 128, 2048:4096], wc_bf[r0:r0 + 128, 2048:4096]))
    tok_bgo = [None, None]
    for i, (srcw, dstw) in enumerate(bg_jobs):
        p = i % 2
        KG.ws([tok_bgo[p]])
        v = kb.bump("bgq", 16)
        kb.q["g"].append(lambda e, s=srcw, p=p: e.dma_start(
            out=bgf[p][:], in_=s).then_inc(kb.sem["bgq"], 16))
        KG.w(("bgq", v))
        vc = kb.bump("bgc")
        kb.q["g"].append(lambda e, p=p: e.tensor_copy(
            bgb[p][:], bgf[p][:]).then_inc(kb.sem["bgc"], 1))
        KG.w(("bgc", vc))
        v2 = kb.bump("bgq", 16)
        kb.q["g"].append(lambda e, d=dstw, p=p: e.dma_start(
            out=d, in_=bgb[p][:]).then_inc(kb.sem["bgq"], 16))
        tok_bgo[p] = ("bgq", v2)
    tok_bg_done = ("bgq", kb.cnt["bgq"])

    dt_sp = sbA("dt_sp", [32, T])
    Acum = sbA("Acum", [32, T])
    nAc = sbA("nAc", [32, T], hi=True)
    AcumT = sbA("AcumT", [128, NCH, 32], hi=True)
    dtT = sbA("dtT", [128, NCH, 32], hi=True)
    dtdsT = sbA("dtdsT", [128, NCH, 32], hi=True)
    sdo_b = sbA("sdo_b", [32, T], BF16, hi=True)
    Dmat = sbA("Dmat", [32, NCH])
    eLast = sbA("eLast", [32, NCH])
    eStart = sbA("eStart", [32, NCH])
    nStart = sbA("nStart", [32, NCH])
    Dagg = sbA("Dagg", [32, 1])
    Dp = sbA("Dp", [128, NCH, 16])
    zero32 = sbA("zero32", [32, T])
    sp_a = sbA("sp_a", [32, T])
    sp_b = sbA("sp_b", [32, T])
    Bn = sbA("Bn", [64, T], BF16, hi=True)
    Ctn = sbA("Ctn", [64, T], BF16, hi=True)
    Btok = sbA("Btok", [128, NCH, 64], BF16, hi=True)
    KV.o("memset", zero32[:], 0.0)
    KV.d()
    KS.o("activation", sp_a[:], dtraw[0:32, 4:1028], AF.Relu, bias=dtb_s[:], scale=1.0)
    KS.o("activation", sp_b[:], dtraw[0:32, 4:1028], AF.Abs, bias=dtb_s[:], scale=1.0)
    KS.d()
    KS.o("activation", sp_b[:], sp_b[:], AF.Exp, bias=zc32[:], scale=-1.0)
    KS.d()
    t_sp = KS.o("activation", sp_b[:], sp_b[:], AF.Ln, bias=onec32[:], scale=1.0,
                inc=True)
    KV.w(t_sp)
    KV.o("tensor_add", dt_sp[:], sp_a[:], sp_b[:])
    KV.d()
    KV.o("tensor_scalar", sp_a[:], dt_sp[:], nAcol[:], None, ALU.mult)
    KV.d()
    KV.o("tensor_tensor_scan", Acum[:], sp_a[:], zero32[:], 0.0, ALU.add, ALU.add)
    KV.d()
    KV.o("tensor_scalar", nAc[:], Acum[:], -1.0, None, ALU.mult)
    KV.o("tensor_copy", eLast[:], Acum.ap().rearrange("p (c l) -> p c l", l=128)[:, :, 127])
    KV.d()
    KV.o("memset", eStart[:, 0:1], 0.0)
    KV.o("tensor_copy", eStart[:, 1:8], eLast[:, 0:7])
    KV.d()
    KV.o("tensor_sub", eLast[:], eLast[:], eStart[:])
    KV.d()
    t_nst = KV.o("tensor_scalar", nStart[:], eStart[:], -1.0, None, ALU.mult, inc=True)
    KS.w(t_nst)
    KS.o("activation", Dmat[:], eLast[:], AF.Exp, bias=zc32[:], scale=1.0)
    KS.o("activation", Dagg[:], Acum[:, 1023:1024], AF.Exp, bias=zc32[:], scale=1.0)
    for ct in range(NCH):
        KS.o("activation", sdo_b[:, ct * 128:(ct + 1) * 128],
             Acum[:, ct * 128:(ct + 1) * 128], AF.Exp,
             bias=nStart[:, ct:ct + 1], scale=1.0)
    KS.d()
    t_ds = None
    for ct in range(NCH):
        t_ds = KS.o("activation", sp_b[:, ct * 128:(ct + 1) * 128],
                    nAc[:, ct * 128:(ct + 1) * 128], AF.Exp,
                    bias=Acum[:, ct * 128 + 127:ct * 128 + 128], scale=1.0,
                    inc=(ct == NCH - 1))
    KV.w(t_ds)
    t_dtds = KV.o("tensor_mul", sp_b[:], sp_b[:], dt_sp[:], inc=True)

    KT.w(t_dtds)
    for ct in range(NCH):
        for srcr, dstt in [(nAc, AcumT), (dt_sp, dtT), (sp_b, dtdsT)]:
            t_tr = KT.o("transpose", p_a[:, 0:32], srcr[:, ct * 128:(ct + 1) * 128],
                        id_fs[0:32, 0:32], inc=True)
            KV.w(t_tr)
            t_ev = KV.o("tensor_copy", dstt[:, ct, :], p_a[:, 0:32], inc=True)
            KT.w(t_ev)

    convq = sbA("convq", [128, T])
    xc_b = sbA("xc_b", [128, T], BF16)
    xtr0 = sbA("xtr0", [128, 8, 128], BF16)
    xtr1 = sbA("xtr1", [128, 8, 128], BF16)
    tok_xstore = None
    tok_trev = None
    tok_xdma = [None, None]
    tok_bt = None
    for c in range(33):
        KV.o("tensor_scalar", convq[:], xBC[:, c, 1:1025],
             cw_s[:, c * 4:c * 4 + 1], None, ALU.mult)
        for k in range(1, 3):
            KV.d()
            KV.o("scalar_tensor_tensor", convq[:], xBC[:, c, 1 + k:1025 + k],
                 cw_s[:, c * 4 + k:c * 4 + k + 1], convq[:], ALU.mult, ALU.add)
        KV.d()
        t_cq = KV.o("scalar_tensor_tensor", convq[:], xBC[:, c, 4:1028],
                    cw_s[:, c * 4 + 3:c * 4 + 4], convq[:], ALU.mult, ALU.add,
                    inc=True)
        KS.ws([t_cq, tok_trev, tok_bt])
        t_sil = KS.o("activation", xc_b[:], convq[:], AF.Silu,
                     bias=cb_s[:, c:c + 1], scale=1.0, inc=True)
        if DBG and c == 0:
            KY.w(t_sil)
            t_d = KY.dma(dbgE, xc_b[:])
            KY.w(t_d)
            t_d2 = KY.dma(dbgD, xBC[:, 0, :])
            KY.w(t_d2)
        KV.w(t_sil)
        if c < 32:
            pp = c % 2
            xb = xtr0 if pp == 0 else xtr1
            for ct in range(NCH):
                KT.ws([t_sil, tok_trev])
                t_tr = KT.o("transpose", p_c[:], xc_b[:, ct * 128:(ct + 1) * 128],
                            id_bs[:], inc=True)
                KV.ws([t_tr, tok_xdma[pp]])
                tok_trev = KV.o("tensor_copy", xb[:, ct, :], p_c[:], inc=True)
            KY.w(tok_trev)
            tok_xdma[pp] = KY.dma(
                x_hbm[:, c * 128:(c + 1) * 128]
                .rearrange("(ct q) m -> q ct m", q=128), xb[:])
            tok_xstore = tok_xdma[pp]
        else:
            KV.w(t_sil)
            KV.o("tensor_copy", Bn[:], xc_b[0:64, :])
            KV.d()
            t_bn = KV.o("tensor_copy", Ctn[:], xc_b[64:128, :], inc=True)
            KT.w(t_bn)
            for ct in range(NCH):
                t_tr = KT.o("transpose", p_c[:, 0:64],
                            Bn[:, ct * 128:(ct + 1) * 128], id_bs[0:64, 0:64],
                            inc=True)
                KV.w(t_tr)
                tok_bt = KV.o("tensor_copy", Btok[:, ct, :], p_c[:, 0:64], inc=True)
                KT.w(tok_bt)
    if DBG or DBG3:
        t_d = KY.dma(dbgA, Acum[:])
        KY.w(t_d)
    kb.barrier_all(exclude=("bgq", "bgc"))
    sbF("xBC", "convq", "xc_b", "xtr0", "xtr1", "dt_sp", "zero32", "sp_a",
        "sp_b", "dtraw")

    StIn = sbA("StIn", [128, NCH, 16, 128], BF16, hi=True)
    tmat0 = sbA("tmat0", [128, 16, T], BF16, hi=True)
    tmat1 = sbA("tmat1", [128, 16, T], BF16, hi=True)
    agg = sbA("agg", [128, 16, 128])
    xchk = sbA("xchk", [128, 32, 128], BF16)
    xdt_ds = sbA("xdt_ds", [128, 128], BF16)
    sstg = sbA("sstg", [128, 128], BF16)
    sslice = sbA("sslice", [128, 16, 128], BF16)
    Dap = sbA("Dap", [128, 16])

    for j in range(16):
        t_mm = KT.o("matmul", p_a[:, 0:8], selP_s[:, j * 128:(j + 1) * 128],
                    Dmat[:], start=True, stop=True, inc=True)
        KV.w(t_mm)
        t_dp = KV.o("tensor_copy", Dp[:, :, j], p_a[:, 0:8], inc=True)
        KT.w(t_dp)
        t_mm = KT.o("matmul", p_a[:, 0:1], selP_s[:, j * 128:(j + 1) * 128],
                    Dagg[:], start=True, stop=True, inc=True)
        KV.w(t_mm)
        t_dap = KV.o("tensor_copy", Dap[:, j:j + 1], p_a[:, 0:1], inc=True)
        KT.w(t_dap)

    tok_schunk = None
    tok_sdma = None
    tok_xwar = None
    for ct in range(NCH):
        KY.w(tok_xwar)
        t_x = KY.dma(xchk[:], x_hbm[ct * 128:(ct + 1) * 128, :]
                     .rearrange("p (hc q) -> p hc q", q=128))
        KV.w(t_x)
        for h in range(NH):
            t_xs = KV.o("tensor_scalar", xdt_ds[:], xchk[:, h, :],
                        dtdsT[:, ct, h:h + 1], None, ALU.mult, inc=True)
            KT.w(t_xs)
            half = 64 * (h % 2)
            t_mm = KT.o("matmul", p_b[0:64, :], Btok[:, ct, :], xdt_ds[:],
                        start=True, stop=True, inc=True)
            KV.ws([t_mm, tok_sdma])
            tok_schunk = KV.o("tensor_copy", sstg[half:half + 64, :],
                              p_b[0:64, :], inc=True)
            KT.w(tok_schunk)
            KV.d()
            KY.w(tok_schunk)
            tok_sdma = KY.dma(
                s_hbm[half:half + 64,
                      (ct * 16 + h // 2) * 128:(ct * 16 + h // 2 + 1) * 128],
                sstg[half:half + 64, :])
        tok_xwar = tok_schunk
    t_ss = None
    for ct in range(NCH):
        KY.ws([tok_sdma, t_ss])
        t_sl = KY.dma(sslice[:].rearrange("p a b -> p (a b)"),
                      s_hbm[:, ct * 2048:(ct + 1) * 2048])
        KV.w(t_sl)
        if ct == 0:
            KV.o("tensor_copy", agg[:].rearrange("p a b -> p (a b)"),
                 sslice[:].rearrange("p a b -> p (a b)"))
        else:
            for j in range(16):
                KV.o("scalar_tensor_tensor", agg[:, j, :], agg[:, j, :],
                     Dp[:, ct, j:j + 1], sslice[:, j, :], ALU.mult, ALU.add)
        KV.d()
        t_ss = KV.o("tensor_copy", Dap[0:1, 0:1], Dap[0:1, 0:1], inc=True)
    t_ccs = KV.o("tensor_copy", Dap[0:1, 0:1], Dap[0:1, 0:1], inc=True)
    KG.w(t_ccs)
    t_ci = KG.dma(cc_in[:, 0:2048], agg[:].rearrange("p a b -> p (a b)"))
    t_ci2 = KG.dma(cc_in[:, 2048:2064], Dap[:])
    if DBG or DBG3:
        KG.ws([t_ci2])
        t_d = KG.dma(dbgH, cc_in)
        KG.w(t_d)
    KG.ws([t_ci2])
    t_cc = KG.cc("AllGather", ALU.bypass, [[0, 1, 2, 3], [4, 5, 6, 7]],
                 [cc_in.opt()], [cc_out.opt()])
    KG.w(t_cc)
    if DBG or DBGCC:
        t_d = KG.dma(dbgI, cc_out)
        KG.w(t_d)
    gsl = sbA("gsl", [128, 1024])
    Dg4 = sbA("Dg4", [128, 4, 16])
    Ecf = sbA("Ecf", [128, 4, 16])
    Wcf = sbA("Wcf", [128, 4, 16])
    t_go = None
    for r in range(4):
        t_go = KG.dma(Dg4[:, r, :], cc_out[r * 128:(r + 1) * 128, 2048:2064])
    KV.w(t_go)
    for r in range(4):
        KV.o("tensor_scalar", Ecf[:, r, :], Dg4[:, r, :],
             pm_s[:, r:r + 1], None, ALU.mult)
        KV.d()
        KV.o("tensor_scalar", Ecf[:, r, :], Ecf[:, r, :], pm1_s[:, r:r + 1],
             None, ALU.add)
        KV.d()
    KV.o("memset", Wcf[:, 3, :], 1.0)
    KV.d()
    KV.o("tensor_copy", Wcf[:, 2, :], Ecf[:, 3, :])
    KV.d()
    KV.o("tensor_mul", Wcf[:, 1, :], Wcf[:, 2, :], Ecf[:, 2, :])
    KV.d()
    KV.o("tensor_mul", Wcf[:, 0, :], Wcf[:, 1, :], Ecf[:, 1, :])
    KV.d()
    for r in range(4):
        KV.o("tensor_scalar", Wcf[:, r, :], Wcf[:, r, :], pm_s[:, r:r + 1],
             None, ALU.mult)
    KV.d()
    KV.o("memset", agg[:].rearrange("p a b -> p (a b)"), 0.0)
    t_gz = KV.o("tensor_copy", Dap[0:1, 0:1], Dap[0:1, 0:1], inc=True)
    for r in range(4):
        for hf in range(2):
            KG.ws([t_gz])
            t_gs = KG.dma(gsl[:], cc_out[r * 128:(r + 1) * 128,
                                         hf * 1024:(hf + 1) * 1024])
            KV.w(t_gs)
            for j in range(8):
                jj = hf * 8 + j
                KV.o("scalar_tensor_tensor", agg[:, jj, :],
                     gsl[:, j * 128:(j + 1) * 128], Wcf[:, r, jj:jj + 1],
                     agg[:, jj, :], ALU.mult, ALU.add)
            KV.d()
            t_gz = KV.o("tensor_copy", Dap[0:1, 0:1], Dap[0:1, 0:1], inc=True)
    for ct in range(NCH):
        for j in range(16):
            KV.o("tensor_copy", StIn[:, ct, j, :], agg[:, j, :])
        KV.d()
        if ct < NCH - 1:
            KY.w(t_ss)
            t_sl = KY.dma(sslice[:].rearrange("p a b -> p (a b)"),
                          s_hbm[:, ct * 2048:(ct + 1) * 2048])
            KV.w(t_sl)
            for j in range(16):
                KV.o("scalar_tensor_tensor", agg[:, j, :], agg[:, j, :],
                     Dp[:, ct, j:j + 1], sslice[:, j, :], ALU.mult, ALU.add)
            KV.d()
            t_ss = KV.o("tensor_copy", Dap[0:1, 0:1], Dap[0:1, 0:1], inc=True)
    t_dbgs = KV.o("tensor_copy", Dap[0:1, 0:1], Dap[0:1, 0:1], inc=True)
    if DBG or DBG3:
        KY.w(t_dbgs)
        t_d = KY.dma(dbgC, StIn[:, 0, 0, 0:64])
        KY.w(t_d)
    kb.barrier_all(exclude=("bgq", "bgc"))
    sbF("agg", "gsl", "Dg4", "Ecf", "Wcf", "Dap", "Btok", "selP_s",
        "Dmat", "eLast", "eStart", "nStart", "Dagg", "Dp", "sstg", "sslice",
        "bgf0", "bgf1", "bgb0", "bgb1")

    Pm = sbA("Pm", [128, 128], BF16)
    Eh = sbA("Eh", [128, 128], BF16)
    Th_ = sbA("Th_", [128, 128], BF16)
    ygate = sbA("ygate", [128, 128])
    ybf = sbA("ybf", [128, 128], BF16)
    ysq = sbA("ysq", [128, 128], BF16)
    xdt_p = sbA("xdt_p", [128, 128], BF16)
    CSs = sbA("CSs", [128, 128], BF16)
    zt0 = sbA("zt0", [128, 16, 128])
    zt1 = sbA("zt1", [128, 16, 128])
    sry_r = rowt("sry_r"); sry2_r = rowt("sry2_r")

    tok_gate = None
    tok_th = None
    tok_statev = None
    tok_xdtp = None
    t_smm = None
    t_ysq = None
    for ct in range(NCH):
        KY.ws([tok_xdtp])
        t_x = KY.dma(xchk[:], x_hbm[ct * 128:(ct + 1) * 128, :]
                     .rearrange("p (hc q) -> p hc q", q=128))
        KY.w(tok_gate)
        t_z = KY.dma(zt0[:], z_hbm[0:2048, ct * 128:(ct + 1) * 128]
                     .rearrange("(hc p) q -> p hc q", p=128))
        t_z2 = KY.dma(zt1[:], z_hbm[2048:4096, ct * 128:(ct + 1) * 128]
                      .rearrange("(hc p) q -> p hc q", p=128))
        KT.ws([tok_th])
        t_pmm = KT.o("matmul", p_b[:], Bn[:, ct * 128:(ct + 1) * 128],
                     Ctn[:, ct * 128:(ct + 1) * 128], start=True, stop=True, inc=True)
        KV.w(t_pmm)
        t_pev = KV.o("tensor_copy", Pm[:], p_b[:], inc=True)
        KT.w(t_pev)
        for h in range(NH):
            KV.ws([t_x, tok_statev])
            tok_xdtp = KV.o("tensor_scalar", xdt_p[:], xchk[:, h, :],
                            dtT[:, ct, h:h + 1], None, ALU.mult, inc=True)
            half = 64 * (h % 2)
            t_csm = KT.o("matmul", p_b[0:64, :128], sel64_s[:, h * 64:(h + 1) * 64],
                         sdo_b[:, ct * 128:(ct + 1) * 128], start=True, stop=True,
                         inc=True)
            KV.w(t_csm)
            t_css = KV.o("tensor_mul", CSs[half:half + 64, :],
                         Ctn[:, ct * 128:(ct + 1) * 128],
                         p_b[0:64, :128], inc=True)
            KT.w(t_css)
            KT.ws([tok_gate])
            KT.o("matmul", p_a[:], self32_s[:, h * 128:(h + 1) * 128],
                 Acum[:, ct * 128:(ct + 1) * 128], start=True, stop=False)
            t_emm = KT.o("matmul", p_a[:], id_fs[:], mtri[:], start=False, stop=True,
                         inc=True)
            KS.w(t_emm)
            t_eh = KS.o("activation", Eh[:], p_a[:], AF.Exp,
                        bias=AcumT[:, ct, h:h + 1], scale=1.0, inc=True)
            KV.ws([t_eh, t_pev])
            tok_th = KV.o("tensor_mul", Th_[:], Pm[:], Eh[:], inc=True)
            KT.ws([tok_th, tok_xdtp, t_eh])
            KT.o("matmul", p_a[:], xdt_p[:], Th_[:], start=True, stop=False)
            t_ymm = KT.o("matmul", p_a[:], StIn[half:half + 64, ct, h // 2, :],
                         CSs[half:half + 64, :], start=False, stop=True, inc=True)
            KV.ws([t_ymm, t_z, t_z2, t_smm, t_ysq])
            zsl = zt0[:, h, :] if h < 16 else zt1[:, h - 16, :]
            tok_gate = KV.o("tensor_mul", ygate[:], p_a[:], zsl, inc=True)
            KV.d()
            t_ybf = KV.o("tensor_copy", ybf[:], ygate[:], inc=True)
            KS.ws([tok_gate, t_smm])
            t_ysq = KS.o("square", ysq[:], ygate[:], inc=True)
            tm_h = tmat0 if h < 16 else tmat1
            t_tm = KV.o("tensor_scalar", tm_h[:, h % 16, ct * 128:(ct + 1) * 128],
                        ygate[:], lnw_s[:, h:h + 1], lnb_s[:, h:h + 1],
                        ALU.mult, op1=ALU.add, inc=True)
            q0 = 256 * (ct % 2)
            KT.ws([t_ybf, t_ysq, tok_statev])
            KT.o("matmul", p_ra[0:1, q0:q0 + 128], ones_cb[:], ybf[:],
                 start=(h == 0), stop=(h == NH - 1))
            t_smm = KT.o("matmul", p_ra[0:1, q0 + 128:q0 + 256], ones_cb[:], ysq[:],
                         start=(h == 0), stop=(h == NH - 1), inc=True)
            if h == NH - 1:
                KV.w(t_smm)
                KV.o("tensor_copy", sry_r[0:1, ct * 128:(ct + 1) * 128],
                     p_ra[0:1, q0:q0 + 128])
                KV.d()
                tok_statev = KV.o("tensor_copy", sry2_r[0:1, ct * 128:(ct + 1) * 128],
                                  p_ra[0:1, q0 + 128:q0 + 256], inc=True)
    kb.barrier_all()
    sbF("CSs", "xchk", "xdt_ds", "zt0", "zt1", "Pm", "Eh", "Th_", "ygate", "ybf", "ysq",
        "xdt_p", "StIn", "self32_s", "sel64_s", "nAc", "AcumT", "dtT", "dtdsT",
        "sdo_b", "Bn", "Ctn", "Acum")

    muy_r = rowt("muy_r"); ay_r = rowt("ay_r")
    f1_r = rowt("f1_r"); f2_r = rowt("f2_r")
    f12 = sbA("f12", [2, T])
    tmp_r = rowt("tmp_r"); tmp2_r = rowt("tmp2_r"); tmp3_r = rowt("tmp3_r")
    KV.o("tensor_scalar", muy_r[:], sry_r[:], 1.0 / DI, None, ALU.mult)
    KV.o("tensor_scalar", tmp_r[:], sry2_r[:], 1.0 / DI, None, ALU.mult)
    KV.d()
    KV.o("tensor_mul", tmp2_r[:], muy_r[:], muy_r[:])
    KV.d()
    t_var = KV.o("tensor_sub", tmp_r[:], tmp_r[:], tmp2_r[:], inc=True)
    KS.w(t_var)
    t_sd = KS.o("activation", tmp2_r[:], tmp_r[:], AF.Sqrt, bias=el1[:], scale=1.0,
                inc=True)
    KV.w(t_sd)
    KV.o("reciprocal", ay_r[:], tmp2_r[:])
    KV.d()
    KV.o("tensor_mul", f1_r[:], muy_r[:], ay_r[:])
    KV.d()
    KV.o("tensor_scalar", f1_r[:], f1_r[:], -1.0, None, ALU.mult)
    KV.o("tensor_scalar", f2_r[:], ay_r[:], -1.0, None, ALU.mult)
    KV.d()
    KV.o("tensor_scalar", f2_r[:], f2_r[:], 1.0, None, ALU.add)
    t_rows = KV.o("tensor_copy", tmp3_r[0:1, 0:1], tmp3_r[0:1, 0:1], inc=True)
    KY.w(t_rows)
    t_f12a = KY.dma(f12[0:1, :], f1_r[:])
    t_f12b = KY.dma(f12[1:2, :], f2_r[:])
    if DBG:
        KY.ws([("ev", kb.cnt["ev"]), ("es", kb.cnt["es"])])
        for i, rr in enumerate([sry_r, sry2_r, muy_r, ay_r, f1_r, f2_r]):
            KY.dma(dbgF[i:i + 1, :], rr[:])
        t_d = KY.dma(dbgG, tmat[:, 0, :])
        KY.w(t_d)
    ay_rep = sbA("ay_rep", [128, T])
    KT.w(t_rows)
    for hh in range(2):
        t_mm = KT.o("matmul", p_big[:, hh * 512:hh * 512 + 512], ones_rf[:],
                    ay_r[0:1, hh * 512:(hh + 1) * 512], start=True, stop=True,
                    inc=True)
        KV.w(t_mm)
        t_ayr = KV.o("tensor_copy", ay_rep[:, hh * 512:(hh + 1) * 512],
                     p_big[:, hh * 512:hh * 512 + 512], inc=True)
        KT.w(t_ayr)
    kb.barrier_all()

    wst_b = sbA("wst5_b", [128, 32, 128], BF16)
    wst_b2 = sbA("wst5_b2", [128, 32, 128], BF16)
    stg1 = sbA("stg1", [128, T])
    stg2 = sbA("stg2", [128, T])
    ust = sbA("ust", [128, T])
    bfst = sbA("bfst", [128, T], BF16)
    sqst = sbA("sqst", [128, T], BF16)
    v12 = sbA("v12", [128, 16, 2])
    r3f = sbA("r3f", [2, 16, 128])
    sr_r = rowt("sr_r"); sr2_r = rowt("sr2_r")
    KV.o("memset", sr_r[:], 0.0)
    KV.o("memset", sr2_r[:], 0.0, inc=True)

    tok_wdone = [None, None]
    tok_r1dma = None
    tok_stadd = None
    tok_r1tok = None
    for dm in range(16):
        pq = dm % 2
        buf_b = wst_b if pq == 0 else wst_b2
        KY.ws([tok_wdone[pq]])
        t_wl = KY.dma(buf_b[:], wout_bf[:, dm * 128:(dm + 1) * 128]
                      .rearrange("(kc k) m -> k kc m", k=128))
        KT.w(t_wl)
        for kcg in range(32):
            t_vmm = KT.o("matmul", p_b[:, 0:2], buf_b[:, kcg, :],
                         lnwb2[:, kcg, :], start=(kcg == 0), stop=(kcg == 31),
                         inc=(kcg == 31))
        KV.w(t_vmm)
        t_v12 = KV.o("tensor_copy", v12[:, dm, :], p_b[:, 0:2], inc=True)
        KT.w(t_v12)
        t_tr = KT.o("transpose", p_a[0:2, 0:128], v12[:, dm, :], id_fs[:], inc=True)
        KV.w(t_tr)
        t_r3 = KV.o("tensor_copy", r3f[:, dm, :], p_a[0:2, 0:128], inc=True)
        KT.ws([t_r3, ("sq", kb.cnt["sq"])])
        for hh in range(2):
            t_r2m = KT.o("matmul", p_big2[:, hh * 512:hh * 512 + 512], r3f[:, dm, :],
                         f12[:, hh * 512:(hh + 1) * 512], start=True, stop=True,
                         inc=(hh == 1))
        for kcg in range(32):
            tm_h = tmat0 if kcg < 16 else tmat1
            for hh in range(2):
                t_mmm = KT.o("matmul", p_big[:, hh * 512:(hh + 1) * 512],
                             buf_b[:, kcg, :],
                             tm_h[:, kcg % 16, hh * 512:(hh + 1) * 512],
                             start=(kcg == 0), stop=(kcg == 31),
                             inc=(kcg == 31 and hh == 1))
        tok_wdone[pq] = t_mmm
        KY.ws([tok_r1tok])
        t_ul = KY.dma(ust[:], u_fm[dm * 128:(dm + 1) * 128, 4:1028])
        KV.ws([t_r2m, t_mmm, t_ul, tok_r1dma])
        t_s1 = KV.o("tensor_scalar", stg1[:], p_big2[:],
                    outb_s[:, dm:dm + 1], None, ALU.add, inc=True)
        KV.o("tensor_mul", stg2[:], p_big[:], ay_rep[:])
        KV.d()
        KV.o("tensor_add", stg2[:], stg2[:], stg1[:])
        KV.d()
        t_r1 = KV.o("tensor_add", stg2[:], stg2[:], ust[:], inc=True)
        tok_r1tok = t_r1
        KY.w(t_r1)
        tok_r1dma = KY.dma(r1_hbm[dm * 128:(dm + 1) * 128, :], stg2[:])
        KV.d()
        t_bf = KV.o("tensor_copy", bfst[:], stg2[:], inc=True)
        KS.w(t_r1)
        t_sq = KS.o("square", sqst[:], stg2[:], inc=True)
        for hh in range(2):
            KT.ws([t_bf, t_sq, tok_stadd])
            t_sa = KT.o("matmul", p_ra[0:1, 0:512], ones_cb[:],
                        bfst[:, hh * 512:(hh + 1) * 512], start=True, stop=True,
                        inc=True)
            KV.w(t_sa)
            KV.o("tensor_add", sr_r[0:1, hh * 512:(hh + 1) * 512],
                 sr_r[0:1, hh * 512:(hh + 1) * 512], p_ra[0:1, 0:512])
            KV.d()
            t_sb_tok = KV.o("tensor_copy", tmp_r[0:1, 0:1], tmp_r[0:1, 0:1], inc=True)
            KT.w(t_sb_tok)
            t_sa = KT.o("matmul", p_ra[0:1, 0:512], ones_cb[:],
                        sqst[:, hh * 512:(hh + 1) * 512], start=True, stop=True,
                        inc=True)
            KV.w(t_sa)
            KV.o("tensor_add", sr2_r[0:1, hh * 512:(hh + 1) * 512],
                 sr2_r[0:1, hh * 512:(hh + 1) * 512], p_ra[0:1, 0:512])
            KV.d()
            tok_stadd = KV.o("tensor_copy", tmp_r[0:1, 0:1], tmp_r[0:1, 0:1],
                             inc=True)
    kb.barrier_all()
    sbF("stg1", "ust", "v12", "r3f", "ay_rep", "tmat0", "tmat1", "sry_r", "sry2_r",
        "muy_r", "ay_r", "f1_r", "f2_r", "f12", "wst5_b", "wst5_b2")

    mu1_r = rowt("mu1_r"); a1_r = rowt("a1_r"); b1_r = rowt("b1_r")
    c_r = rowt("c_r"); ca_r = rowt("ca_r"); cb_r = rowt("cb_r")
    KV.o("tensor_scalar", mu1_r[:], sr_r[:], 1.0 / D, None, ALU.mult)
    KV.o("tensor_scalar", tmp_r[:], sr2_r[:], 1.0 / D, None, ALU.mult)
    KV.d()
    KV.o("tensor_mul", tmp2_r[:], mu1_r[:], mu1_r[:])
    KV.d()
    t_var = KV.o("tensor_sub", tmp_r[:], tmp_r[:], tmp2_r[:], inc=True)
    KS.w(t_var)
    t_sd = KS.o("activation", tmp2_r[:], tmp_r[:], AF.Sqrt, bias=el1[:], scale=1.0,
                inc=True)
    KV.w(t_sd)
    KV.o("reciprocal", a1_r[:], tmp2_r[:])
    KV.d()
    KV.o("tensor_mul", b1_r[:], mu1_r[:], a1_r[:])
    KV.d()
    KV.o("tensor_scalar", b1_r[:], b1_r[:], -1.0, None, ALU.mult)
    KV.d()
    KV.o("tensor_mul", tmp_r[:], a1_r[:], a1_r[:])
    KV.d()
    KV.o("tensor_mul", tmp_r[:], tmp_r[:], sr2_r[:])
    KV.d()
    KV.o("tensor_scalar", tmp_r[:], tmp_r[:], 1.0 / D, None, ALU.mult)
    KV.o("tensor_mul", tmp2_r[:], a1_r[:], b1_r[:])
    KV.d()
    KV.o("tensor_mul", tmp2_r[:], tmp2_r[:], sr_r[:])
    KV.d()
    KV.o("tensor_scalar", tmp2_r[:], tmp2_r[:], 2.0 / D, None, ALU.mult)
    KV.d()
    KV.o("tensor_add", tmp_r[:], tmp_r[:], tmp2_r[:])
    KV.o("tensor_mul", tmp3_r[:], b1_r[:], b1_r[:])
    KV.d()
    t_msx = KV.o("tensor_add", tmp_r[:], tmp_r[:], tmp3_r[:], inc=True)
    KS.w(t_msx)
    t_sd = KS.o("activation", tmp2_r[:], tmp_r[:], AF.Sqrt, bias=ef1[:], scale=1.0,
                inc=True)
    KV.w(t_sd)
    KV.o("reciprocal", c_r[:], tmp2_r[:])
    KV.d()
    KV.o("reciprocal", tmp3_r[0:1, 0:1], tcl_r[:])
    KV.d()
    KV.o("tensor_mul", ca_r[:], c_r[:], a1_r[:])
    KV.d()
    KV.o("tensor_scalar", ca_r[:], ca_r[:], tmp3_r[0:1, 0:1], None, ALU.mult)
    KV.o("tensor_mul", cb_r[:], c_r[:], b1_r[:])
    KV.d()
    t_cb = KV.o("tensor_scalar", cb_r[:], cb_r[:], tmp3_r[0:1, 0:1], None, ALU.mult,
                inc=True)
    a1_rep = sbA("a1_rep", [128, T])
    b1_rep = sbA("b1_rep", [128, T])
    c_rep = sbA("c_rep", [128, T])
    ca_rep = sbA("ca_rep", [128, T])
    KT.w(t_cb)
    tok_repev = None
    for src_r, dst in [(a1_r, a1_rep), (b1_r, b1_rep), (c_r, c_rep), (ca_r, ca_rep)]:
        for hh in range(2):
            KT.w(tok_repev)
            t_mm = KT.o("matmul", p_big[:, 0:512], ones_rf[:],
                        src_r[0:1, hh * 512:(hh + 1) * 512], start=True, stop=True,
                        inc=True)
            KV.w(t_mm)
            tok_repev = KV.o("tensor_copy", dst[:, hh * 512:(hh + 1) * 512],
                             p_big[:, 0:512], inc=True)
    kb.barrier_all()
    sbF("sr_r", "sr2_r", "mu1_r", "a1_r", "b1_r", "c_r", "ca_r", "tmp_r",
        "tmp2_r", "tmp3_r")

    genes_f = sbA("genes_f", [128, 32, GD])
    genes_b = sbA("genes_b", [128, 32, GD], BF16)
    WdT_b = sbA("WdT_b", [128, 2, 16, 128], BF16)
    WuT_b = sbA("WuT_b", [128, 2, 16, 128], BF16)
    t_gl = KY.dma(genes_f[:], genesd.rearrange("(gc p) n -> p gc n", p=128))
    KG.w(t_gl)
    t_gc = KG.o("tensor_copy", genes_b[:].rearrange("p a b -> p (a b)"),
                genes_f[:].rearrange("p a b -> p (a b)"), inc=True)
    for dst, srcw in [(WdT_b, WdTd), (WuT_b, WuTd)]:
        KY.w(t_gc)
        stview = genes_f[:].rearrange("p a b -> p (a b)")[:, 0:4096]
        t_wl = KY.dma(stview.rearrange("p (kk m) -> p kk m", kk=2),
                      srcw.rearrange("(kk p) m -> p kk m", p=128))
        KG.w(t_wl)
        t_gc = KG.o("tensor_copy", dst[:].rearrange("p a b c -> p (a b c)"),
                    stview, inc=True)
    kb.barrier_all()
    sbF("genes_f")

    lgT = sbA("lgT", [128, 2, 32, 128])
    scr = sbA("scr", [128, G])
    tv = sbA("tv", [128, 32])
    tvE = sbA("tvE", [128, 32])
    negm = sbA("negm", [128, 1])
    sw = sbA("sw", [128, 1])
    rden = sbA("rden", [128, 1])
    wm = sbA("wm", [128, G], BF16)
    wT = sbA("wT", [128, 32, 128], BF16)
    ex_b = sbA("ex_b", [128, GD], BF16)
    eT = sbA("eT", [128, 2, 128], BF16)
    r2b = sbA("r2b", [128, 16, GRP])
    r1s = sbA("r1s", [128, GRP])
    xnb = sbA("xnb", [128, 16, GRP], BF16)
    lgst = sbA("lgst", [128, GRP])
    xtmp = sbA("xtmp", [128, 128])
    tmp_r = rowt("tmp_r", GRP); tmp2_r = rowt("tmp2_r", GRP)
    dtile = sbA("dtile", [128, 128])
    utile = sbA("utile", [128, 128])
    m1t = sbA("m1t", [128, 128])
    g2t = sbA("g2t", [128, 128])
    m2_rep = sbA("m2_rep", [128, GRP])
    s2_rep = sbA("s2_rep", [128, GRP])
    gout = sbA("gout", [128, GRP])
    wh_b = sbA("wh_b", [128, 16, 128], BF16)
    wh_b2 = sbA("wh_b2", [128, 16, 128], BF16)
    wcnw_b = sbA("wcnw_b", [1, G], BF16)
    cb_b = sbA("cb_b", [1, T], BF16)
    mu2_r = rowt("mu2_r", GRP); rs2_r = rowt("rs2_r", GRP)
    t2a_r = rowt("t2a_r", GRP); t2b_r = rowt("t2b_r", GRP)

    KV.o("tensor_copy", cb_b[:], cb_r[:])
    KV.o("tensor_copy", negm[:], negm[:], inc=True)
    tok_outdma = None
    tok_lgev = None
    tok_lgtr = None
    tok_xnwar = None
    tok_wcbuf = [None, None]
    for g in range(NGRP):
        g0 = g * GRP
        t_x4 = None
        for dm in range(16):
            KY.ws([tok_xnwar])
            t_r1g = KY.dma(r1s[:], r1_hbm[dm * 128:(dm + 1) * 128, g0:g0 + GRP])
            KV.w(t_r1g)
            t_x4 = KV.o("tensor_scalar", xnb[:, dm, :], r1s[:],
                        nw_s[:, dm:dm + 1], None, ALU.mult, inc=True)
            tok_xnwar = t_x4
        for gc in range(32):
            pq = gc % 2
            buf_b = wh_b if pq == 0 else wh_b2
            KY.ws([tok_wcbuf[pq]])
            t_wl = KY.dma(buf_b[:], wc_bf[:, gc * 128:(gc + 1) * 128]
                          .rearrange("(kc k) m -> k kc m", k=128))
            KT.ws([t_wl, t_x4, tok_lgev])
            t_lmm = None
            for kc in range(16):
                t_lmm = KT.o("matmul", p_big[:, 0:GRP], buf_b[:, kc, :],
                             xnb[:, kc, :], start=(kc == 0), stop=(kc == 15),
                             inc=(kc == 15))
            tok_wcbuf[pq] = t_lmm
            if g == 0:
                for kc in range(16):
                    t_wv = KT.o("matmul", p_b[:, 0:1], buf_b[:, kc, :],
                                nw_bs[:, kc:kc + 1], start=(kc == 0),
                                stop=(kc == 15), inc=(kc == 15))
                tok_wcbuf[pq] = t_wv
                KV.w(t_wv)
                t_wcol = KV.o("tensor_copy", negm[:], p_b[:, 0:1], inc=True)
                KT.w(t_wcol)
                t_wt = KT.o("transpose", p_a[0:1, 0:128], negm[:], id_fs[:],
                            inc=True)
                KV.w(t_wt)
                t_wrow = KV.o("tensor_copy", wcnw_b[0:1, gc * 128:(gc + 1) * 128],
                              p_a[0:1, 0:128], inc=True)
                KT.w(t_wrow)
            t_r1m = KT.o("matmul", p_big2[:, 0:GRP],
                         wcnw_b[0:1, gc * 128:(gc + 1) * 128],
                         cb_b[0:1, g0:g0 + GRP], start=True, stop=True, inc=True)
            KV.ws([t_lmm, t_r1m, tok_lgtr])
            KV.o("tensor_mul", lgst[:], p_big[:, 0:GRP], ca_rep[:, g0:g0 + GRP])
            KV.d()
            t_lg = KV.o("tensor_add", lgst[:], lgst[:], p_big2[:, 0:GRP], inc=True)
            for tc in range(2):
                KT.ws([t_lg, tok_lgev])
                t_tr = KT.o("transpose", p_a[:], lgst[:, tc * 128:(tc + 1) * 128],
                            id_fs[:], inc=True)
                tok_lgtr = t_tr
                KV.w(t_tr)
                tok_lgev = KV.o("tensor_copy", lgT[:, tc, gc, :], p_a[:], inc=True)
        if DBG and g == 0:
            KY.ws([tok_lgev])
            t_d = KY.dma(dbgB, lgT[:, 0, :, :].rearrange("p a b -> p (a b)"))
            KY.w(t_d)
        t2z = KV.o("memset", t2a_r[:], 0.0)
        KV.o("memset", t2b_r[:], 0.0)
        KV.d()
        tok_r2 = None
        tok_l2 = None
        tok_r1war = None
        for tc in range(2):
            lg = lgT[:, tc, :, :].rearrange("p a b -> p (a b)")
            KV.ws([tok_lgev])
            KV.o("max", out=tv[:, 0:8], in_=lg)
            KV.d()
            KV.o("match_replace", out=scr[:], in_to_replace=tv[:, 0:8],
                 in_values=lg, imm_value=-BIG)
            KV.d()
            KV.o("max", out=tv[:, 8:16], in_=scr[:])
            KV.d()
            KV.o("match_replace", out=scr[:], in_to_replace=tv[:, 8:16],
                 in_values=scr[:], imm_value=-BIG)
            KV.d()
            KV.o("max", out=tv[:, 16:24], in_=scr[:])
            KV.d()
            KV.o("match_replace", out=scr[:], in_to_replace=tv[:, 16:24],
                 in_values=scr[:], imm_value=-BIG)
            KV.d()
            KV.o("max", out=tv[:, 24:32], in_=scr[:])
            KV.d()
            t_negm = KV.o("tensor_scalar", negm[:], tv[:, 0:1], -1.0, None, ALU.mult,
                          inc=True)
            KS.w(t_negm)
            t_sw = KS.o("activation", tvE[:], tv[:], AF.Exp, bias=negm[:], scale=1.0,
                        accum_out=sw[:], inc=True)
            KV.w(t_sw)
            t_rden = KV.o("reciprocal", rden[:], sw[:], inc=True)
            t_ef = KS.o("activation", wm[:], lg, AF.Exp, bias=negm[:], scale=1.0,
                        inc=True)
            KV.w(t_ef)
            t_wm = KV.o("scalar_tensor_tensor", wm[:], lg, tv[:, 31:32], wm[:],
                        ALU.is_ge, ALU.mult, inc=True)
            tok_wtev = None
            for gc in range(32):
                KT.ws([t_wm, tok_wtev])
                t_tr = KT.o("transpose", p_c[:], wm[:, gc * 128:(gc + 1) * 128],
                            id_bs[:], inc=True)
                KV.w(t_tr)
                tok_wtev = KV.o("tensor_copy", wT[:, gc, :], p_c[:], inc=True)
            KT.ws([tok_wtev, tok_r2])
            for gc in range(32):
                t_xmm = KT.o("matmul", p_big[:, 0:GD], wT[:, gc, :],
                             genes_b[:, gc, :], start=(gc == 0), stop=(gc == 31),
                             inc=(gc == 31))
            KV.ws([t_xmm, t_rden])
            t_exb = KV.o("tensor_scalar", ex_b[:], p_big[:, 0:GD], rden[:], None,
                         ALU.mult, inc=True)
            tok_etev = None
            for kk in range(2):
                KT.ws([t_exb, tok_etev])
                t_tr = KT.o("transpose", p_c[:], ex_b[:, kk * 128:(kk + 1) * 128],
                            id_bs[:], inc=True)
                KV.w(t_tr)
                tok_etev = KV.o("tensor_copy", eT[:, kk, :], p_c[:], inc=True)
            t_dt = None
            t_ut = None
            for dm in range(16):
                KT.ws([tok_etev, t_dt])
                KT.o("matmul", p_a[:], WdT_b[:, 0, dm, :], eT[:, 0, :],
                     start=True, stop=False)
                t_dmm = KT.o("matmul", p_a[:], WdT_b[:, 1, dm, :], eT[:, 1, :],
                             start=False, stop=True, inc=True)
                KT.w(t_ut)
                KT.o("matmul", p_b[:], WuT_b[:, 0, dm, :], eT[:, 0, :],
                     start=True, stop=False)
                t_umm = KT.o("matmul", p_b[:], WuT_b[:, 1, dm, :], eT[:, 1, :],
                             start=False, stop=True, inc=True)
                KS.w(t_dmm)
                t_dt = KS.o("activation", dtile[:], p_a[:], AF.Tanh,
                            bias=zcol128[:], scale=1.0, inc=True)
                KV.w(t_umm)
                t_ut = KV.o("tensor_copy", utile[:], p_b[:], inc=True)
                KY.ws([tok_r1war, t_x4])
                t_r1sl = KY.dma(r1s[:, 0:128],
                                r1_hbm[dm * 128:(dm + 1) * 128,
                                       g0 + tc * 128:g0 + (tc + 1) * 128])
                KV.ws([t_dt, t_r1sl])
                KV.o("scalar_tensor_tensor", m1t[:], dtile[:], nwsc[:, dm:dm + 1],
                     utile[:], ALU.mult, ALU.mult)
                KV.o("tensor_mul", xtmp[:], r1s[:, 0:128],
                     a1_rep[:, g0 + tc * 128:g0 + (tc + 1) * 128])
                KV.d()
                KV.o("tensor_add", xtmp[:], xtmp[:],
                     b1_rep[:, g0 + tc * 128:g0 + (tc + 1) * 128])
                KV.o("tensor_mul", g2t[:], m1t[:],
                     c_rep[:, g0 + tc * 128:g0 + (tc + 1) * 128])
                KV.d()
                KV.o("tensor_scalar", g2t[:], g2t[:], 1.0, None, ALU.add)
                KV.d()
                t_r2 = KV.o("tensor_mul", r2b[:, dm, tc * 128:(tc + 1) * 128],
                            xtmp[:], g2t[:], inc=True)
                tok_r1war = t_r2
                tok_r2 = t_r2
                KV.d()
                t_bf = KV.o("tensor_copy", bfst[:, 0:128],
                            r2b[:, dm, tc * 128:(tc + 1) * 128], inc=True)
                KS.w(t_r2)
                t_sq = KS.o("square", sqst[:, 0:128],
                            r2b[:, dm, tc * 128:(tc + 1) * 128], inc=True)
                KT.ws([t_bf, t_sq, tok_l2])
                t_sa = KT.o("matmul", p_ra[0:1, 0:128], ones_cb[:], bfst[:, 0:128],
                            start=True, stop=True)
                t_sb = KT.o("matmul", p_ra[0:1, 128:256], ones_cb[:], sqst[:, 0:128],
                            start=True, stop=True, inc=True)
                KV.w(t_sb)
                KV.o("tensor_add", t2a_r[0:1, tc * 128:(tc + 1) * 128],
                     t2a_r[0:1, tc * 128:(tc + 1) * 128], p_ra[0:1, 0:128])
                KV.d()
                KV.o("tensor_add", t2b_r[0:1, tc * 128:(tc + 1) * 128],
                     t2b_r[0:1, tc * 128:(tc + 1) * 128], p_ra[0:1, 128:256])
                KV.d()
                tok_l2 = KV.o("tensor_copy", tmp_r[0:1, 0:1], tmp_r[0:1, 0:1],
                              inc=True)
        KV.o("tensor_scalar", mu2_r[:], t2a_r[:], 1.0 / D, None, ALU.mult)
        KV.o("tensor_scalar", tmp_r[0:1, 0:GRP], t2b_r[:], 1.0 / D, None, ALU.mult)
        KV.d()
        KV.o("tensor_mul", tmp2_r[0:1, 0:GRP], mu2_r[:], mu2_r[:])
        KV.d()
        t_v2 = KV.o("tensor_sub", tmp_r[0:1, 0:GRP], tmp_r[0:1, 0:GRP],
                    tmp2_r[0:1, 0:GRP], inc=True)
        KS.w(t_v2)
        t_sd2 = KS.o("activation", tmp2_r[0:1, 0:GRP], tmp_r[0:1, 0:GRP], AF.Sqrt,
                     bias=el1[:], scale=1.0, inc=True)
        KV.w(t_sd2)
        t_rs2 = KV.o("reciprocal", rs2_r[:], tmp2_r[0:1, 0:GRP], inc=True)
        KT.w(t_rs2)
        t_mm = KT.o("matmul", p_big[:, 0:GRP], ones_rf[:], mu2_r[:], start=True,
                    stop=True, inc=True)
        KV.w(t_mm)
        t_m2 = KV.o("tensor_copy", m2_rep[:], p_big[:, 0:GRP], inc=True)
        KT.w(t_m2)
        t_mm = KT.o("matmul", p_big[:, 0:GRP], ones_rf[:], rs2_r[:], start=True,
                    stop=True, inc=True)
        KV.w(t_mm)
        t_s2 = KV.o("tensor_copy", s2_rep[:], p_big[:, 0:GRP], inc=True)
        KT.w(t_s2)
        tok_xnwar = tok_r1war
        for dm in range(16):
            KV.ws([t_s2, tok_outdma])
            KV.o("tensor_sub", gout[:], r2b[:, dm, :], m2_rep[:])
            KV.d()
            t_o = KV.o("tensor_mul", gout[:], gout[:], s2_rep[:], inc=True)
            KY.w(t_o)
            tok_outdma = KY.dma(out_d[dm * 128:(dm + 1) * 128, g0:g0 + GRP], gout[:])
    KY.w(("sq", kb.cnt["sq"]))
    KG.w(("gq", kb.cnt["gq"]))
    if kb.cnt["cq"]:
        KS.w(("cq", kb.cnt["cq"]))

    with nc.Block() as block:
        @block.tensor
        def _(t):
            for fn in kb.q["t"]:
                fn(t)

        @block.vector
        def _(v):
            for fn in kb.q["v"]:
                fn(v)

        @block.scalar
        def _(s):
            for fn in kb.q["s"]:
                fn(s)

        @block.gpsimd
        def _(g):
            for fn in kb.q["g"]:
                fn(g)

        @block.sync
        def _(y):
            for fn in kb.q["y"]:
                fn(y)

    es.close()
    return nc


def _prep_inputs(inputs):
    u = np.asarray(inputs["L0_attn_u"])
    ipw = np.asarray(inputs["L0_attn_in_proj_w"])
    ipb = np.asarray(inputs["L0_attn_in_proj_b"])
    cvw = np.asarray(inputs["L0_attn_conv_w"])
    cvb = np.asarray(inputs["L0_attn_conv_b"])
    dtb = np.asarray(inputs["L0_attn_dt_bias"])
    alog = np.asarray(inputs["L0_attn_A_log"])
    lnw = np.asarray(inputs["L0_attn_ln_w"])
    lnb = np.asarray(inputs["L0_attn_ln_b"])
    opw = np.asarray(inputs["L0_attn_out_proj_w"])
    opb = np.asarray(inputs["L0_attn_out_proj_b"])
    Wc = np.asarray(inputs["L0_ffn_Wc"])
    temp = np.asarray(inputs["L0_ffn_temperature"])
    gen = np.asarray(inputs["L0_ffn_genes"])
    Wd = np.asarray(inputs["L0_ffn_Wd"])
    Wu = np.asarray(inputs["L0_ffn_Wu"])
    nw = np.asarray(inputs["L0_ffn_norm_w"])
    scale = np.asarray(inputs["L0_ffn_scale"])

    f = np.float32
    WinT = np.zeros((D, DIP), f)
    WinT[:, :8352] = ipw.T
    WinT2 = np.ascontiguousarray(
        WinT.reshape(16, 128, 66, 128).transpose(1, 2, 0, 3).reshape(128, 66 * 2048))
    inb_a = np.zeros((DIP,), f)
    inb_a[:8352] = ipb
    inb_a = np.ascontiguousarray(inb_a.reshape(66, 128).T)
    cw_a = np.ascontiguousarray(
        cvw.reshape(33, 128, 4).transpose(1, 0, 2).reshape(128, 132)).astype(f)
    cb_a = np.ascontiguousarray(cvb.reshape(33, 128).T).astype(f)
    lnw_a = np.ascontiguousarray(lnw.reshape(32, 128).T).astype(f)
    lnb_a = np.ascontiguousarray(lnb.reshape(32, 128).T).astype(f)
    WoutT = np.ascontiguousarray(opw.T).astype(f)
    outb_a = np.ascontiguousarray(opb.reshape(16, 128).T).astype(f)
    WcT = np.ascontiguousarray(Wc.T).astype(f)
    nw_a = np.ascontiguousarray(nw.reshape(16, 128).T).astype(f)
    WdT = np.ascontiguousarray(Wd.T).astype(f)
    WuT = np.ascontiguousarray(Wu.T).astype(f)
    temp_a = np.broadcast_to(np.asarray(temp, f).reshape(1, 1), (1, 1)).copy()
    scl_a = np.broadcast_to(np.asarray(scale, f).reshape(1, 1), (128, 1)).copy()

    self32 = np.zeros((32, 32 * 128), f)
    for h in range(32):
        self32[h, h * 128:(h + 1) * 128] = 1.0
    sel64 = np.zeros((32, 32 * 64), np.float32)
    for h in range(32):
        sel64[h, h * 64:(h + 1) * 64] = 1.0
    sel64 = sel64.astype(ml_dtypes.bfloat16)
    selP = np.zeros((32, 16 * 128), f)
    for j in range(16):
        for p in range(128):
            selP[2 * j + (1 if p >= 64 else 0), j * 128 + p] = 1.0
    ones_rf_a = np.ones((1, 128), f)
    ones_cb_a = np.ones((128, 1), np.float32).astype(ml_dtypes.bfloat16)
    id_b_a = np.eye(128, dtype=np.float32).astype(ml_dtypes.bfloat16)
    id_f_a = np.eye(128, dtype=f)

    shared = dict(
        WinT2=WinT2, inb=inb_a, cwd=cw_a, cbd=cb_a,
        dtbd=np.ascontiguousarray(dtb.reshape(32, 1)).astype(f),
        alogd=np.ascontiguousarray(alog.reshape(32, 1)).astype(f),
        lnwd=lnw_a, lnbd=lnb_a, WoutT=WoutT, outbd=outb_a, WcT=WcT, nwd=nw_a,
        genesd=gen.astype(f), WdTd=WdT, WuTd=WuT, tempd=temp_a, scld=scl_a,
        self32d=self32, sel64d=sel64, selPd=selP, onesrfd=ones_rf_a,
        onescbd=ones_cb_a, idbd=id_b_a, idfd=id_f_a,
    )

    in_maps = []
    for core in range(8):
        b, r = core // 4, core % 4
        s0 = r * T
        useq = u[b]
        pad = np.zeros((4, D), f)
        if s0 >= 4:
            blk = useq[s0 - 4:s0 + T]
        else:
            blk = np.concatenate([pad[:4 - 0], useq[0:s0 + T]], 0) if s0 == 0 else None
        if s0 == 0:
            blk = np.concatenate([pad, useq[0:T]], 0)
        u_c = np.ascontiguousarray(blk.T)
        pm_a = np.zeros((128, 4), f)
        pm_a[:, :r] = 1.0
        pm1_a = 1.0 - pm_a
        m = dict(shared)
        m["u_fm"] = u_c
        m["pmd"] = pm_a
        m["pm1d"] = pm1_a
        in_maps.append(m)
    return in_maps


def kernel(**inputs):
    if "nc" not in _CACHE:
        _CACHE["nc"] = build_core()
    nc = _CACHE["nc"]
    in_maps = _prep_inputs(inputs)
    res = run_bass_kernel_spmd(nc, in_maps, core_ids=list(range(8)))
    out = np.empty((2, 4096, D), np.float32)
    for core in range(8):
        b, r = core // 4, core % 4
        out[b, r * T:(r + 1) * T, :] = res.results[core]["out"].T
    return out
